# version 45
# baseline (speedup 1.0000x reference)
import math
import contextlib
import numpy as np
import ml_dtypes
import concourse.bass as bass
import concourse.mybir as mybir
from concourse.bass_utils import run_bass_kernel_spmd

F32 = mybir.dt.float32
BF16 = mybir.dt.bfloat16
Alu = mybir.AluOpType
Act = mybir.ActivationFunctionType
AX = mybir.AxisListType.X
BF = ml_dtypes.bfloat16

DM, NST, DTR, NH, DH, IF, TC = 1024, 16, 64, 16, 64, 2816, 512

RIEMANN = (14.134725142, 21.022039639, 25.01085758, 30.424876126, 32.935061588,
           37.586178159, 40.918719012, 43.327073281, 48.005150881, 49.773832478,
           52.970321478, 56.446247697, 59.347044003, 60.831778525, 65.112544048,
           67.079810529, 69.546401711, 72.067157674, 75.704690699, 77.144840069,
           79.33737502, 82.910380854, 84.735492981, 87.425274613, 88.809111208,
           92.491899271, 94.651344041, 95.870634228, 98.831194218, 101.317851006,
           103.72553804, 105.446623052)
TWO_PI = 2.0 * math.pi
CONST_VALS = [1e-5, -0.5, 0.5, -1.0, 1.0, 2.0, 0.0, 1e-30]


class KB:
    ENGINES = ("tensor", "vector", "scalar", "gpsimd", "sync")
    QUEUES = ("sync", "gpsimd")
    NDMA = 40

    def __init__(self, nc):
        self.nc = nc
        self.ops = {e: [] for e in self.ENGINES}
        self.kinds = {e: [] for e in self.ENGINES}
        self.ndma_ops = {e: 0 for e in self.ENGINES}
        self.ncomp_ops = {e: 0 for e in self.ENGINES}
        self.last_writer = {}
        self.readers = {}

    def _emit(self, eng, fn, reads, writes, isdma=False):
        seq = len(self.ops[eng])
        deps = {}
        def add(e, s2):
            if not (e == eng and s2 == seq):
                kind, idx = self.kinds[e][s2]
                key = (e, "q", idx % self.NDMA) if kind else (e, "c")
                if deps.get(key, (None, -1))[1] < s2:
                    deps[key] = (e, s2)
        for k in reads:
            w = self.last_writer.get(k)
            if w:
                add(*w)
        for k in writes:
            w = self.last_writer.get(k)
            if w:
                add(*w)
            for r in self.readers.get(k, ()):
                add(*r)
        self.ops[eng].append((fn, sorted(deps.values()), isdma))
        if isdma:
            self.kinds[eng].append((True, self.ndma_ops[eng]))
            self.ndma_ops[eng] += 1
        else:
            self.kinds[eng].append((False, self.ncomp_ops[eng]))
            self.ncomp_ops[eng] += 1
        for k in reads:
            self.readers.setdefault(k, []).append((eng, seq))
        for k in writes:
            self.last_writer[k] = (eng, seq)
            self.readers[k] = []

    GRAN = 4096

    @staticmethod
    def _keys(ap):
        try:
            name = ap.tensor.name
        except Exception:
            return []
        try:
            import concourse.mybir as _mb
            esz = _mb.dt.size(ap.tensor.dtype)
            dims = list(ap.ap)
            off = ap.offset * esz
            ext = 0
            for (step, count) in dims[1:]:
                ext += abs(step) * (count - 1)
            end = off + (ext + 1) * esz
            g = KB.GRAN
            lo, hi = off // g, (end - 1) // g
            if hi - lo > 8:
                return ["#" + name]
            return [f"#{name}:{i}" for i in range(lo, hi + 1)]
        except Exception:
            return ["#" + name]

    def _auto(self, r, w, rins, wouts):
        r = list(r)
        w = list(w)
        for ap in rins:
            if ap is not None and not isinstance(ap, (int, float)):
                r.extend(self._keys(ap))
        for ap in wouts:
            if ap is not None and not isinstance(ap, (int, float)):
                w.extend(self._keys(ap))
        return r, w

    def dma(self, out, in_, r, w, q="sync", **kw):
        r, w = self._auto(r, w, [in_], [out])
        self._emit(q, lambda o: o.dma_start(out=out, in_=in_, **kw), r, w, isdma=True)

    def mm(self, psb, lhsT, rhs, r, w, start=True, stop=True):
        r, w = self._auto(r, w, [lhsT, rhs], [psb])
        self._emit("tensor", lambda o: o.matmul(psb, lhsT, rhs, start=start, stop=stop), r, w)

    def mmg(self, items, r, w):
        rins = []
        wouts = []
        for (psb, lhsT, rhs, st, sp) in items:
            rins += [lhsT, rhs]
            wouts.append(psb)
        r, w = self._auto(r, w, rins, wouts)
        def fn(o):
            last = None
            for (psb, lhsT, rhs, st, sp) in items:
                last = o.matmul(psb, lhsT, rhs, start=st, stop=sp)
            return last
        self._emit("tensor", fn, r, w)

    def transp(self, psb, in_, ident, r, w):
        r, w = self._auto(r, w, [in_, ident], [psb])
        self._emit("tensor", lambda o: o.transpose(psb, in_, ident), r, w)

    def act(self, out, in_, func, r, w, bias=0.0, scale=1.0):
        r, w = self._auto(list(r) + ["cvals"], w, [in_, bias, scale], [out])
        self._emit("scalar", lambda o: o.activation(out, in_, func, bias=bias, scale=scale), r, w)

    def cp_s(self, out, in_, r, w):
        r, w = self._auto(r, w, [in_], [out])
        self._emit("scalar", lambda o: o.copy(out, in_), r, w)

    def cp_v(self, out, in_, r, w):
        r, w = self._auto(r, w, [in_], [out])
        self._emit("vector", lambda o: o.tensor_copy(out, in_), r, w)

    def tt(self, out, a, b, op, r, w, eng="vector"):
        r, w = self._auto(r, w, [a, b], [out])
        self._emit(eng, lambda o: o.tensor_tensor(out, a, b, op), r, w)

    def ts(self, out, in_, s1, s2, op0, op1, r, w, eng="vector"):
        r, w = self._auto(r, w, [in_, s1, s2], [out])
        if s2 is None and op1 is None:
            self._emit(eng, lambda o: o.tensor_scalar(out, in_, s1, None, op0), r, w)
        else:
            self._emit(eng, lambda o: o.tensor_scalar(out, in_, s1, s2, op0, op1), r, w)

    def stt(self, out, in0, scalar, in1, op0, op1, r, w, eng="vector"):
        r, w = self._auto(r, w, [in0, scalar, in1], [out])
        self._emit(eng, lambda o: o.scalar_tensor_tensor(out, in0, scalar, in1, op0, op1), r, w)

    def red(self, out, in_, op, r, w, absval=False):
        r, w = self._auto(r, w, [in_], [out])
        self._emit("vector", lambda o: o.tensor_reduce(out, in_, AX, op, apply_absolute_value=absval), r, w)

    def scan(self, out, d0, d1, init, r, w):
        r, w = self._auto(r, w, [d0, d1, init], [out])
        self._emit("vector", lambda o: o.tensor_tensor_scan(out, d0, d1, init, Alu.mult, Alu.add), r, w)

    def memset(self, t, v, w, eng="vector"):
        _, w = self._auto([], w, [], [t])
        self._emit(eng, lambda o: o.memset(t, v), [], w)

    def recip(self, out, in_, r, w):
        r, w = self._auto(r, w, [in_], [out])
        self._emit("vector", lambda o: o.reciprocal(out, in_), r, w)

    def lay(self, block):
        stack = contextlib.ExitStack()
        sems = {e: stack.enter_context(self.nc.semaphore(f"sem_{e}"))
                for e in self.ENGINES if any(not k for k, _ in self.kinds[e])}
        qsems = {}
        for e in self.ENGINES:
            nd = self.ndma_ops[e]
            if nd:
                qsems[e] = [stack.enter_context(self.nc.semaphore(f"sem_{e}_{i}"))
                            for i in range(min(self.NDMA, nd))]

        def make(eng):
            ops = self.ops[eng]
            kinds = self.kinds[eng]
            def body(o):
                hw = {}
                floor = 0
                for seq, (fn, deps, isdma) in enumerate(ops):
                    for pe, pseq in deps:
                        pk, pidx = self.kinds[pe][pseq]
                        if pe == eng and not pk and not isdma:
                            if pseq >= floor and eng != "tensor":
                                o.drain()
                                floor = seq
                            continue
                        if pk:
                            slot = pidx % self.NDMA
                            need = 16 * (pidx // self.NDMA + 1)
                            key = (pe, "q", slot)
                            if hw.get(key, -1) < need:
                                o.wait_ge(qsems[pe][slot], need)
                                hw[key] = need
                        else:
                            need = pidx + 1
                            key = (pe, "c")
                            if hw.get(key, -1) < need:
                                o.wait_ge(sems[pe], need)
                                hw[key] = need
                    if isdma:
                        _, didx = kinds[seq]
                        slot = didx % self.NDMA
                        rnd = didx // self.NDMA
                        if rnd > 0:
                            key = (eng, "q", slot)
                            need = 16 * rnd
                            if hw.get(key, -1) < need:
                                o.wait_ge(qsems[eng][slot], need)
                                hw[key] = need
                        fn(o).then_inc(qsems[eng][slot], 16)
                    else:
                        fn(o).then_inc(sems[eng], 1)
                nd = self.ndma_ops[eng]
                if nd:
                    for slot in range(min(self.NDMA, nd)):
                        cnt = (nd - slot + self.NDMA - 1) // self.NDMA
                        o.wait_ge(qsems[eng][slot], 16 * cnt)
            return body
        for eng in ("sync", "tensor", "vector", "scalar", "gpsimd"):
            if self.ops[eng]:
                getattr(block, eng)(make(eng))
        return stack


def host_constants():
    c = {}
    c["ident_f32"] = np.eye(128, dtype=np.float32)
    c["ident_bf16"] = np.eye(128, dtype=BF)
    E = np.zeros((16, 128, 128), dtype=BF)
    for g in range(16):
        for p in range(128):
            E[g, 8 * g + p // 16, p] = 1
    c["Esel"] = np.ascontiguousarray(E.transpose(1, 0, 2).reshape(128, 16 * 128))
    c["Ssel"] = np.ascontiguousarray(E.transpose(2, 0, 1).reshape(128, 16 * 128))
    R = np.zeros((32, 2, 128), dtype=BF)
    for p in range(128):
        R[p % 16, 0, p] = 1
        R[16 + p % 16, 1, p] = 1
    c["Rsel"] = np.ascontiguousarray(R.reshape(32, 256))
    o16 = np.zeros((128, 8, 16), dtype=BF)
    for cc in range(8):
        for j in range(2):
            o16[64 * j:64 * j + 64, cc, 2 * cc + j] = 1
    c["ones16"] = np.ascontiguousarray(o16.reshape(128, 128))
    eh = np.zeros((16, 16, 128), dtype=BF)
    for h in range(16):
        eh[h, h, :] = 1
    c["eh"] = np.ascontiguousarray(eh.transpose(1, 0, 2).reshape(16, 16 * 128))
    masks = np.zeros((128, 4, 512), dtype=BF)
    for m in range(4):
        for k in range(128):
            qs = k + m * 128
            if qs < 512:
                masks[k, m, qs:] = 1
    c["cmask"] = np.ascontiguousarray(masks.reshape(128, 4 * 512))
    gamma = np.asarray(RIEMANN, dtype=np.float32)
    inv_freq = (1.0 / (gamma / gamma[0])).astype(np.float32)
    c["wfac"] = np.ascontiguousarray(np.tile(inv_freq, 4)[:, None])
    pm = np.zeros((128, 1), np.float32)
    for p in range(128):
        pm[p, 0] = -1.0 if (p % 64) < 32 else 1.0
    c["pm_col"] = pm
    c["onesq"] = np.ones((128, 128), dtype=BF)
    psw = np.zeros((128, 128), dtype=BF)
    for p in range(128):
        psw[(p ^ 32) if (p % 64) < 64 else p, p] = 0
    psw[:] = 0
    for p in range(128):
        src = (p + 32) if (p % 64) < 32 else (p - 32)
        psw[src, p] = 1
    c["pswap"] = psw
    cv = np.zeros((128, 8), dtype=np.float32)
    for i, v in enumerate(CONST_VALS):
        cv[:, i] = v
    c["cvals"] = cv
    hsel = np.zeros((16, 8, 128), dtype=BF)
    for dc in range(8):
        for p in range(128):
            hsel[2 * dc + p // 64, dc, p] = 1
    c["hsel"] = np.ascontiguousarray(hsel.reshape(16, 8 * 128))
    return c


def qk_perm():
    perm = np.arange(3072)
    for sec in range(2):
        for h in range(NH):
            base = sec * 1024 + h * 64
            perm[base:base + 64] = base + np.concatenate([np.arange(0, 64, 2), np.arange(1, 64, 2)])
    return perm


def prep_weights(ws):
    out = {}
    out["w_xproj"] = np.ascontiguousarray(np.asarray(ws["L0_attn_W_xproj"], np.float32).T)
    out["w_dt_t"] = np.ascontiguousarray(np.asarray(ws["L0_attn_W_dt"], np.float32).T)
    out["b_dt_col"] = np.ascontiguousarray(np.asarray(ws["L0_attn_b_dt"], np.float32).reshape(8, 128).T)
    al = np.asarray(ws["L0_attn_A_log"], np.float32)
    acol = np.zeros((128, 128), dtype=np.float32)
    for dc in range(8):
        for g in range(16):
            for dl in range(8):
                acol[dl * 16:dl * 16 + 16, dc * 16 + g] = al[128 * dc + 8 * g + dl, :]
    out["a_log_col"] = acol
    out["d_col"] = np.ascontiguousarray(np.asarray(ws["L0_attn_D"], np.float32).reshape(8, 128).T)

    def coefgmat(G1, G2):
        ct = np.zeros((4, 1024), np.float32)
        for b in range(32):
            for cc in range(4):
                ct[cc, b * 32:b * 32 + 32] = G1[0, b, cc, :]
        gm = np.zeros((4, 256), np.float32)
        for cc in range(4):
            gm[cc] = np.ascontiguousarray(G2[cc, :, 0, :].T).reshape(256)
        return ct, gm
    out["coefT_R"], out["gmat_R"] = coefgmat(np.asarray(ws["L0_ffn_G1_R"], np.float32),
                                             np.asarray(ws["L0_ffn_G2_R"], np.float32))
    out["coefT_I"], out["gmat_I"] = coefgmat(np.asarray(ws["L0_ffn_G1_I"], np.float32),
                                             np.asarray(ws["L0_ffn_G2_I"], np.float32))
    out["bias_ri"] = np.concatenate([np.asarray(ws["L0_ffn_bias_real"], np.float32),
                                     np.asarray(ws["L0_ffn_bias_imag"], np.float32)])[None, :]
    out["w_qkv"] = np.ascontiguousarray(np.asarray(ws["L1_attn_w_qkv"], np.float32)[:, qk_perm()])
    out["w_o"] = np.ascontiguousarray(np.asarray(ws["L1_attn_w_o"], np.float32))
    out["log_scale"] = np.asarray(ws["L1_attn_log_scale"], np.float32).reshape(16, 1)
    out["w_gate"] = np.ascontiguousarray(np.asarray(ws["L1_ffn_w_gate"], np.float32))
    out["w_up"] = np.ascontiguousarray(np.asarray(ws["L1_ffn_w_up"], np.float32))
    out["w_down"] = np.ascontiguousarray(np.asarray(ws["L1_ffn_w_down"], np.float32))
    return out


INPUT_SPECS = [
    ("x", None, F32), ("ident_f32", (128, 128), F32), ("ident_bf16", (128, 128), BF16),
    ("Esel", (128, 2048), BF16), ("Ssel", (128, 2048), BF16), ("Rsel", (32, 256), BF16),
    ("ones16", (128, 128), BF16), ("eh", (16, 2048), BF16), ("cmask", (128, 2048), BF16),
    ("wfac", (128, 1), F32), ("hsel", (16, 1024), BF16), ("cvals", (128, 8), F32),
    ("pm_col", (128, 1), F32), ("onesq", (128, 128), BF16), ("pswap", (128, 128), BF16),
    ("w_xproj", (1024, 96), F32), ("w_dt_t", (64, 1024), F32), ("b_dt_col", (128, 8), F32),
    ("a_log_col", (128, 128), F32), ("d_col", (128, 8), F32),
    ("coefT_R", (4, 1024), F32), ("gmat_R", (4, 256), F32),
    ("coefT_I", (4, 1024), F32), ("gmat_I", (4, 256), F32), ("bias_ri", (1, 1024), F32),
    ("w_qkv", (1024, 3072), F32), ("w_o", (1024, 1024), F32), ("log_scale", (16, 1), F32),
    ("w_gate", (IF, 1024), F32), ("w_up", (IF, 1024), F32), ("w_down", (1024, IF), F32),
]


def build_program(T=2048, stage="full"):
    nc = bass.Bass("TRN2", target_bir_lowering=False)
    NCH = T // TC
    dram = {}
    for nm, sh, dt in INPUT_SPECS:
        shape = [T, DM] if nm == "x" else list(sh)
        dram[nm] = nc.dram_tensor(nm, shape, dt, kind="ExternalInput").ap()
    dram["out"] = nc.dram_tensor("out", [T, DM], F32, kind="ExternalOutput").ap()
    dram["wb_scr"] = nc.dram_tensor("wb_scr", [2, 1024, 256], F32, kind="Internal").ap()
    dram["wqkv_scr"] = nc.dram_tensor("wqkv_scr", [8, 128, 3072], BF16, kind="Internal").ap()
    dram["wg_scr"] = nc.dram_tensor("wg_scr", [8, 128, IF], BF16, kind="Internal").ap()
    dram["wu_scr"] = nc.dram_tensor("wu_scr", [8, 128, IF], BF16, kind="Internal").ap()
    dram["wd_scr"] = nc.dram_tensor("wd_scr", [22, 128, DM], BF16, kind="Internal").ap()
    dram["wo_scr"] = nc.dram_tensor("wo_scr", [16, 64, 1024], BF16, kind="Internal").ap()
    dram["ls_scr"] = nc.dram_tensor("ls_scr", [16, 1], F32, kind="Internal").ap()
    dram["mt_scr"] = nc.dram_tensor("mt_scr", [8, 128, 1024], BF16, kind="Internal").ap()
    dram["rk_scr"] = nc.dram_tensor("rk_scr", [8, 128, 2048], BF16, kind="Internal").ap()
    dram["v_scr"] = nc.dram_tensor("v_scr", [16, 128, 1024], BF16, kind="Internal").ap()
    dram["hqT_scr"] = nc.dram_tensor("hqT_scr", [4, 22, 128, 128], BF16, kind="Internal").ap()
    dram["avn_scr"] = nc.dram_tensor("avn_scr", [16, 64, 512], BF16, kind="Internal").ap()

    kb = KB(nc)
    es = contextlib.ExitStack()
    def sb(name, shape, dtype=F32):
        return es.enter_context(nc.sbuf_tensor(name, list(shape), dtype))

    S = {}
    def sbd(name, shape, dtype=F32):
        S[name] = sb(name, shape, dtype)
        return S[name]

    c_identf = sbd("c_identf", (128, 128))
    c_identb = sbd("c_identb", (128, 128), BF16)
    c_E = sbd("c_E", (128, 2048), BF16)
    c_S = sbd("c_S", (128, 2048), BF16)
    c_R = sbd("c_R", (32, 256), BF16)
    c_o16 = sbd("c_o16", (128, 128), BF16)
    c_eh = sbd("c_eh", (16, 2048), BF16)
    c_mask = sbd("c_mask", (128, 2048), BF16)
    c_wf = sbd("c_wf", (128, 1))
    c_hsel = sbd("c_hsel", (16, 1024), BF16)
    c_cv = sbd("c_cv", (128, 8))
    c_pm = sbd("c_pm", (128, 1))
    c_osq = sbd("c_osq", (128, 128), BF16)
    c_psw = sbd("c_psw", (128, 128), BF16)
    w_acol = sbd("w_acol", (128, 128))
    w_dcol = sbd("w_dcol", (128, 8))
    w_bdt = sbd("w_bdt", (128, 8))
    w_xpb = sbd("w_xpb", (128, 8 * 96), BF16)
    w_dtt = sbd("w_dtt", (64, 1024), BF16)
    w_bias0 = sbd("w_bias0", (128, 1024))
    w_stage = sbd("w_stage", (128, 1024))
    w_kv = sbd("w_kv", (128, 3072), BF16)
    w_str = sbd("w_str", (128, IF), BF16)
    w_oh = sbd("w_oh", (64, 1024), BF16)
    wg_b = sbd("wg_b", (128, 8 * 512), BF16)
    wu_b = sbd("wu_b", (128, 8 * 512), BF16)
    w_f0 = sbd("w_f0", (128, 2 * 512), BF16)
    wtt = sbd("wtt", (128, 512), BF16)
    avb = sbd("avb", (64, TC), BF16)
    rqh = sbd("rqh", (64, TC), BF16)
    bias_b = sbd("bias_b", (1, 1024), BF16)
    ls_b = sbd("ls_b", (16, 1), BF16)
    bufA = sbd("bufA", (128, 4 * DM))
    bufB = sbd("bufB", (128, 4 * DM))
    bufD = sbd("bufD", (128, DM))
    sq_s = sbd("sq_s", (128, DM))
    stat = sbd("stat", (128, 32))
    xn_fb = sbd("xn_fb", (128, 8 * TC), BF16)
    delta_b = sbd("delta_b", (128, TC), BF16)
    u_b = sbd("u_b", (128, TC), BF16)
    xp_b = sbd("xp_b", (64, TC), BF16)
    xp_bc = sbd("xp_bc", (32, TC), BF16)
    z_s = sbd("z_s", (128, TC))
    bc_f = sbd("bc_f", (128, 2 * TC))
    W_EE = sbd("W_EE", (128, 2 * TC), BF16)
    dA_s = W_EE[:, TC:2 * TC]; q_ee = W_EE[:, 0:TC]
    S["dA_s"] = dA_s; S["q_ee"] = q_ee
    W_CC = sbd("W_CC", (128, 2 * TC), BF16)
    dBx_s = W_CC[:, TC:2 * TC]; q_cc = W_CC[:, 0:TC]
    S["dBx_s"] = dBx_s; S["q_cc"] = q_cc
    W_VA = sbd("W_VA", (128, 2 * TC))
    h_s = W_VA[:, TC:2 * TC]; q_va = W_VA[:, 0:TC]
    S["h_s"] = h_s; S["q_va"] = q_va
    hc_s = sbd("hc_s", (128, TC), BF16)
    carry = sbd("carry", (128, 128))
    rp_powC = sbd("rp_powC", (128, 12 * 8))
    rp_powS = sbd("rp_powS", (128, 12 * 8))
    rp_sc = sbd("rp_sc", (128, 40))
    rp1 = sbd("rp1", (128, 512), BF16)
    rp2 = sbd("rp2", (128, 512), BF16)
    rp_t1 = sbd("rp_t1", (128, 256))
    rp_t2 = sbd("rp_t2", (128, 256))
    finC = sbd("finC", (128, 32 * 8)); finS = sbd("finS", (128, 32 * 8))
    coaC = sbd("coaC", (128, 64 * 8)); coaS = sbd("coaS", (128, 64 * 8))
    tbl = sbd("tbl", (128, 4 * TC), BF16)
    q_c = sbd("q_c", (128, TC), BF16)
    k_c = sbd("k_c", (128, TC), BF16)
    qs_c = sbd("qs_c", (128, TC), BF16)
    ks_c = sbd("ks_c", (128, TC), BF16)
    rq_s = sbd("rq_s", (128, 8 * TC), BF16)
    rkh = sbd("rkh", (64, 2048), BF16)
    vh = sbd("vh", (128, 16 * 65), BF16)
    nrm_k = sbd("nrm_k", (16, 512))
    nrm_q = sbd("nrm_q", (16, 512))
    mh_s = sbd("mh_s", (16, 8))
    shf_s = sbd("shf_s", (128, 16))
    pt_s = sbd("pt_s", (128, 2 * TC), BF16)
    av_s = sbd("av_s", (128, TC))
    W_H = sbd("W_H", (128, 2 * TC))
    hh_s = W_H[:, 0:TC]; rcp_s = W_H[:, TC:2 * TC]
    S["hh_s"] = hh_s; S["rcp_s"] = rcp_s
    g_s = sbd("g_s", (128, TC))
    q_sp = sbd("q_sp", (128, 512))
    q_am = sbd("q_am", (128, 64))
    q_rs = sbd("q_rs", (128, 64))
    hq_t = sbd("hq_t", (128, TC), BF16)
    xq_t = sbd("xq_t", (128, DM), BF16)
    xq_f = sbd("xq_f", (128, 8 * 128), BF16)
    wnk = sbd("wnk", (128, IF), BF16)
    ones_b = sbd("ones_b", (128, 1), BF16)
    ones_r = sbd("ones_r", (1, 128), BF16)
    gat = sbd("gat", (128, 512))
    wbt = gat
    coefT = w_str[0:4, 0:2048]
    gmat = w_kv[0:4, 2048:2560]

    ps = [es.enter_context(nc.psum_tensor(f"ps{i}", [128, 512], F32)) for i in range(8)]
    psT16 = ps[0][:, 0:64].bitcast(BF16)
    psT16b = ps[3][:, 0:64].bitcast(BF16)
    S["ps"] = ps
    S["psT16"] = psT16
    S["psT16b"] = psT16b
    S["dram"] = dram

    for nm, t in [("ident_f32", c_identf), ("ident_bf16", c_identb), ("Esel", c_E),
                  ("Ssel", c_S), ("Rsel", c_R), ("ones16", c_o16), ("eh", c_eh),
                  ("cmask", c_mask), ("wfac", c_wf), ("hsel", c_hsel),
                  ("a_log_col", w_acol), ("d_col", w_dcol), ("b_dt_col", w_bdt),
                  ("cvals", c_cv), ("pm_col", c_pm), ("onesq", c_osq), ("pswap", c_psw)]:
        kb.dma(t[:], dram[nm], [], [nm])
    for i, v in enumerate(CONST_VALS):
        nc.const_aps.register(F32, v, c_cv[:, i:i + 1])
    kb.memset(ones_b[:], 1.0, ["ones_b"])
    kb.memset(ones_r[:], 1.0, ["ones_r"])
    kb.memset(gat[:], 0.0, ["gat"])
    kb.memset(vh[:, 64::65], 1.0, ["vh_ones"])
    kb.act(w_acol[:], w_acol[:], Act.Exp, ["a_log_col"], ["a_exp"])
    kb.ts(w_acol[:], w_acol[:], -1.0, None, Alu.mult, None, ["a_exp"], ["a_col"])
    kb.dma(w_stage[:64, :1024], dram["w_dt_t"], ["w_stage"], ["w_stage"])
    kb.cp_s(w_dtt[:], w_stage[:64, :1024], ["w_stage"], ["w_dtt"])
    tbl32 = tbl.bitcast(F32)
    def quant_slice(src, dst, kin, kout, nb, par=0):
        if par == 0:
            qva, qcc, qee, qsp, qam, qrs = q_va, q_cc, q_ee, q_sp, q_am, q_rs
        else:
            qva, qcc, qee = h_s, dBx_s, dA_s
            qsp = z_s
            qam, qrs = bc_f[:, 0:64], bc_f[:, 64:128]
        kb.red(qam[:, :nb].rearrange("p (b o) -> p b o", o=1),
               src.rearrange("p (b k) -> p b k", k=16), Alu.max, kin, [kout + "_am"], absval=True)
        kb.act(qsp[:, :nb * 16], src, Act.Sign, kin, [kout + "_sg"])
        kb.ts(qam[:, :nb], qam[:, :nb], 1.0 / 6.0, 1e-12, Alu.mult, Alu.max,
              [kout + "_am"], [kout + "_am2"])
        kb.recip(qrs[:, :nb], qam[:, :nb], [kout + "_am2"], [kout + "_rs"])
        kb.stt(qva[:, :nb * 16], src, -1.0, src, Alu.mult, Alu.max, kin, [kout + "_abs"])
        kb.ts(qam[:, :nb], qam[:, :nb], 0.5, None, Alu.mult, None,
              [kout + "_am2", kout + "_rs"], [kout + "_hs"])
        rsb = qrs[:, :nb].rearrange("p (b o) -> p b o", o=1).broadcast_to([128, nb, 16])
        kb.tt(qva[:, :nb * 16].rearrange("p (b k) -> p b k", k=16),
              qva[:, :nb * 16].rearrange("p (b k) -> p b k", k=16), rsb, Alu.mult,
              [kout + "_abs", kout + "_rs"], [kout + "_va"])
        scb = qam[:, :nb].rearrange("p (b o) -> p b o", o=1).broadcast_to([128, nb, 16])
        va = qva[:, :nb * 16]
        kb.ts(qcc[:, :nb * 16], va, 0.25, None, Alu.is_gt, None, [kout + "_va"], [kout + "_c"])
        kb.ts(qee[:, :nb * 16], va, 5.0, None, Alu.is_gt, None, [kout + "_va"], [kout + "_e"])
        kb.tt(qsp[:, :nb * 16].rearrange("p (b k) -> p b k", k=16),
              qsp[:, :nb * 16].rearrange("p (b k) -> p b k", k=16), scb, Alu.mult,
              [kout + "_sg", kout + "_hs"], [kout + "_sp"], eng="gpsimd")
        for mc, me in ((0.75, 5.0), (1.25, 3.5), (1.75, 2.5)):
            kb.stt(qcc[:, :nb * 16], va, mc, qcc[:, :nb * 16], Alu.is_gt, Alu.add,
                   [kout + "_va", kout + "_c"], [kout + "_c"])
            kb.stt(qee[:, :nb * 16], va, me, qee[:, :nb * 16], Alu.is_gt, Alu.add,
                   [kout + "_va", kout + "_e"], [kout + "_e"])
        kb.stt(qcc[:, :nb * 16], qee[:, :nb * 16], 2.0, qcc[:, :nb * 16], Alu.mult, Alu.add,
               [kout + "_e", kout + "_c"], [kout + "_c2"])
        kb.tt(dst, qcc[:, :nb * 16], qsp[:, :nb * 16], Alu.mult,
              [kout + "_c2", kout + "_sp"], [kout])

    QSET0 = dict(va=W_VA, cc=W_CC, ee=W_EE, am=q_am, rs=q_rs, sps=None)
    def quant_multi(jobs):
        js = []
        for (src, kin, kout, nb, dsts, qs) in jobs:
            if qs is None:
                qs = QSET0
            sps = qs["sps"] or [q_sp, z_s]
            w = nb * 16
            nd = len(dsts)
            js.append((src, kin, kout, nb, dsts, qs, sps, w, nd, w // nd, nb // nd))
        for src, kin, kout, nb, dsts, qs, sps, w, nd, hw, nbh in js:
            kb.red(qs["am"][:, :nb].rearrange("p (b o) -> p b o", o=1),
                   src.rearrange("p (b k) -> p b k", k=16), Alu.max, kin, [kout + "_am"],
                   absval=True)
            for i in range(nd):
                kb.act(sps[i][:, :hw], src[:, i * hw:(i + 1) * hw], Act.Sign, kin,
                       [kout + f"_sg{i}"])
        for src, kin, kout, nb, dsts, qs, sps, w, nd, hw, nbh in js:
            kb.ts(qs["am"][:, :nb], qs["am"][:, :nb], 1.0 / 6.0, 1e-12, Alu.mult, Alu.max,
                  [kout + "_am"], [kout + "_am2"])
        for src, kin, kout, nb, dsts, qs, sps, w, nd, hw, nbh in js:
            kb.recip(qs["rs"][:, :nb], qs["am"][:, :nb], [kout + "_am2"], [kout + "_rs"])
            kb.stt(qs["va"][:, :w], src, -1.0, src, Alu.mult, Alu.max, kin, [kout + "_abs"])
        for src, kin, kout, nb, dsts, qs, sps, w, nd, hw, nbh in js:
            rsb = qs["rs"][:, :nb].rearrange("p (b o) -> p b o", o=1).broadcast_to([128, nb, 16])
            kb.tt(qs["va"][:, :w].rearrange("p (b k) -> p b k", k=16),
                  qs["va"][:, :w].rearrange("p (b k) -> p b k", k=16), rsb, Alu.mult,
                  [kout + "_abs", kout + "_rs"], [kout + "_va"])
        for src, kin, kout, nb, dsts, qs, sps, w, nd, hw, nbh in js:
            va = qs["va"][:, :w]
            kb.ts(qs["cc"][:, :w], va, 0.25, None, Alu.is_gt, None, [kout + "_va"], [kout + "_c"])
            kb.ts(qs["ee"][:, :w], va, 5.0, None, Alu.is_gt, None, [kout + "_va"], [kout + "_e"])
            for i in range(nd):
                scb = qs["am"][:, i * nbh:(i + 1) * nbh].rearrange("p (b o) -> p b o", o=1)\
                    .broadcast_to([128, nbh, 16])
                kb.tt(sps[i][:, :hw].rearrange("p (b k) -> p b k", k=16),
                      sps[i][:, :hw].rearrange("p (b k) -> p b k", k=16), scb, Alu.mult,
                      [kout + f"_sg{i}", kout + "_am2"], [kout + f"_sp{i}"], eng="gpsimd")
        for mc, me in ((0.75, 5.0), (1.25, 3.5), (1.75, 2.5)):
            for src, kin, kout, nb, dsts, qs, sps, w, nd, hw, nbh in js:
                va = qs["va"][:, :w]
                kb.stt(qs["cc"][:, :w], va, mc, qs["cc"][:, :w], Alu.is_gt, Alu.add,
                       [kout + "_va", kout + "_c"], [kout + "_c"])
                kb.stt(qs["ee"][:, :w], va, me, qs["ee"][:, :w], Alu.is_gt, Alu.add,
                       [kout + "_va", kout + "_e"], [kout + "_e"])
        for src, kin, kout, nb, dsts, qs, sps, w, nd, hw, nbh in js:
            kb.stt(qs["cc"][:, :w], qs["cc"][:, :w], 0.5, qs["ee"][:, :w], Alu.mult, Alu.add,
                   [kout + "_e", kout + "_c"], [kout + "_c2"])
        for src, kin, kout, nb, dsts, qs, sps, w, nd, hw, nbh in js:
            for i, (dst, dkey) in enumerate(dsts):
                kb.tt(dst, qs["cc"][:, i * hw:(i + 1) * hw], sps[i][:, :hw], Alu.mult,
                      [kout + "_c2", kout + f"_sp{i}"], [dkey])

    def quant_wide(src, kin, kout, nb, dsts):
        quant_multi([(src, kin, kout, nb, dsts, None)])

    S["quant_slice"] = quant_slice
    S["quant_wide"] = quant_wide
    S["quant_multi"] = quant_multi
    xnf32 = xn_fb.bitcast(F32)
    pt32 = pt_s.bitcast(F32)
    QSET1 = dict(va=xnf32[:, 0:1024], cc=rq_s[:, 0:1024], ee=rq_s[:, 1024:2048],
                 am=pt32[:, 0:64], rs=pt32[:, 64:128],
                 sps=[xnf32[:, 1024:1536], xnf32[:, 1536:2048]])
    psT16e = ps[6][:, 0:64].bitcast(BF16)
    psT16f = ps[7][:, 0:64].bitcast(BF16)
    W2Q = []
    W2T = []
    if stage in ("full", "ffnw", "l1"):
        for wname, scr, NT, K in [("w_gate", "wg_scr", 22, 1024), ("w_up", "wu_scr", 22, 1024),
                                  ("w_down", "wd_scr", 8, IF)]:
            for nt in range(NT):
                W2T.append((wname, scr, nt, K))

    stgB = rq_s.bitcast(F32)[:, 1024:2048]
    def w2trans(wname, scr, nt, K, woff, pvs, pks, wslots, wslotks):
        nkc = K // 128
        for kc in range(nkc):
            pcq = (kc * 128) // 1024
            kwq = min(1024, K - pcq * 1024)
            qk = f"wq_{wname}_{nt}_{pcq}_{0 if (kc * 128) % 1024 < kwq // 2 else 1}"
            pv, pk = pvs[kc % 2], pks[kc % 2]
            wslot, wsk = wslots[kc % 4], wslotks[kc % 4]
            kb.transp(pv[:, :], wnk[:, woff + kc * 128:woff + (kc + 1) * 128], c_identb[:],
                      [qk, "ident_bf16"], [pk + "T", pk])
            kb.act(wslot, pv[:, :], Act.Copy, [pk + "T", pk], [wsk])
            kb.dma(dram[scr][kc][:, nt * 128:(nt + 1) * 128], wslot,
                   [wsk], [f"wqs_{wname}_{kc}_{nt}"])
    def w2piece(wname, nt, K, pc, stg, stgk, woff, qs):
        kw = min(1024, K - pc * 1024)
        kb.dma(stg[:, :kw], dram[wname][nt * 128:(nt + 1) * 128, pc * 1024:pc * 1024 + kw],
               [stgk], [stgk, f"wq_ld_{wname}_{nt}_{pc}"], q="gpsimd")
        hw = kw // 2
        return (stg[:, :kw], [f"wq_ld_{wname}_{nt}_{pc}"],
                f"wq_{wname}_{nt}_{pc}", kw // 16,
                [(wnk[:, woff + pc * 1024: woff + pc * 1024 + hw], f"wq_{wname}_{nt}_{pc}_0"),
                 (wnk[:, woff + pc * 1024 + hw: woff + pc * 1024 + kw], f"wq_{wname}_{nt}_{pc}_1")],
                qs)
    wttks = [f"wtt{i}" for i in range(4)]
    xqks = [f"xqw{i}" for i in range(4)]
    xqsl = [xq_t[:, i * 128:(i + 1) * 128] for i in range(4)]
    wtsl = [wtt[:, i * 128:(i + 1) * 128] for i in range(4)]
    NEARLY = min(40, len(W2T) - len(W2T) % 2)
    for p in range(0, NEARLY, 2):
        def w2pair(p=p):
            a = W2T[p]; b = W2T[p + 1]
            ja = w2piece(a[0], a[2], a[3], 0, tbl32, "tbl32", 0, QSET0)
            jb = w2piece(b[0], b[2], b[3], 0, stgB, "stgB", 1024, QSET1)
            quant_multi([ja, jb])
            w2trans(a[0], a[1], a[2], a[3], 0, [psT16, psT16b], ["@ps0", "@ps3"], wtsl, wttks)
            w2trans(b[0], b[1], b[2], b[3], 1024, [psT16e, psT16f], ["@ps6", "@ps7"], xqsl, xqks)
        W2Q.append(w2pair)
    for i in range(NEARLY, len(W2T)):
        def w2one(i=i):
            wname, scr, nt, K = W2T[i]
            for pc in range((K + 1023) // 1024):
                quant_multi([w2piece(wname, nt, K, pc, w_stage, "w_stage", 0, QSET0)])
            w2trans(wname, scr, nt, K, 0, [psT16, psT16b], ["@ps0", "@ps3"], wtsl, wttks)
        W2Q.append(w2one)
    def w2pump(n):
        while n > 0 and W2Q:
            W2Q.pop(0)()
            n -= 1
    S["w2pump"] = w2pump

    for ch in range(8):
        kb.dma(w_stage[:, :96], dram["w_xproj"][ch * 128:(ch + 1) * 128, :], ["w_stage"], ["w_stage"])
        kb.cp_s(w_xpb[:, ch * 96:(ch + 1) * 96], w_stage[:, :96], ["w_stage"], [f"w_xpb_{ch}"])
    kb.dma(w_stage[:1, :1024], dram["bias_ri"], ["w_stage"], ["w_stage"])
    kb.cp_s(bias_b[:], w_stage[:1, :1024], ["w_stage"], ["bias_b"])
    for half in range(2):
        kb.mm(ps[half][:, :512], ones_r[:], bias_b[:, half * 512:(half + 1) * 512],
              ["bias_b", "ones_r"], [f"ps_bias{half}", f"@ps{half}"])
        kb.cp_v(w_bias0[:, half * 512:(half + 1) * 512], ps[half][:, :512],
                [f"ps_bias{half}", f"@ps{half}"], ["w_bias0"])
    for ch in range(8):
        cstg, cstgk = (w_stage, "w_stage") if ch % 2 == 0 else (bufD, "bufDc")
        kb.dma(cstg[:, :1024], dram["w_o"][ch * 128:(ch + 1) * 128, :], [cstgk, "w_kv"], [cstgk])
        kb.cp_s(w_kv[:, :1024], cstg[:, :1024], [cstgk], ["w_kv"])
        kb.dma(dram["wo_scr"][2 * ch], w_kv[:64, :1024], ["w_kv"], [f"wo_scr_{2 * ch}"])
        kb.dma(dram["wo_scr"][2 * ch + 1], w_kv[64:128, :1024], ["w_kv"], [f"wo_scr_{2 * ch + 1}"])
        S["w2pump"](1)
    for ch in range(8):
        for piece in range(3):
            cstg, cstgk = (w_stage, "w_stage") if (ch * 3 + piece) % 2 == 0 else (bufD, "bufDc")
            kb.dma(cstg[:, :1024],
                   dram["w_qkv"][ch * 128:(ch + 1) * 128, piece * 1024:(piece + 1) * 1024],
                   [cstgk, "w_kv"], [cstgk])
            kb.cp_s(w_kv[:, piece * 1024:(piece + 1) * 1024], cstg[:, :1024],
                    [cstgk], ["w_kv"])
        kb.dma(dram["wqkv_scr"][ch], w_kv[:], ["w_kv"], [f"wqkv_scr_{ch}"])
        S["w2pump"](1)

    for i, (cn, gn) in enumerate([("coefT_R", "gmat_R"), ("coefT_I", "gmat_I")]):
        kb.dma(w_stage[:4, :1024], dram[cn], ["w_stage"], ["w_stage"])
        kb.cp_s(coefT[:, i * 1024:(i + 1) * 1024], w_stage[:4, :1024], ["w_stage"], [f"coefT_{i}", "w_str"])
        kb.dma(w_stage[:4, :256], dram[gn], ["w_stage"], ["w_stage"])
        kb.cp_s(gmat[:, i * 256:(i + 1) * 256], w_stage[:4, :256], ["w_stage"], [f"gmat_{i}", "w_kv"])
    for i in range(2):
        for ch in range(8):
            kb.mm(ps[2][:, :256], coefT[:, i * 1024 + ch * 128:i * 1024 + (ch + 1) * 128],
                  gmat[:, i * 256:(i + 1) * 256], [f"coefT_{i}", f"gmat_{i}", "wbt"], ["ps_wb", "@ps2"])
            kb.cp_v(wbt[:, :256], ps[2][:, :256], ["ps_wb", "@ps2"], ["wbt"])
            kb.dma(dram["wb_scr"][i, ch * 128:(ch + 1) * 128, :], wbt[:, :256], ["wbt"], [f"wbscr_{i}_{ch}"])
            S["w2pump"](1)
    wb_deps = [f"wbscr_{i}_{c}" for i in range(2) for c in range(8)]
    for ic in range(8):
        ihalf, icl = ic // 4, ic % 4
        for ohalf in range(2):
            src = 0 if (ihalf + ohalf) % 2 == 0 else 1
            neg = (ihalf == 1 and ohalf == 0)
            for dl in range(8):
                kb.dma(gat[dl * 16:(dl + 1) * 16, :].rearrange("p (b e) -> p b e", b=32),
                       dram["wb_scr"][src].rearrange("(b d) (g e) -> d g b e", b=32, g=16)[8 * icl + dl],
                       wb_deps + ["gat", "wbt"], [f"gat_{dl}"])
            kb.act(w_kv[:, ohalf * 512:(ohalf + 1) * 512], gat[:],
                   Act.Copy, [f"gat_{dl}" for dl in range(8)], ["gat", "wbt", f"mtst_{ic}_{ohalf}"],
                   scale=(-1.0 if neg else 1.0))
        kb.dma(dram["mt_scr"][ic], w_kv[:, 0:1024], [f"mtst_{ic}_0", f"mtst_{ic}_1"], [f"mt_{ic}"])

    kb.dma(w_stage[:16, :1], dram["log_scale"], ["w_stage"], ["w_stage"])
    kb.dma(dram["ls_scr"], w_stage[:16, :1], ["w_stage"], ["ls_scr"])
    for hh in range(2):
        src = dram["ls_scr"].rearrange("(dc two) one -> two (dc one)", two=2)[hh]
        src = src.rearrange("(o dc) -> o dc", o=1).broadcast_to([64, 8])
        kb.dma(rp_sc[64 * hh:64 * (hh + 1), 32:40], src, ["ls_scr"], [f"ls_rep_{hh}"],
               allow_slow_non_contiguous=True)
    kb.act(rp_sc[:, 0:8], rp_sc[:, 32:40], Act.Exp, ["ls_rep_0", "ls_rep_1"], ["s_rep"])
    kb.ts(rp_sc[:, 8:16], rp_sc[:, 0:8], c_wf[:, 0:1], None, Alu.mult, None,
          ["s_rep", "wfac"], ["omega"])
    for lev in range(9):
        m = TWO_PI * (2 ** (8 - lev))
        kb.ts(rp_sc[:, 16:24], rp_sc[:, 8:16], m, None, Alu.is_ge, None, ["omega"], ["rr_t"])
        kb.stt(rp_sc[:, 8:16], rp_sc[:, 16:24], -m, rp_sc[:, 8:16], Alu.mult, Alu.add,
               ["rr_t", "omega"], ["omega"])
    kb.ts(rp_sc[:, 16:24], rp_sc[:, 8:16], math.pi, None, Alu.is_ge, None, ["omega"], ["rr_t"])
    kb.stt(rp_sc[:, 8:16], rp_sc[:, 16:24], -TWO_PI, rp_sc[:, 8:16], Alu.mult, Alu.add,
           ["rr_t", "omega"], ["omega"])
    kb.act(rp_sc[:, 24:32], rp_sc[:, 8:16], Act.Sin, ["omega"], ["seed_sh"], scale=0.5)
    kb.act(rp_powS[:, 0:8], rp_sc[:, 8:16], Act.Sin, ["omega"], ["powS_0"])
    kb.tt(rp_sc[:, 16:24], rp_sc[:, 24:32], rp_sc[:, 24:32], Alu.mult, ["seed_sh"], ["sh2"])
    kb.ts(rp_powC[:, 0:8], rp_sc[:, 16:24], -2.0, 1.0, Alu.mult, Alu.add, ["sh2"], ["powC_0"])
    for k in range(11):
        kb.tt(rp_sc[:, 16:24], rp_powC[:, k * 8:(k + 1) * 8], rp_powC[:, k * 8:(k + 1) * 8],
              Alu.mult, [f"powC_{k}"], ["pk_c2"])
        kb.ts(rp_powC[:, (k + 1) * 8:(k + 2) * 8], rp_sc[:, 16:24], 2.0, -1.0, Alu.mult, Alu.add,
              ["pk_c2"], [f"powC_{k + 1}"])
        kb.tt(rp_sc[:, 24:32], rp_powS[:, k * 8:(k + 1) * 8], rp_powC[:, k * 8:(k + 1) * 8],
              Alu.mult, [f"powS_{k}", f"powC_{k}"], ["pk_sc"])
        kb.ts(rp_powS[:, (k + 1) * 8:(k + 2) * 8], rp_sc[:, 24:32], 2.0, None, Alu.mult, None,
              ["pk_sc"], [f"powS_{k + 1}"])

    def bc8(t, width):
        return t.rearrange("p (o dc) -> p o dc", o=1).broadcast_to([128, width, 8])

    def r3(t, width):
        return t.rearrange("p (w dc) -> p w dc", dc=8)

    def rot_block(dstC, dstS, srcC, srcS, pk, rdeps, wtagC, wtagS, width):
        pc = bc8(rp_powC[:, pk * 8:(pk + 1) * 8], width)
        pss = bc8(rp_powS[:, pk * 8:(pk + 1) * 8], width)
        n = width * 8
        kb.tt(r3(rp_t1[:, 0:n], width), r3(srcC, width), pc, Alu.mult, rdeps, ["rp_t1"])
        kb.tt(r3(rp_t2[:, 0:n], width), r3(srcS, width), pss, Alu.mult, rdeps, ["rp_t2"])
        kb.tt(dstC, rp_t1[:, 0:n], rp_t2[:, 0:n], Alu.subtract, ["rp_t1", "rp_t2"], [wtagC])
        kb.tt(r3(rp_t1[:, 0:n], width), r3(srcS, width), pc, Alu.mult, rdeps + [wtagC], ["rp_t1"])
        kb.tt(r3(rp_t2[:, 0:n], width), r3(srcC, width), pss, Alu.mult, rdeps + [wtagC], ["rp_t2"])
        kb.tt(dstS, rp_t1[:, 0:n], rp_t2[:, 0:n], Alu.add, ["rp_t1", "rp_t2"], [wtagS])

    kb.memset(finC[:, 0:8], 1.0, ["finC_b0"])
    kb.memset(finS[:, 0:8], 0.0, ["finS_b0"])
    for k in range(5):
        w = 2 ** k
        rot_block(finC[:, w * 8:2 * w * 8], finS[:, w * 8:2 * w * 8],
                  finC[:, 0:w * 8], finS[:, 0:w * 8], k,
                  [f"finC_b{j}" for j in range(k + 1)] + [f"finS_b{j}" for j in range(k + 1)]
                  + [f"powC_{k}", f"powS_{k}"], f"finC_b{k + 1}", f"finS_b{k + 1}", w)
    kb.memset(coaC[:, 0:8], 1.0, ["coaC_b0"])
    kb.memset(coaS[:, 0:8], 0.0, ["coaS_b0"])
    for k in range(6):
        w = 2 ** k
        rot_block(coaC[:, w * 8:2 * w * 8], coaS[:, w * 8:2 * w * 8],
                  coaC[:, 0:w * 8], coaS[:, 0:w * 8], 5 + k,
                  [f"coaC_b{j}" for j in range(k + 1)] + [f"coaS_b{j}" for j in range(k + 1)]
                  + [f"powC_{5 + k}", f"powS_{5 + k}"], f"coaC_b{k + 1}", f"coaS_b{k + 1}", w)
    fin_done = [f"finC_b{j}" for j in range(6)] + [f"finS_b{j}" for j in range(6)]
    coa_done = [f"coaC_b{j}" for j in range(7)] + [f"coaS_b{j}" for j in range(7)]
    S["fin_done"] = fin_done
    S["coa_done"] = coa_done
    S["quant_slice"] = None

    if stage == "weights":
        for ic in range(4):
            kb.cp_s(sq_s[:], w_ffn0[:, ic * 1024:(ic + 1) * 1024], [f"w_ffn0_{ic}", "sq_s"], ["sq_s"])
            kb.dma(dram["out"][ic * 128:(ic + 1) * 128, :], sq_s[:], ["sq_s"], ["outd", "sq_s2"])
        kb.dma(dram["out"][512:640, 0:256], finC[:], fin_done, ["outd2"])
        kb.dma(dram["out"][512:640, 256:512], finS[:], fin_done, ["outd3"])
        kb.dma(dram["out"][640:768, 0:512], coaC[:], coa_done, ["outd4"])
        kb.dma(dram["out"][640:768, 512:1024], coaS[:], coa_done, ["outd5"])
        kb.dma(dram["out"][768:896, 0:128], w_acol[:], ["a_col"], ["outd6"])
        kb.dma(dram["out"][768:896, 128:129], rp_sc[:, 8:9], ["omega"], ["outd7"],
               allow_slow_non_contiguous=True)
        block = es.enter_context(nc.Block())
        es.enter_context(kb.lay(block))
        return nc, dram, kb, es


    def xpre(c2):
        cc2 = f"c{c2}"
        for j in range(4):
            kb.dma(bufB[:, j * DM:(j + 1) * DM],
                   dram["x"][c2 * TC + j * 128: c2 * TC + (j + 1) * 128, :],
                   [], [f"xB_{cc2}_{j}"])
        for j in range(4):
            xa = bufB[:, j * DM:(j + 1) * DM]
            kb.act(sq_s[:], xa, Act.Square, [f"xB_{cc2}_{j}", "sq_s"], ["sq_s"])
            kb.red(stat[:, j:j + 1], xa, Alu.add, [f"xB_{cc2}_{j}"], [f"st_sum_{j}"])
            kb.red(stat[:, 4 + j:5 + j], sq_s[:], Alu.add, ["sq_s"], [f"st_sq_{j}"])
            kb.ts(stat[:, 8 + j:9 + j], stat[:, j:j + 1], 1.0 / DM, None, Alu.mult, None,
                  [f"st_sum_{j}"], [f"st_mean_{j}"])
            kb.ts(stat[:, 12 + j:13 + j], stat[:, 8 + j:9 + j], -1.0, None, Alu.mult, None,
                  [f"st_mean_{j}"], [f"st_nm_{j}"])
            kb.ts(stat[:, 16 + j:17 + j], stat[:, 4 + j:5 + j], 1.0 / DM, None, Alu.mult, None,
                  [f"st_sq_{j}"], [f"st_msq_{j}"])
            kb.stt(stat[:, 20 + j:21 + j], stat[:, 12 + j:13 + j], stat[:, 8 + j:9 + j],
                   stat[:, 16 + j:17 + j], Alu.mult, Alu.add,
                   [f"st_nm_{j}", f"st_mean_{j}", f"st_msq_{j}"], [f"st_var_{j}"])
            kb.act(stat[:, 24 + j:25 + j], stat[:, 20 + j:21 + j], Act.Ln,
                   [f"st_var_{j}"], [f"st_ln_{j}"], bias=1e-5)
            kb.act(stat[:, 28 + j:29 + j], stat[:, 24 + j:25 + j], Act.Exp,
                   [f"st_ln_{j}"], [f"st_rstd_{j}"], scale=-0.5)
    S["xpre"] = xpre

    for c in range(NCH):
        cc = f"c{c}"
        if c == 0:
            xpre(0)
        for j in range(4):
            kb.cp_s(bufA[:, j * DM:(j + 1) * DM], bufB[:, j * DM:(j + 1) * DM],
                    [f"xB_{cc}_{j}", f"bufA_{cc}"], [f"xA_{cc}_{j}", f"bufA_{cc}"])
        for j in range(4):
            kb.ts(bufB[:, j * DM:(j + 1) * DM], bufB[:, j * DM:(j + 1) * DM],
                  stat[:, 8 + j:9 + j], stat[:, 28 + j:29 + j], Alu.subtract, Alu.mult,
                  [f"xB_{cc}_{j}", f"xA_{cc}_{j}", f"st_mean_{j}", f"st_rstd_{j}", f"bufB_{cc}"],
                  [f"xn_{cc}_{j}", f"bufB_{cc}"])
        for dc in range(8):
            for j in range(4):
                pr = (dc * 4 + j) % 2
                bk = ps[0] if pr == 0 else ps[3]
                pkx = "@ps0" if pr == 0 else "@ps3"
                kb.transp(bk[:, :128], bufB[:, j * DM + dc * 128: j * DM + (dc + 1) * 128],
                          c_identf[:], [f"xn_{cc}_{j}", "ident_f32"], [f"psT{pr}x", pkx])
                kb.act(xn_fb[:, dc * TC + j * 128: dc * TC + (j + 1) * 128], bk[:, :128],
                       Act.Copy, [f"psT{pr}x", pkx], [f"xnf_{cc}_{dc}"])
        xnf_all = [f"xnf_{cc}_{dc}" for dc in range(8)]
        kb.mmg([(ps[6][:64, :512], w_xpb[:, ch * 96: ch * 96 + 64],
                 xn_fb[:, ch * TC:(ch + 1) * TC], ch == 0, ch == 7) for ch in range(8)],
               xnf_all + [f"w_xpb_{ch}" for ch in range(8)], [f"ps_dt_{cc}", "@ps6"])
        kb.mmg([(ps[7][:32, :512], w_xpb[:, ch * 96 + 64: ch * 96 + 96],
                 xn_fb[:, ch * TC:(ch + 1) * TC], ch == 0, ch == 7) for ch in range(8)],
               xnf_all + [f"w_xpb_{ch}" for ch in range(8)], [f"ps_bc_{cc}", "@ps7"])
        kb.act(xp_b[:, :], ps[6][:64, :512], Act.Copy, [f"ps_dt_{cc}", "@ps6"], [f"xpb_{cc}"])
        kb.act(xp_bc[:, :], ps[7][:32, :512], Act.Copy, [f"ps_bc_{cc}", "@ps7"], [f"xpbc_{cc}"])
        kb.mm(ps[6][:, :512], c_R[:, 0:128], xp_bc[:], [f"xpbc_{cc}", "Rsel"], [f"ps_Bb_{cc}", "@ps6"])
        kb.cp_v(bc_f[:, 0:512], ps[6][:, :512], [f"ps_Bb_{cc}", "@ps6"], [f"bcB_{cc}"])
        kb.mm(ps[6][:, :512], c_R[:, 128:256], xp_bc[:], [f"xpbc_{cc}", "Rsel"], [f"ps_Cb_{cc}", "@ps6"])
        kb.cp_v(bc_f[:, 512:1024], ps[6][:, :512], [f"ps_Cb_{cc}", "@ps6"], [f"bcC_{cc}"])
        dAb = [dA_s[:], tbl[:, 0:512]]
        dBxb = [dBx_s[:], tbl[:, 512:1024]]
        hb = [h_s[:], av_s[:]]
        hcb = [hc_s[:], qs_c[:]]
        deltab = [delta_b[:], k_c[:]]
        ub = [u_b[:], ks_c[:]]
        drb = [3, 1]
        urb = [4, 2]
        dpb = [7, 6]
        for dc in range(8):
            p2 = dc % 2
            dlt, uu = deltab[p2], ub[p2]
            dpk = dpb[p2]
            kb.mmg([(ps[dpk][:, :512], w_dtt[:, dc * 128:(dc + 1) * 128], xp_b[:], True, True)],
                   [f"xpb_{cc}", "w_dtt"], [f"ps_dp_{cc}_{dc}", f"@ps{dpk}"])
            kb.ts(sq_s[:, p2 * 512:(p2 + 1) * 512], ps[dpk][:, :512], w_bdt[:, dc:dc + 1], 88.0,
                  Alu.add, Alu.min,
                  [f"ps_dp_{cc}_{dc}", f"@ps{dpk}", "b_dt_col"], [f"dpre_{cc}_{dc}"])
            kb.act(sq_s[:, p2 * 512:(p2 + 1) * 512], sq_s[:, p2 * 512:(p2 + 1) * 512], Act.Exp,
                   [f"dpre_{cc}_{dc}"], [f"dexp_{cc}_{dc}"])
            kb.act(dlt, sq_s[:, p2 * 512:(p2 + 1) * 512], Act.Ln, [f"dexp_{cc}_{dc}"],
                   [f"delta_{cc}_{dc}"], bias=1.0)
            kb.tt(uu, dlt, xn_fb[:, dc * TC:(dc + 1) * TC], Alu.mult,
                  [f"delta_{cc}_{dc}", f"xnf_{cc}_{dc}"], [f"u_{cc}_{dc}"])
            pend_sel = []
            for g0 in range(0, 16, 2):
                pair = [(g0 + i, (g0 + i) % 2, f"{cc}_{dc}_{g0 + i}") for i in range(2)]
                for g, gp, gk in pair:
                    kb.mm(ps[drb[gp]][:, :512], c_E[:, g * 128:(g + 1) * 128], dlt,
                          [f"delta_{cc}_{dc}", "Esel"], [f"ps_dr_{gk}", f"@ps{drb[gp]}"])
                    kb.mm(ps[urb[gp]][:, :512], c_E[:, g * 128:(g + 1) * 128], uu,
                          [f"u_{cc}_{dc}", "Esel"], [f"ps_ur_{gk}", f"@ps{urb[gp]}"])
                for fn_ in pend_sel:
                    fn_()
                pend_sel = []
                for g, gp, gk in pair:
                    kb.act(dAb[gp], ps[drb[gp]][:, :512], Act.Exp, [f"ps_dr_{gk}", f"@ps{drb[gp]}", "a_col"],
                           [f"dA_{gk}"], scale=w_acol[:, dc * 16 + g: dc * 16 + g + 1])
                for g, gp, gk in pair:
                    kb.tt(dBxb[gp], ps[urb[gp]][:, :512], bc_f[:, 0:512], Alu.mult,
                          [f"ps_ur_{gk}", f"@ps{urb[gp]}", f"bcB_{cc}"], [f"dBx_{gk}"])
                for g, gp, gk in pair:
                    init = 0.0 if c == 0 else carry[:, dc * 16 + g: dc * 16 + g + 1]
                    rd = [f"dA_{gk}", f"dBx_{gk}"] + ([] if c == 0 else [f"carry_{dc}_{g}"])
                    kb.scan(hb[gp], dAb[gp], dBxb[gp], init, rd, [f"h_{gk}"])
                for g, gp, gk in pair:
                    if c < NCH - 1:
                        kb.cp_v(carry[:, dc * 16 + g: dc * 16 + g + 1], hb[gp][:, 511:512],
                                [f"h_{gk}"], [f"carry_{dc}_{g}"])
                    kb.tt(hcb[gp], hb[gp], bc_f[:, 512:1024], Alu.mult,
                          [f"h_{gk}", f"bcC_{cc}"], [f"hc_{gk}"], eng="gpsimd")
                for g, gp, gk in pair:
                    def selfn(g=g, gp=gp, gk=gk):
                        kb.mm(ps[5][:, :512], c_S[:, g * 128:(g + 1) * 128], hcb[gp],
                              [f"hc_{gk}", "Ssel"], [f"ps_y_{cc}_{dc}", "@ps5"],
                              start=(g == 0), stop=(g == 15))
                    pend_sel.append(selfn)
            for fn_ in pend_sel:
                fn_()
            kb.stt(z_s[:], xn_fb[:, dc * TC:(dc + 1) * TC], w_dcol[:, dc:dc + 1],
                   ps[5][:, :512], Alu.mult, Alu.add,
                   [f"ps_y_{cc}_{dc}", "@ps5", f"xnf_{cc}_{dc}", "d_col"], [f"z_{cc}_{dc}"])
            for j in range(4):
                bank, pk = (ps[0], "@ps0") if j % 2 == 0 else (ps[6], "@ps6")
                kb.transp(bank[:, :128], z_s[:, j * 128:(j + 1) * 128], c_identf[:],
                          [f"z_{cc}_{dc}", "ident_f32"], [f"psT{j % 2}", pk])
                kb.tt(bufA[:, j * DM + dc * 128: j * DM + (dc + 1) * 128],
                      bufA[:, j * DM + dc * 128: j * DM + (dc + 1) * 128], bank[:, :128],
                      Alu.add, [f"psT{j % 2}", pk, f"xA_{cc}_{j}", f"bufA_{cc}"],
                      [f"x2p_{cc}_{dc}_{j}", f"bufA_{cc}"])
        f0banks = [ps[1], ps[2], ps[4], ps[5]]
        for half in range(2):
            for ic in range(8):
                par = ic % 2
                kb.dma(w_f0[:, par * 512:(par + 1) * 512],
                       dram["mt_scr"][ic][:, half * 512:(half + 1) * 512],
                       [f"mt_{ic}", f"f0w_{par}"], [f"f0w_{par}"])
                for j in range(4):
                    kb.mm(f0banks[j][:, :512],
                          xn_fb[:, ic * TC + j * 128: ic * TC + (j + 1) * 128],
                          w_f0[:, par * 512:(par + 1) * 512],
                          xnf_all + [f"f0w_{par}"],
                          [f"ps_f0_{cc}_{j}_{half}", f"@psf0{j}", f"f0w_{par}"],
                          start=(ic == 0), stop=(ic == 7))
                if not (half == 1 and ic >= 4):
                    S["w2pump"](1)
            for j in range(4):
                kb.tt(bufD[:, half * 512:(half + 1) * 512], f0banks[j][:, :512],
                      w_bias0[:, half * 512:(half + 1) * 512], Alu.add,
                      [f"ps_f0_{cc}_{j}_{half}", f"@psf0{j}", "w_bias0", f"bufDh_{j % 2}"],
                      [f"f0_{cc}_{j}_{half}", f"bufDh_{j % 2}"])
                kb.tt(bufA[:, j * DM + half * 512:j * DM + (half + 1) * 512],
                      bufA[:, j * DM + half * 512:j * DM + (half + 1) * 512],
                      bufD[:, half * 512:(half + 1) * 512],
                      Alu.add,
                      [f"f0_{cc}_{j}_{half}", f"bufDh_{j % 2}", f"bufA_{cc}"]
                      + [f"x2p_{cc}_{dc}_{j}" for dc in range(8)],
                      [f"x2_{cc}_{j}_{half}", f"bufA_{cc}"])
        if stage == "l0":
            for j in range(4):
                kb.dma(dram["out"][c * TC + j * 128: c * TC + (j + 1) * 128, :],
                       bufA[:, j * DM:(j + 1) * DM], [f"x2_{cc}_{j}"], [f"outd_{cc}_{j}"])
        elif stage in ("full", "l1"):
            build_l1(kb, S, c, NCH, stage)

    block = es.enter_context(nc.Block())
    es.enter_context(kb.lay(block))
    return nc, dram, kb, es


def build_l1(kb, S, c, NCH, stage):
    ps = S["ps"]; psT16 = S["psT16"]; psT16b = S["psT16b"]; dram = S["dram"]
    cc = f"c{c}"
    g = S
    bufA, bufB, bufD = g["bufA"], g["bufB"], g["bufD"]
    sq_s, stat, xn_fb = g["sq_s"], g["stat"], g["xn_fb"]
    c_identf, c_identb = g["c_identf"], g["c_identb"]
    w_kv, w_oh, w_f0, wtt = g["w_kv"], g["w_oh"], g["w_f0"], g["wtt"]
    tbl, q_c, k_c, qs_c, ks_c = g["tbl"], g["q_c"], g["k_c"], g["qs_c"], g["ks_c"]
    rq_s, rkh, vh, rqh, avb = g["rq_s"], g["rkh"], g["vh"], g["rqh"], g["avb"]
    nrm_k, nrm_q, mh_s, shf_s = g["nrm_k"], g["nrm_q"], g["mh_s"], g["shf_s"]
    pt_s, av_s, rcp_s = g["pt_s"], g["av_s"], g["rcp_s"]
    g_s, hh_s, hq_t = g["g_s"], g["hh_s"], g["hq_t"]
    xq_t, xq_f = g["xq_t"], g["xq_f"]
    wg_b, wu_b = g["wg_b"], g["wu_b"]
    coaC, coaS, finC, finS = g["coaC"], g["coaS"], g["finC"], g["finS"]
    z_s, hc_s = g["z_s"], g["hc_s"]
    c_o16, c_eh, c_mask, c_pm, c_osq = g["c_o16"], g["c_eh"], g["c_mask"], g["c_pm"], g["c_osq"]
    c_psw = g["c_psw"]
    ls_b = g["ls_b"]
    rp1, rp2 = g["rp1"], g["rp2"]
    bc_f, dA_s, dBx_s, h_s, z_s = g["bc_f"], g["dA_s"], g["dBx_s"], g["h_s"], g["z_s"]
    q_va, q_cc, q_ee, q_sp, q_am, q_rs = g["q_va"], g["q_cc"], g["q_ee"], g["q_sp"], g["q_am"], g["q_rs"]
    wnk = g["wnk"]
    quant_slice = S["quant_slice"]
    quant_wide = S["quant_wide"]
    fin_done, coa_done = S["fin_done"], S["coa_done"]

    for j in range(4):
        xa = bufA[:, j * DM:(j + 1) * DM]
        kb.act(sq_s[:], xa, Act.Square, [f"x2_{cc}_{j}_0", f"x2_{cc}_{j}_1"], ["sq_s"])
        kb.red(stat[:, j:j + 1], xa, Alu.add, [f"x2_{cc}_{j}_0", f"x2_{cc}_{j}_1"], [f"l2_sum_{j}"])
        kb.red(stat[:, 4 + j:5 + j], sq_s[:], Alu.add, ["sq_s"], [f"l2_sq_{j}"])
        kb.ts(stat[:, 8 + j:9 + j], stat[:, j:j + 1], 1.0 / DM, None, Alu.mult, None,
              [f"l2_sum_{j}"], [f"l2_mean_{j}"])
        kb.ts(stat[:, 12 + j:13 + j], stat[:, 8 + j:9 + j], -1.0, None, Alu.mult, None,
              [f"l2_mean_{j}"], [f"l2_nm_{j}"])
        kb.ts(stat[:, 16 + j:17 + j], stat[:, 4 + j:5 + j], 1.0 / DM, None, Alu.mult, None,
              [f"l2_sq_{j}"], [f"l2_msq_{j}"])
        kb.stt(stat[:, 20 + j:21 + j], stat[:, 12 + j:13 + j], stat[:, 8 + j:9 + j],
               stat[:, 16 + j:17 + j], Alu.mult, Alu.add,
               [f"l2_nm_{j}", f"l2_mean_{j}", f"l2_msq_{j}"], [f"l2_var_{j}"])
        kb.act(stat[:, 24 + j:25 + j], stat[:, 20 + j:21 + j], Act.Ln,
               [f"l2_var_{j}"], [f"l2_ln_{j}"], bias=1e-5)
        kb.act(stat[:, 28 + j:29 + j], stat[:, 24 + j:25 + j], Act.Exp,
               [f"l2_ln_{j}"], [f"l2_rstd_{j}"], scale=-0.5)
        kb.ts(bufB[:, j * DM:(j + 1) * DM], xa, stat[:, 8 + j:9 + j],
              stat[:, 28 + j:29 + j], Alu.subtract, Alu.mult,
              [f"x2_{cc}_{j}_0", f"x2_{cc}_{j}_1", f"l2_mean_{j}", f"l2_rstd_{j}"],
              [f"xn2_{cc}_{j}"])
    for dc in range(8):
        for j in range(4):
            bank, pk = (ps[0], "@ps0") if j % 2 == 0 else (ps[3], "@ps3")
            kb.transp(bank[:, :128], bufB[:, j * DM + dc * 128: j * DM + (dc + 1) * 128],
                      c_identf[:], [f"xn2_{cc}_{j}"], [f"psT{j % 2}", pk])
            kb.act(xn_fb[:, dc * TC + j * 128: dc * TC + (j + 1) * 128], bank[:, :128],
                   Act.Copy, [f"psT{j % 2}", pk], [f"x2f_{cc}_{dc}"])
    x2f_all = [f"x2f_{cc}_{dc}" for dc in range(8)]

    for dc in range(8):
        for wsec in range(3):
            kb.dma(w_kv[:, wsec * 1024:(wsec + 1) * 1024].rearrange("p (ch m) -> p ch m", ch=8),
                   dram["wqkv_scr"].rearrange("ch p m -> p ch m")[:, :, (wsec * 8 + dc) * 128:(wsec * 8 + dc + 1) * 128],
                   [f"wqkv_scr_{ch2}" for ch2 in range(8)] + [f"kvb{wsec}"], [f"kvb{wsec}"],
                   q="gpsimd")
        for wsec, bank in [(0, 1), (1, 2), (2, 3)]:
            for ch in range(8):
                kb.mm(ps[bank][:, :512], w_kv[:, wsec * 1024 + ch * 128: wsec * 1024 + (ch + 1) * 128],
                      xn_fb[:, ch * TC:(ch + 1) * TC],
                      [f"x2f_{cc}_{ch}", f"kvb{wsec}"],
                      [f"ps_qkv_{cc}_{dc}_{wsec}", f"@ps{bank}", f"kvb{wsec}"],
                      start=(ch == 0), stop=(ch == 7))
        t1o = 16 * c
        cslice = lambda tb: tb[:, (t1o * 8 + dc):(t1o * 8 + dc + 121):8]
        fslice = lambda tb: tb[:, dc::8]
        def bc_c(ap):
            return ap.rearrange("p (t o) -> p t o", o=1).broadcast_to([128, 16, 32])
        def bfin(ap):
            return ap.rearrange("p (o t) -> p o t", o=1).broadcast_to([128, 16, 32])
        r3t = lambda t: t.rearrange("p (a b) -> p a b", a=16)
        kb.tt(r3t(rp1[:]), bc_c(cslice(coaC)), bfin(fslice(finC)), Alu.mult,
              coa_done + fin_done, ["rp1"])
        kb.tt(r3t(rp2[:]), bc_c(cslice(coaS)), bfin(fslice(finS)), Alu.mult,
              coa_done + fin_done, ["rp2"])
        kb.tt(tbl[:, 0:512], rp1[:], rp2[:], Alu.subtract, ["rp1", "rp2"], [f"tbc_{cc}_{dc}"])
        kb.tt(r3t(rp1[:]), bc_c(cslice(coaS)), bfin(fslice(finC)), Alu.mult,
              coa_done + fin_done, ["rp1"])
        kb.tt(r3t(rp2[:]), bc_c(cslice(coaC)), bfin(fslice(finS)), Alu.mult,
              coa_done + fin_done, ["rp2"])
        kb.tt(tbl[:, 512:1024], rp1[:], rp2[:], Alu.add, ["rp1", "rp2"], [f"tbs0_{cc}_{dc}"])
        kb.ts(tbl[:, 512:1024], tbl[:, 512:1024], c_pm[:, 0:1], None, Alu.mult, None,
              [f"tbs0_{cc}_{dc}"], [f"tbs_{cc}_{dc}"])
        kb.ts(tbl[:, 1024:1536], tbl[:, 0:512], 0.125, None, Alu.mult, None,
              [f"tbc_{cc}_{dc}"], [f"tbc8_{cc}_{dc}"])
        kb.ts(tbl[:, 1536:2048], tbl[:, 512:1024], 0.125, None, Alu.mult, None,
              [f"tbs_{cc}_{dc}"], [f"tbs8_{cc}_{dc}"])
        kb.act(q_c[:], ps[1][:, :512], Act.Copy, [f"ps_qkv_{cc}_{dc}_0", "@ps1"], [f"qc_{cc}_{dc}"])
        kb.act(k_c[:], ps[2][:, :512], Act.Copy, [f"ps_qkv_{cc}_{dc}_1", "@ps2"], [f"kc_{cc}_{dc}"])
        kb.mm(ps[0][:, :512], c_psw[:], q_c[:], [f"qc_{cc}_{dc}", "pswap"],
              [f"qsp_{cc}_{dc}", "@ps0"])
        kb.tt(pt_s[:, 0:512], q_c[:], tbl[:, 1024:1536], Alu.mult,
              [f"qc_{cc}_{dc}", f"tbc8_{cc}_{dc}"], ["pt1"])
        kb.tt(pt_s[:, 512:1024], ps[0][:, :512], tbl[:, 1536:2048], Alu.mult,
              [f"qsp_{cc}_{dc}", "@ps0", f"tbs8_{cc}_{dc}"], ["pt2"])
        kb.tt(rq_s[:, dc * 512:(dc + 1) * 512], pt_s[:, 0:512], pt_s[:, 512:1024], Alu.add,
              ["pt1", "pt2"], [f"rq_{cc}_{dc}"])
        kb.mm(ps[0][:, :512], c_psw[:], k_c[:], [f"kc_{cc}_{dc}", "pswap"],
              [f"ksp_{cc}_{dc}", "@ps0"])
        kb.tt(pt_s[:, 0:512], k_c[:], tbl[:, 0:512], Alu.mult,
              [f"kc_{cc}_{dc}", f"tbc_{cc}_{dc}"], ["pt1"], eng="gpsimd")
        kb.tt(pt_s[:, 512:1024], ps[0][:, :512], tbl[:, 512:1024], Alu.mult,
              [f"ksp_{cc}_{dc}", "@ps0", f"tbs_{cc}_{dc}"], ["pt2"])
        kb.tt(ks_c[:], pt_s[:, 0:512], pt_s[:, 512:1024], Alu.add,
              ["pt1", "pt2"], [f"rk_{cc}_{dc}"])
        kb.dma(dram["rk_scr"][dc][:, c * 512:(c + 1) * 512], ks_c[:], [f"rk_{cc}_{dc}"], [f"rks_{cc}_{dc}"])
        kb.act(q_c[:], rq_s[:, dc * 512:(dc + 1) * 512], Act.Square, [f"rq_{cc}_{dc}"], [f"q2_{cc}_{dc}"])
        kb.mm(ps[6][:16, :512], c_o16[:, dc * 16:(dc + 1) * 16], q_c[:],
              [f"q2_{cc}_{dc}", "ones16"], [f"ps_qn_{cc}", "@ps6"], start=(dc == 0), stop=(dc == 7))
        kb.act(qs_c[:], ks_c[:], Act.Square, [f"rk_{cc}_{dc}"], [f"k2_{cc}_{dc}"])
        kb.mm(ps[7][:16, :512], c_o16[:, dc * 16:(dc + 1) * 16], qs_c[:],
              [f"k2_{cc}_{dc}", "ones16"], [f"ps_kn_{cc}", "@ps7"], start=(dc == 0), stop=(dc == 7))
        kb.act(qs_c[:], ps[3][:, :512], Act.Copy, [f"ps_qkv_{cc}_{dc}_2", "@ps3"], [f"vb_{cc}_{dc}"])
        for j in range(4):
            pv = psT16 if j % 2 == 0 else psT16b
            pk = "@ps0" if j % 2 == 0 else "@ps3"
            wslot = wtt[:, j * 128:(j + 1) * 128]
            kb.transp(pv[:, :], qs_c[:, j * 128:(j + 1) * 128], c_identb[:],
                      [f"vb_{cc}_{dc}", "ident_bf16"], [f"psT{j % 2}", pk])
            kb.act(wslot, pv[:, :], Act.Copy, [f"psT{j % 2}", pk], [f"wtt{j}"])
            kb.dma(dram["v_scr"][c * 4 + j][:, dc * 128:(dc + 1) * 128], wslot,
                   [f"wtt{j}"], [f"vs_{cc}_{dc}_{j}"])
    kb.cp_v(nrm_q[:, :], ps[6][:16, :512], [f"ps_qn_{cc}", "@ps6"], [f"nq_{cc}"])
    kb.cp_v(nrm_k[:, :], ps[7][:16, :512], [f"ps_kn_{cc}", "@ps7"], [f"nk_{cc}"])
    kb.red(mh_s[:16, 0:1], nrm_k[:, :], Alu.max, [f"nk_{cc}"], [f"mh0_{cc}"])
    if c == 0:
        kb.cp_v(mh_s[:16, 1:2], mh_s[:16, 0:1], [f"mh0_{cc}"], [f"mh1_{cc}"])
    else:
        kb.tt(mh_s[:16, 1:2], mh_s[:16, 1:2], mh_s[:16, 0:1], Alu.max,
              [f"mh0_{cc}", f"mh1_c{c - 1}"], [f"mh1_{cc}"])
    kb.act(nrm_q[:, :], nrm_q[:, :], Act.Ln, [f"nq_{cc}"], [f"nq2_{cc}"], bias=1e-30)
    kb.act(nrm_q[:, :], nrm_q[:, :], Act.Exp, [f"nq2_{cc}"], [f"nq3_{cc}"], scale=0.5)
    kb.red(mh_s[:16, 2:3], nrm_q[:, :], Alu.max, [f"nq3_{cc}"], [f"mh2_{cc}"])
    kb.act(mh_s[:16, 3:4], mh_s[:16, 1:2], Act.Ln, [f"mh1_{cc}"], [f"mh3_{cc}"], bias=1e-30)
    kb.act(mh_s[:16, 3:4], mh_s[:16, 3:4], Act.Exp, [f"mh3_{cc}"], [f"mh4_{cc}"], scale=0.5)
    kb.tt(mh_s[:16, 5:6], mh_s[:16, 2:3], mh_s[:16, 3:4], Alu.mult,
          [f"mh2_{cc}", f"mh4_{cc}"], [f"mh5_{cc}"])
    kb.ts(mh_s[:16, 5:6], mh_s[:16, 5:6], -0.125, None, Alu.mult, None, [f"mh5_{cc}"], [f"mh6_{cc}"])
    kb.cp_s(ls_b[:], mh_s[:16, 5:6], [f"mh6_{cc}"], [f"negB_{cc}"])
    for h in range(NH):
        shb, shk = (ps[0], "@ps0") if h % 2 == 0 else (ps[3], "@ps3")
        kb.mm(shb[:, 0:1], c_eh[:, h * 128:(h + 1) * 128], ls_b[:],
              [f"negB_{cc}", "eh"], [f"ps_sh_{cc}_{h}", shk])
        kb.cp_v(shf_s[:, h:h + 1], shb[:, 0:1], [f"ps_sh_{cc}_{h}", shk], [f"shf_{cc}_{h}"])

    nkt = 4 * (c + 1)
    for h in range(NH):
        dc, rows = h // 2, (h % 2) * 64
        kb.dma(rqh[:, :], rq_s[rows:rows + 64, dc * 512:(dc + 1) * 512],
               [f"rq_{cc}_{dc}"], [f"rqh_{cc}_{h}"], q="gpsimd")
        kb.dma(rkh[:, 0:nkt * 128], dram["rk_scr"][dc][rows:rows + 64, 0:nkt * 128],
               [f"rks_c{ci}_{dc}" for ci in range(c + 1)], [f"rkh_{cc}_{h}"], q="gpsimd")
        kb.dma(vh[:, 0:nkt * 65].rearrange("p (kt e) -> p kt e", e=65)[:, :, 0:64],
               dram["v_scr"].rearrange("kt p e -> p kt e")[:, 0:nkt, h * 64:(h + 1) * 64],
               [f"vs_c{ci}_{h // 2}_{jj}" for ci in range(c + 1) for jj in range(4)],
               [f"vh_{cc}_{h}"], q="gpsimd")
        sbanks = [4, 6, 1, 2]
        ptslots = [pt_s[:, 0:512], pt_s[:, 512:1024], tbl[:, 0:512], tbl[:, 512:1024]]
        pend_av = []
        for kt in range(nkt):
            sbk = sbanks[kt % 4]
            pts = ptslots[kt % 4]
            kb.mm(ps[sbk][:, :512], rkh[:, kt * 128:(kt + 1) * 128], rqh[:, :],
                  [f"rkh_{cc}_{h}", f"rqh_{cc}_{h}"], [f"ps_s_{cc}_{h}_{kt}", f"@ps{sbk}"])
            for fn_ in pend_av:
                fn_()
            pend_av = []
            kb.act(pts, ps[sbk][:, :512], Act.Exp,
                   [f"ps_s_{cc}_{h}_{kt}", f"@ps{sbk}", f"shf_{cc}_{h}"],
                   [f"pt_{cc}_{h}_{kt}"], bias=shf_s[:, h:h + 1])
            if kt >= 4 * c:
                kb.tt(pts, pts, c_mask[:, (kt - 4 * c) * 512:(kt - 4 * c + 1) * 512],
                      Alu.mult, [f"pt_{cc}_{h}_{kt}", "cmask"], [f"pt_{cc}_{h}_{kt}"])
            avb_bk = 5 if h % 2 == 0 else 7
            rr_bk = 0 if h % 2 == 0 else 3
            def avfn(kt=kt, pts=pts, avb_bk=avb_bk):
                kb.mm(ps[avb_bk][0:65, :512], vh[:, kt * 65:(kt + 1) * 65], pts,
                      [f"pt_{cc}_{h}_{kt}", f"vh_{cc}_{h}", "vh_ones"],
                      [f"ps_av_{cc}_{h}", f"@ps{avb_bk}"], start=(kt == 0), stop=(kt == nkt - 1))
            pend_av.append(avfn)
        for fn_ in pend_av:
            fn_()
        kb.recip(rcp_s[64:65, :512], ps[avb_bk][64:65, :512],
                 [f"ps_av_{cc}_{h}", f"@ps{avb_bk}"], [f"rcpf_{cc}_{h}"])
        kb.cp_s(hq_t[64:65, 0:512], rcp_s[64:65, :512], [f"rcpf_{cc}_{h}"], [f"rcpb_{cc}_{h}"])
        kb.mm(ps[rr_bk][0:64, :512], c_osq[64:65, 0:64], hq_t[64:65, 0:512],
              [f"rcpb_{cc}_{h}", "onesq"], [f"ps_rr_{cc}_{h}", f"@ps{rr_bk}"])
        kb.cp_s(av_s[0:64, :512], ps[avb_bk][0:64, :512],
                [f"ps_av_{cc}_{h}", f"@ps{avb_bk}"], [f"avf_{cc}_{h}"])
        kb.tt(avb[:, :], av_s[0:64, :512], ps[rr_bk][0:64, :512], Alu.mult,
              [f"avf_{cc}_{h}", f"ps_rr_{cc}_{h}", f"@ps{rr_bk}"], [f"avn_{cc}_{h}"])
        kb.dma(dram["avn_scr"][h], avb[:, :], [f"avn_{cc}_{h}"], [f"avns_{cc}_{h}"])
        S["w2pump"](1)
    S["w2pump"](60)
    for j in range(4):
        quant_wide(bufB[:, j * DM:(j + 1) * DM], [f"xn2_{cc}_{j}"], f"xq_{cc}_{j}", 64,
                   [(xq_t[:, 0:512], f"xq_{cc}_{j}_0"), (xq_t[:, 512:1024], f"xq_{cc}_{j}_1")])
        for dc in range(8):
            pv = psT16 if dc % 2 == 0 else psT16b
            pk = "@ps0" if dc % 2 == 0 else "@ps3"
            kb.transp(pv[:, :], xq_t[:, dc * 128:(dc + 1) * 128], c_identb[:],
                      [f"xq_{cc}_{j}_{dc // 4}", "ident_bf16"], [f"psT{dc % 2}", pk])
            kb.act(rq_s[:, dc * 512 + j * 128: dc * 512 + (j + 1) * 128], pv[:, :],
                   Act.Copy, [f"psT{dc % 2}", pk], [f"xqf_{cc}_{dc}"])
    xqf_all = [f"xqf_{cc}_{dc}" for dc in range(8)]
    psT16c = ps[1][:, 0:64].bitcast(BF16)
    avsl = [rkh[:, 0:512], rkh[:, 512:1024]]
    wosl = [w_oh[:, 0:1024], rkh[:, 1024:2048]]
    for h in range(NH):
        sl = h % 2
        kb.dma(avsl[sl], dram["avn_scr"][h], [f"avns_{cc}_{h}", f"avh{sl}"], [f"avh{sl}"],
               q="gpsimd")
        kb.dma(wosl[sl], dram["wo_scr"][h],
               [f"wo_scr_{h}", f"woh{sl}"], [f"woh{sl}"], q="gpsimd")
        for do in range(8):
            kb.mm(ps[do][:, :512], wosl[sl][:, do * 128:(do + 1) * 128], avsl[sl],
                  [f"avh{sl}", f"woh{sl}"],
                  [f"ps_wo_{cc}_{do}", f"@ps{do}", f"avh{sl}", f"woh{sl}"],
                  start=(h == 0), stop=(h == NH - 1))
    for do in range(8):
        kb.act(hc_s[:], ps[do][:, :512], Act.Copy, [f"ps_wo_{cc}_{do}", f"@ps{do}"],
               [f"attnf_{cc}_{do}"])
        for j in range(4):
            use1 = (do >= 2 and j % 2 == 1)
            pv = psT16c if use1 else psT16
            pk = "@ps1" if use1 else "@ps0"
            kb.transp(pv[:, :], hc_s[:, j * 128:(j + 1) * 128], c_identb[:],
                      [f"attnf_{cc}_{do}", "ident_bf16"], ["psT0" if not use1 else "psT1c", pk])
            kb.tt(bufA[:, j * DM + do * 128: j * DM + (do + 1) * 128],
                  bufA[:, j * DM + do * 128: j * DM + (do + 1) * 128], pv[:, :],
                  Alu.add, ["psT0" if not use1 else "psT1c", pk, f"x2_{cc}_{j}_0", f"x2_{cc}_{j}_1"],
                  [f"xa_{cc}_{do}_{j}"])

    NS = [(i * 512, 512) for i in range(5)] + [(2560, 256)]
    for nsi, (nso, nsw) in enumerate(NS):
        kb.dma(wg_b[:].rearrange("p (kc m) -> p kc m", kc=8)[:, :, 0:nsw],
               dram["wg_scr"].rearrange("kc p m -> p kc m")[:, :, nso:nso + nsw],
               [f"wqs_w_gate_{kc}_{nt}" for nt in range(22) for kc in range(8)] + ["wg_b"],
               ["wg_b"])
        kb.dma(wu_b[:].rearrange("p (kc m) -> p kc m", kc=8)[:, :, 0:nsw],
               dram["wu_scr"].rearrange("kc p m -> p kc m")[:, :, nso:nso + nsw],
               [f"wqs_w_up_{kc}_{nt}" for nt in range(22) for kc in range(8)] + ["wu_b"],
               ["wu_b"], q="gpsimd")
        W_H = g["W_H"]
        hqb = [hq_t, bc_f[:, 128:384].bitcast(BF16)]
        hbufs = [W_H, bufD]
        for jp in range(2):
            hb = hbufs[(2 * nsi + jp) % 2]
            for j in (2 * jp, 2 * jp + 1):
                gb, ub = (1, 2) if j % 2 == 0 else (6, 7)
                for kc in range(8):
                    kb.mm(ps[gb][:, :nsw], rq_s[:, kc * 512 + j * 128: kc * 512 + (j + 1) * 128],
                          wg_b[:, kc * 512: kc * 512 + nsw],
                          xqf_all + ["wg_b"], [f"ps_g_{cc}_{nsi}_{j}", f"@ps{gb}"],
                          start=(kc == 0), stop=(kc == 7))
                for kc in range(8):
                    kb.mm(ps[ub][:, :nsw], rq_s[:, kc * 512 + j * 128: kc * 512 + (j + 1) * 128],
                          wu_b[:, kc * 512: kc * 512 + nsw],
                          xqf_all + ["wu_b"], [f"ps_u_{cc}_{nsi}_{j}", f"@ps{ub}"],
                          start=(kc == 0), stop=(kc == 7))
                gsb = [g_s, bc_f[:, 384:896]][j % 2]
                kb.act(gsb[:, :nsw], ps[gb][:, :nsw], Act.Silu, [f"ps_g_{cc}_{nsi}_{j}", f"@ps{gb}"],
                       [f"sig_{cc}_{nsi}_{j}"])
                kb.tt(hb[:, (j % 2) * nsw:(j % 2) * nsw + nsw], gsb[:, :nsw], ps[ub][:, :nsw],
                      Alu.mult,
                      [f"sig_{cc}_{nsi}_{j}", f"ps_u_{cc}_{nsi}_{j}", f"@ps{ub}"], [f"hh_{cc}_{nsi}_{j}"])
            def qfn(nsi=nsi, nso=nso, nsw=nsw, jp=jp, hb=hb):
                quant_wide(hb[:, 0:2 * nsw],
                           [f"hh_{cc}_{nsi}_{2 * jp}", f"hh_{cc}_{nsi}_{2 * jp + 1}"],
                           f"hq_{cc}_{nsi}_{jp}", 2 * nsw // 16,
                           [(hq_t[:, :nsw], f"hq_{cc}_{nsi}_{2 * jp}"),
                            (bc_f[:, 128:384].bitcast(BF16)[:, :nsw], f"hq_{cc}_{nsi}_{2 * jp + 1}")])
                for j in (2 * jp, 2 * jp + 1):
                    for i in range(nsw // 128):
                        fc = (nso // 128) + i
                        pv = psT16 if i % 2 == 0 else psT16b
                        pk = "@ps0" if i % 2 == 0 else "@ps3"
                        wslot = wtt[:, (i % 4) * 128:(i % 4 + 1) * 128]
                        kb.transp(pv[:, :], hqb[j % 2][:, i * 128:(i + 1) * 128], c_identb[:],
                                  [f"hq_{cc}_{nsi}_{j}", "ident_bf16"], [f"psT{i % 2}", pk])
                        kb.act(wslot, pv[:, :], Act.Copy, [f"psT{i % 2}", pk], [f"wtt{i % 4}"])
                        kb.dma(dram["hqT_scr"][j][fc], wslot, [f"wtt{i % 4}"], [f"hqT_{cc}_{j}_{fc}"])
            for fn_ in S.get("pend_q", []):
                fn_()
            S["pend_q"] = [qfn]
    for fn_ in S.get("pend_q", []):
        fn_()
    S["pend_q"] = []
    if c + 1 < NCH:
        S["xpre"](c + 1)
    for fc in range(22):
        par = fc % 2
        kb.dma(wnk[:, par * 1024:(par + 1) * 1024], dram["wd_scr"][fc],
               [f"wqs_w_down_{fc}_{nt}" for nt in range(8)] + [f"wnk{par}"], [f"wnk{par}"])
        kb.dma(w_f0[:, par * 512:(par + 1) * 512].rearrange("p (j m) -> p j m", j=4),
               dram["hqT_scr"].rearrange("j fc p m -> p j fc m")[:, :, fc, :],
               [f"hqT_{cc}_{j2}_{fc}" for j2 in range(4)] + [f"dwl_{par}"], [f"dwl_{par}"], q="gpsimd")
        for j in range(4):
            for half in range(2):
                bk = 2 * j + half
                kb.mm(ps[bk][:, :512], w_f0[:, par * 512 + j * 128: par * 512 + (j + 1) * 128],
                      wnk[:, par * 1024 + half * 512: par * 1024 + (half + 1) * 512],
                      [f"dwl_{par}", f"wnk{par}"],
                      [f"ps_d_{cc}_{j}_{half}", f"@ps{bk}", f"dwl_{par}", f"wnk{par}"],
                      start=(fc == 0), stop=(fc == 21))
    for j in range(4):
        for half in range(2):
            bk = 2 * j + half
            kb.tt(bufD[:, half * 512:(half + 1) * 512],
                  bufA[:, j * DM + half * 512: j * DM + (half + 1) * 512], ps[bk][:, :512],
                  Alu.add,
                  [f"ps_d_{cc}_{j}_{half}", f"@ps{bk}", f"bufDh_{half}"]
                  + [f"xa_{cc}_{do}_{j}" for do in range(8)],
                  [f"x3_{cc}_{j}_{half}", f"bufDh_{half}"])
        kb.dma(dram["out"][c * TC + j * 128: c * TC + (j + 1) * 128, :],
               bufD[:, :], [f"x3_{cc}_{j}_0", f"x3_{cc}_{j}_1", "bufDh_0", "bufDh_1"],
               [f"outd_{cc}_{j}", "bufDh_0", "bufDh_1"])


def kernel(**inputs):
    consts = host_constants()
    w = prep_weights(inputs)
    x = np.asarray(inputs["L0_attn_x"], np.float32)
    nc, dram, kb, es = build_program(T=2048, stage="full")
    feed = {}
    feed.update(consts)
    feed.update(w)
    in_maps = []
    for ci in range(8):
        m = dict(feed)
        m["x"] = np.ascontiguousarray(x[ci])
        in_maps.append(m)
    res = run_bass_kernel_spmd(nc, in_maps, core_ids=list(range(8)))
    es.close()
    out = np.stack([res.results[ci]["out"] for ci in range(8)], axis=0)
    return out


# version 47
# speedup vs baseline: 1.0042x; 1.0042x over previous
import math
import contextlib
import numpy as np
import ml_dtypes
import concourse.bass as bass
import concourse.mybir as mybir
from concourse.bass_utils import run_bass_kernel_spmd

F32 = mybir.dt.float32
BF16 = mybir.dt.bfloat16
Alu = mybir.AluOpType
Act = mybir.ActivationFunctionType
AX = mybir.AxisListType.X
BF = ml_dtypes.bfloat16

DM, NST, DTR, NH, DH, IF, TC = 1024, 16, 64, 16, 64, 2816, 512

RIEMANN = (14.134725142, 21.022039639, 25.01085758, 30.424876126, 32.935061588,
           37.586178159, 40.918719012, 43.327073281, 48.005150881, 49.773832478,
           52.970321478, 56.446247697, 59.347044003, 60.831778525, 65.112544048,
           67.079810529, 69.546401711, 72.067157674, 75.704690699, 77.144840069,
           79.33737502, 82.910380854, 84.735492981, 87.425274613, 88.809111208,
           92.491899271, 94.651344041, 95.870634228, 98.831194218, 101.317851006,
           103.72553804, 105.446623052)
TWO_PI = 2.0 * math.pi
CONST_VALS = [1e-5, -0.5, 0.5, -1.0, 1.0, 2.0, 0.0, 1e-30]


class KB:
    ENGINES = ("tensor", "vector", "scalar", "gpsimd", "sync")
    QUEUES = ("sync", "gpsimd")
    NDMA = 40

    def __init__(self, nc):
        self.nc = nc
        self.ops = {e: [] for e in self.ENGINES}
        self.kinds = {e: [] for e in self.ENGINES}
        self.ndma_ops = {e: 0 for e in self.ENGINES}
        self.ncomp_ops = {e: 0 for e in self.ENGINES}
        self.last_writer = {}
        self.readers = {}

    def _emit(self, eng, fn, reads, writes, isdma=False):
        seq = len(self.ops[eng])
        deps = {}
        def add(e, s2):
            if not (e == eng and s2 == seq):
                kind, idx = self.kinds[e][s2]
                key = (e, "q", idx % self.NDMA) if kind else (e, "c")
                if deps.get(key, (None, -1))[1] < s2:
                    deps[key] = (e, s2)
        for k in reads:
            w = self.last_writer.get(k)
            if w:
                add(*w)
        for k in writes:
            w = self.last_writer.get(k)
            if w:
                add(*w)
            for r in self.readers.get(k, ()):
                add(*r)
        self.ops[eng].append((fn, sorted(deps.values()), isdma))
        if isdma:
            self.kinds[eng].append((True, self.ndma_ops[eng]))
            self.ndma_ops[eng] += 1
        else:
            self.kinds[eng].append((False, self.ncomp_ops[eng]))
            self.ncomp_ops[eng] += 1
        for k in reads:
            self.readers.setdefault(k, []).append((eng, seq))
        for k in writes:
            self.last_writer[k] = (eng, seq)
            self.readers[k] = []

    GRAN = 4096

    @staticmethod
    def _keys(ap):
        try:
            name = ap.tensor.name
        except Exception:
            return []
        try:
            import concourse.mybir as _mb
            esz = _mb.dt.size(ap.tensor.dtype)
            dims = list(ap.ap)
            off = ap.offset * esz
            ext = 0
            for (step, count) in dims[1:]:
                ext += abs(step) * (count - 1)
            end = off + (ext + 1) * esz
            g = KB.GRAN
            lo, hi = off // g, (end - 1) // g
            if hi - lo > 8:
                return ["#" + name]
            return [f"#{name}:{i}" for i in range(lo, hi + 1)]
        except Exception:
            return ["#" + name]

    def _auto(self, r, w, rins, wouts):
        r = list(r)
        w = list(w)
        for ap in rins:
            if ap is not None and not isinstance(ap, (int, float)):
                r.extend(self._keys(ap))
        for ap in wouts:
            if ap is not None and not isinstance(ap, (int, float)):
                w.extend(self._keys(ap))
        return r, w

    def dma(self, out, in_, r, w, q="sync", **kw):
        r, w = self._auto(r, w, [in_], [out])
        self._emit(q, lambda o: o.dma_start(out=out, in_=in_, **kw), r, w, isdma=True)

    def mm(self, psb, lhsT, rhs, r, w, start=True, stop=True):
        r, w = self._auto(r, w, [lhsT, rhs], [psb])
        self._emit("tensor", lambda o: o.matmul(psb, lhsT, rhs, start=start, stop=stop), r, w)

    def mmg(self, items, r, w):
        rins = []
        wouts = []
        for (psb, lhsT, rhs, st, sp) in items:
            rins += [lhsT, rhs]
            wouts.append(psb)
        r, w = self._auto(r, w, rins, wouts)
        def fn(o):
            last = None
            for (psb, lhsT, rhs, st, sp) in items:
                last = o.matmul(psb, lhsT, rhs, start=st, stop=sp)
            return last
        self._emit("tensor", fn, r, w)

    def transp(self, psb, in_, ident, r, w):
        r, w = self._auto(r, w, [in_, ident], [psb])
        self._emit("tensor", lambda o: o.transpose(psb, in_, ident), r, w)

    def act(self, out, in_, func, r, w, bias=0.0, scale=1.0):
        r, w = self._auto(list(r) + ["cvals"], w, [in_, bias, scale], [out])
        self._emit("scalar", lambda o: o.activation(out, in_, func, bias=bias, scale=scale), r, w)

    def cp_s(self, out, in_, r, w):
        r, w = self._auto(r, w, [in_], [out])
        self._emit("scalar", lambda o: o.copy(out, in_), r, w)

    def cp_v(self, out, in_, r, w):
        r, w = self._auto(r, w, [in_], [out])
        self._emit("vector", lambda o: o.tensor_copy(out, in_), r, w)

    def tt(self, out, a, b, op, r, w, eng="vector"):
        r, w = self._auto(r, w, [a, b], [out])
        self._emit(eng, lambda o: o.tensor_tensor(out, a, b, op), r, w)

    def ts(self, out, in_, s1, s2, op0, op1, r, w, eng="vector"):
        r, w = self._auto(r, w, [in_, s1, s2], [out])
        if s2 is None and op1 is None:
            self._emit(eng, lambda o: o.tensor_scalar(out, in_, s1, None, op0), r, w)
        else:
            self._emit(eng, lambda o: o.tensor_scalar(out, in_, s1, s2, op0, op1), r, w)

    def stt(self, out, in0, scalar, in1, op0, op1, r, w, eng="vector"):
        r, w = self._auto(r, w, [in0, scalar, in1], [out])
        self._emit(eng, lambda o: o.scalar_tensor_tensor(out, in0, scalar, in1, op0, op1), r, w)

    def red(self, out, in_, op, r, w, absval=False):
        r, w = self._auto(r, w, [in_], [out])
        self._emit("vector", lambda o: o.tensor_reduce(out, in_, AX, op, apply_absolute_value=absval), r, w)

    def scan(self, out, d0, d1, init, r, w):
        r, w = self._auto(r, w, [d0, d1, init], [out])
        self._emit("vector", lambda o: o.tensor_tensor_scan(out, d0, d1, init, Alu.mult, Alu.add), r, w)

    def memset(self, t, v, w, eng="vector"):
        _, w = self._auto([], w, [], [t])
        self._emit(eng, lambda o: o.memset(t, v), [], w)

    def recip(self, out, in_, r, w):
        r, w = self._auto(r, w, [in_], [out])
        self._emit("vector", lambda o: o.reciprocal(out, in_), r, w)

    def lay(self, block):
        stack = contextlib.ExitStack()
        sems = {e: stack.enter_context(self.nc.semaphore(f"sem_{e}"))
                for e in self.ENGINES if any(not k for k, _ in self.kinds[e])}
        qsems = {}
        for e in self.ENGINES:
            nd = self.ndma_ops[e]
            if nd:
                qsems[e] = [stack.enter_context(self.nc.semaphore(f"sem_{e}_{i}"))
                            for i in range(min(self.NDMA, nd))]

        def make(eng):
            ops = self.ops[eng]
            kinds = self.kinds[eng]
            def body(o):
                hw = {}
                floor = 0
                for seq, (fn, deps, isdma) in enumerate(ops):
                    for pe, pseq in deps:
                        pk, pidx = self.kinds[pe][pseq]
                        if pe == eng and not pk and not isdma:
                            if pseq >= floor and eng != "tensor":
                                o.drain()
                                floor = seq
                            continue
                        if pk:
                            slot = pidx % self.NDMA
                            need = 16 * (pidx // self.NDMA + 1)
                            key = (pe, "q", slot)
                            if hw.get(key, -1) < need:
                                o.wait_ge(qsems[pe][slot], need)
                                hw[key] = need
                        else:
                            need = pidx + 1
                            key = (pe, "c")
                            if hw.get(key, -1) < need:
                                o.wait_ge(sems[pe], need)
                                hw[key] = need
                    if isdma:
                        _, didx = kinds[seq]
                        slot = didx % self.NDMA
                        rnd = didx // self.NDMA
                        if rnd > 0:
                            key = (eng, "q", slot)
                            need = 16 * rnd
                            if hw.get(key, -1) < need:
                                o.wait_ge(qsems[eng][slot], need)
                                hw[key] = need
                        fn(o).then_inc(qsems[eng][slot], 16)
                    else:
                        fn(o).then_inc(sems[eng], 1)
                nd = self.ndma_ops[eng]
                if nd:
                    for slot in range(min(self.NDMA, nd)):
                        cnt = (nd - slot + self.NDMA - 1) // self.NDMA
                        o.wait_ge(qsems[eng][slot], 16 * cnt)
            return body
        for eng in ("sync", "tensor", "vector", "scalar", "gpsimd"):
            if self.ops[eng]:
                getattr(block, eng)(make(eng))
        return stack


def host_constants():
    c = {}
    c["ident_f32"] = np.eye(128, dtype=np.float32)
    c["ident_bf16"] = np.eye(128, dtype=BF)
    E = np.zeros((16, 128, 128), dtype=BF)
    for g in range(16):
        for p in range(128):
            E[g, 8 * g + p // 16, p] = 1
    c["Esel"] = np.ascontiguousarray(E.transpose(1, 0, 2).reshape(128, 16 * 128))
    c["Ssel"] = np.ascontiguousarray(E.transpose(2, 0, 1).reshape(128, 16 * 128))
    R = np.zeros((32, 2, 128), dtype=BF)
    for p in range(128):
        R[p % 16, 0, p] = 1
        R[16 + p % 16, 1, p] = 1
    c["Rsel"] = np.ascontiguousarray(R.reshape(32, 256))
    o16 = np.zeros((128, 8, 16), dtype=BF)
    for cc in range(8):
        for j in range(2):
            o16[64 * j:64 * j + 64, cc, 2 * cc + j] = 1
    c["ones16"] = np.ascontiguousarray(o16.reshape(128, 128))
    eh = np.zeros((16, 16, 128), dtype=BF)
    for h in range(16):
        eh[h, h, :] = 1
    c["eh"] = np.ascontiguousarray(eh.transpose(1, 0, 2).reshape(16, 16 * 128))
    masks = np.zeros((128, 4, 512), dtype=BF)
    for m in range(4):
        for k in range(128):
            qs = k + m * 128
            if qs < 512:
                masks[k, m, qs:] = 1
    c["cmask"] = np.ascontiguousarray(masks.reshape(128, 4 * 512))
    gamma = np.asarray(RIEMANN, dtype=np.float32)
    inv_freq = (1.0 / (gamma / gamma[0])).astype(np.float32)
    c["wfac"] = np.ascontiguousarray(np.tile(inv_freq, 4)[:, None])
    pm = np.zeros((128, 1), np.float32)
    for p in range(128):
        pm[p, 0] = -1.0 if (p % 64) < 32 else 1.0
    c["pm_col"] = pm
    c["onesq"] = np.ones((128, 128), dtype=BF)
    psw = np.zeros((128, 128), dtype=BF)
    for p in range(128):
        psw[(p ^ 32) if (p % 64) < 64 else p, p] = 0
    psw[:] = 0
    for p in range(128):
        src = (p + 32) if (p % 64) < 32 else (p - 32)
        psw[src, p] = 1
    c["pswap"] = psw
    cv = np.zeros((128, 8), dtype=np.float32)
    for i, v in enumerate(CONST_VALS):
        cv[:, i] = v
    c["cvals"] = cv
    hsel = np.zeros((16, 8, 128), dtype=BF)
    for dc in range(8):
        for p in range(128):
            hsel[2 * dc + p // 64, dc, p] = 1
    c["hsel"] = np.ascontiguousarray(hsel.reshape(16, 8 * 128))
    return c


def qk_perm():
    perm = np.arange(3072)
    for sec in range(2):
        for h in range(NH):
            base = sec * 1024 + h * 64
            perm[base:base + 64] = base + np.concatenate([np.arange(0, 64, 2), np.arange(1, 64, 2)])
    return perm


def prep_weights(ws):
    out = {}
    out["w_xproj"] = np.ascontiguousarray(np.asarray(ws["L0_attn_W_xproj"], np.float32).T)
    out["w_dt_t"] = np.ascontiguousarray(np.asarray(ws["L0_attn_W_dt"], np.float32).T)
    out["b_dt_col"] = np.ascontiguousarray(np.asarray(ws["L0_attn_b_dt"], np.float32).reshape(8, 128).T)
    al = np.asarray(ws["L0_attn_A_log"], np.float32)
    acol = np.zeros((128, 128), dtype=np.float32)
    for dc in range(8):
        for g in range(16):
            for dl in range(8):
                acol[dl * 16:dl * 16 + 16, dc * 16 + g] = al[128 * dc + 8 * g + dl, :]
    out["a_log_col"] = acol
    out["d_col"] = np.ascontiguousarray(np.asarray(ws["L0_attn_D"], np.float32).reshape(8, 128).T)

    def coefgmat(G1, G2):
        ct = np.zeros((4, 1024), np.float32)
        for b in range(32):
            for cc in range(4):
                ct[cc, b * 32:b * 32 + 32] = G1[0, b, cc, :]
        gm = np.zeros((4, 256), np.float32)
        for cc in range(4):
            gm[cc] = np.ascontiguousarray(G2[cc, :, 0, :].T).reshape(256)
        return ct, gm
    out["coefT_R"], out["gmat_R"] = coefgmat(np.asarray(ws["L0_ffn_G1_R"], np.float32),
                                             np.asarray(ws["L0_ffn_G2_R"], np.float32))
    out["coefT_I"], out["gmat_I"] = coefgmat(np.asarray(ws["L0_ffn_G1_I"], np.float32),
                                             np.asarray(ws["L0_ffn_G2_I"], np.float32))
    out["bias_ri"] = np.concatenate([np.asarray(ws["L0_ffn_bias_real"], np.float32),
                                     np.asarray(ws["L0_ffn_bias_imag"], np.float32)])[None, :]
    out["w_qkv"] = np.ascontiguousarray(np.asarray(ws["L1_attn_w_qkv"], np.float32)[:, qk_perm()])
    out["w_o"] = np.ascontiguousarray(np.asarray(ws["L1_attn_w_o"], np.float32))
    out["log_scale"] = np.asarray(ws["L1_attn_log_scale"], np.float32).reshape(16, 1)
    out["w_gate"] = np.ascontiguousarray(np.asarray(ws["L1_ffn_w_gate"], np.float32))
    out["w_up"] = np.ascontiguousarray(np.asarray(ws["L1_ffn_w_up"], np.float32))
    out["w_down"] = np.ascontiguousarray(np.asarray(ws["L1_ffn_w_down"], np.float32))
    return out


INPUT_SPECS = [
    ("x", None, F32), ("ident_f32", (128, 128), F32), ("ident_bf16", (128, 128), BF16),
    ("Esel", (128, 2048), BF16), ("Ssel", (128, 2048), BF16), ("Rsel", (32, 256), BF16),
    ("ones16", (128, 128), BF16), ("eh", (16, 2048), BF16), ("cmask", (128, 2048), BF16),
    ("wfac", (128, 1), F32), ("hsel", (16, 1024), BF16), ("cvals", (128, 8), F32),
    ("pm_col", (128, 1), F32), ("onesq", (128, 128), BF16), ("pswap", (128, 128), BF16),
    ("w_xproj", (1024, 96), F32), ("w_dt_t", (64, 1024), F32), ("b_dt_col", (128, 8), F32),
    ("a_log_col", (128, 128), F32), ("d_col", (128, 8), F32),
    ("coefT_R", (4, 1024), F32), ("gmat_R", (4, 256), F32),
    ("coefT_I", (4, 1024), F32), ("gmat_I", (4, 256), F32), ("bias_ri", (1, 1024), F32),
    ("w_qkv", (1024, 3072), F32), ("w_o", (1024, 1024), F32), ("log_scale", (16, 1), F32),
    ("w_gate", (IF, 1024), F32), ("w_up", (IF, 1024), F32), ("w_down", (1024, IF), F32),
]


def build_program(T=2048, stage="full"):
    nc = bass.Bass("TRN2", target_bir_lowering=False)
    NCH = T // TC
    dram = {}
    for nm, sh, dt in INPUT_SPECS:
        shape = [T, DM] if nm == "x" else list(sh)
        dram[nm] = nc.dram_tensor(nm, shape, dt, kind="ExternalInput").ap()
    dram["out"] = nc.dram_tensor("out", [T, DM], F32, kind="ExternalOutput").ap()
    dram["wb_scr"] = nc.dram_tensor("wb_scr", [2, 1024, 256], F32, kind="Internal").ap()
    dram["wqkv_scr"] = nc.dram_tensor("wqkv_scr", [8, 128, 3072], BF16, kind="Internal").ap()
    dram["wg_scr"] = nc.dram_tensor("wg_scr", [8, 128, IF], BF16, kind="Internal").ap()
    dram["wu_scr"] = nc.dram_tensor("wu_scr", [8, 128, IF], BF16, kind="Internal").ap()
    dram["wd_scr"] = nc.dram_tensor("wd_scr", [22, 128, DM], BF16, kind="Internal").ap()
    dram["wo_scr"] = nc.dram_tensor("wo_scr", [16, 64, 1024], BF16, kind="Internal").ap()
    dram["ls_scr"] = nc.dram_tensor("ls_scr", [16, 1], F32, kind="Internal").ap()
    dram["mt_scr"] = nc.dram_tensor("mt_scr", [8, 128, 1024], BF16, kind="Internal").ap()
    dram["rk_scr"] = nc.dram_tensor("rk_scr", [8, 128, 2048], BF16, kind="Internal").ap()
    dram["v_scr"] = nc.dram_tensor("v_scr", [16, 128, 1024], BF16, kind="Internal").ap()
    dram["hqT_scr"] = nc.dram_tensor("hqT_scr", [4, 22, 128, 128], BF16, kind="Internal").ap()
    dram["avn_scr"] = nc.dram_tensor("avn_scr", [16, 64, 512], BF16, kind="Internal").ap()

    kb = KB(nc)
    es = contextlib.ExitStack()
    def sb(name, shape, dtype=F32):
        return es.enter_context(nc.sbuf_tensor(name, list(shape), dtype))

    S = {}
    def sbd(name, shape, dtype=F32):
        S[name] = sb(name, shape, dtype)
        return S[name]

    c_identf = sbd("c_identf", (128, 128))
    c_identb = sbd("c_identb", (128, 128), BF16)
    c_E = sbd("c_E", (128, 2048), BF16)
    c_S = sbd("c_S", (128, 2048), BF16)
    c_R = sbd("c_R", (32, 256), BF16)
    c_o16 = sbd("c_o16", (128, 128), BF16)
    c_eh = sbd("c_eh", (16, 2048), BF16)
    c_mask = sbd("c_mask", (128, 2048), BF16)
    c_wf = sbd("c_wf", (128, 1))
    c_hsel = sbd("c_hsel", (16, 1024), BF16)
    c_cv = sbd("c_cv", (128, 8))
    c_pm = sbd("c_pm", (128, 1))
    c_osq = sbd("c_osq", (128, 128), BF16)
    c_psw = sbd("c_psw", (128, 128), BF16)
    w_acol = sbd("w_acol", (128, 128))
    w_dcol = sbd("w_dcol", (128, 8))
    w_bdt = sbd("w_bdt", (128, 8))
    w_xpb = sbd("w_xpb", (128, 8 * 96), BF16)
    w_dtt = sbd("w_dtt", (64, 1024), BF16)
    w_bias0 = sbd("w_bias0", (128, 1024))
    w_stage = sbd("w_stage", (128, 1024))
    w_kv = sbd("w_kv", (128, 3072), BF16)
    w_str = sbd("w_str", (128, IF), BF16)
    w_oh = sbd("w_oh", (64, 1024), BF16)
    wg_b = sbd("wg_b", (128, 8 * 512), BF16)
    wu_b = sbd("wu_b", (128, 8 * 512), BF16)
    w_f0 = sbd("w_f0", (128, 2 * 512), BF16)
    wtt = sbd("wtt", (128, 512), BF16)
    avb = sbd("avb", (64, TC), BF16)
    rqh = sbd("rqh", (64, TC), BF16)
    bias_b = sbd("bias_b", (1, 1024), BF16)
    ls_b = sbd("ls_b", (16, 1), BF16)
    bufA = sbd("bufA", (128, 4 * DM))
    bufB = sbd("bufB", (128, 4 * DM))
    bufD = sbd("bufD", (128, DM))
    sq_s = sbd("sq_s", (128, DM))
    stat = sbd("stat", (128, 32))
    xn_fb = sbd("xn_fb", (128, 8 * TC), BF16)
    delta_b = sbd("delta_b", (128, TC), BF16)
    u_b = sbd("u_b", (128, TC), BF16)
    xp_b = sbd("xp_b", (64, TC), BF16)
    xp_bc = sbd("xp_bc", (32, TC), BF16)
    z_s = sbd("z_s", (128, TC))
    bc_f = sbd("bc_f", (128, 2 * TC))
    W_EE = sbd("W_EE", (128, 2 * TC), BF16)
    dA_s = W_EE[:, TC:2 * TC]; q_ee = W_EE[:, 0:TC]
    S["dA_s"] = dA_s; S["q_ee"] = q_ee
    W_CC = sbd("W_CC", (128, 2 * TC), BF16)
    dBx_s = W_CC[:, TC:2 * TC]; q_cc = W_CC[:, 0:TC]
    S["dBx_s"] = dBx_s; S["q_cc"] = q_cc
    W_VA = sbd("W_VA", (128, 2 * TC))
    h_s = W_VA[:, TC:2 * TC]; q_va = W_VA[:, 0:TC]
    S["h_s"] = h_s; S["q_va"] = q_va
    hc_s = sbd("hc_s", (128, TC), BF16)
    carry = sbd("carry", (128, 128))
    rp_powC = sbd("rp_powC", (128, 12 * 8))
    rp_powS = sbd("rp_powS", (128, 12 * 8))
    rp_sc = sbd("rp_sc", (128, 40))
    rp1 = sbd("rp1", (128, 512), BF16)
    rp2 = sbd("rp2", (128, 512), BF16)
    rp_t1 = sbd("rp_t1", (128, 256))
    rp_t2 = sbd("rp_t2", (128, 256))
    finC = sbd("finC", (128, 32 * 8)); finS = sbd("finS", (128, 32 * 8))
    coaC = sbd("coaC", (128, 64 * 8)); coaS = sbd("coaS", (128, 64 * 8))
    tbl = sbd("tbl", (128, 4 * TC), BF16)
    q_c = sbd("q_c", (128, TC), BF16)
    k_c = sbd("k_c", (128, TC), BF16)
    qs_c = sbd("qs_c", (128, TC), BF16)
    ks_c = sbd("ks_c", (128, TC), BF16)
    rq_s = sbd("rq_s", (128, 8 * TC), BF16)
    rkh = sbd("rkh", (64, 2048), BF16)
    vh = sbd("vh", (128, 16 * 65), BF16)
    nrm_k = sbd("nrm_k", (16, 512))
    nrm_q = sbd("nrm_q", (16, 512))
    mh_s = sbd("mh_s", (16, 8))
    shf_s = sbd("shf_s", (128, 16))
    pt_s = sbd("pt_s", (128, 2 * TC), BF16)
    av_s = sbd("av_s", (128, TC))
    W_H = sbd("W_H", (128, 2 * TC))
    hh_s = W_H[:, 0:TC]; rcp_s = W_H[:, TC:2 * TC]
    S["hh_s"] = hh_s; S["rcp_s"] = rcp_s
    g_s = sbd("g_s", (128, TC))
    q_sp = sbd("q_sp", (128, 512))
    q_am = sbd("q_am", (128, 64))
    q_rs = sbd("q_rs", (128, 64))
    hq_t = sbd("hq_t", (128, TC), BF16)
    xq_t = sbd("xq_t", (128, DM), BF16)
    xq_f = sbd("xq_f", (128, 8 * 128), BF16)
    wnk = sbd("wnk", (128, IF), BF16)
    ones_b = sbd("ones_b", (128, 1), BF16)
    ones_r = sbd("ones_r", (1, 128), BF16)
    gat = sbd("gat", (128, 512))
    wbt = gat
    coefT = w_str[0:4, 0:2048]
    gmat = w_kv[0:4, 2048:2560]

    ps = [es.enter_context(nc.psum_tensor(f"ps{i}", [128, 512], F32)) for i in range(8)]
    psT16 = ps[0][:, 0:64].bitcast(BF16)
    psT16b = ps[3][:, 0:64].bitcast(BF16)
    S["ps"] = ps
    S["psT16"] = psT16
    S["psT16b"] = psT16b
    S["dram"] = dram

    for nm, t in [("ident_f32", c_identf), ("ident_bf16", c_identb), ("Esel", c_E),
                  ("Ssel", c_S), ("Rsel", c_R), ("ones16", c_o16), ("eh", c_eh),
                  ("cmask", c_mask), ("wfac", c_wf), ("hsel", c_hsel),
                  ("a_log_col", w_acol), ("d_col", w_dcol), ("b_dt_col", w_bdt),
                  ("cvals", c_cv), ("pm_col", c_pm), ("onesq", c_osq), ("pswap", c_psw)]:
        kb.dma(t[:], dram[nm], [], [nm])
    for i, v in enumerate(CONST_VALS):
        nc.const_aps.register(F32, v, c_cv[:, i:i + 1])
    kb.memset(ones_b[:], 1.0, ["ones_b"])
    kb.memset(ones_r[:], 1.0, ["ones_r"])
    kb.memset(gat[:], 0.0, ["gat"])
    kb.memset(vh[:, 64::65], 1.0, ["vh_ones"])
    kb.act(w_acol[:], w_acol[:], Act.Exp, ["a_log_col"], ["a_exp"])
    kb.ts(w_acol[:], w_acol[:], -1.0, None, Alu.mult, None, ["a_exp"], ["a_col"])
    kb.dma(w_stage[:64, :1024], dram["w_dt_t"], ["w_stage"], ["w_stage"])
    kb.cp_s(w_dtt[:], w_stage[:64, :1024], ["w_stage"], ["w_dtt"])
    tbl32 = tbl.bitcast(F32)
    def quant_slice(src, dst, kin, kout, nb, par=0):
        if par == 0:
            qva, qcc, qee, qsp, qam, qrs = q_va, q_cc, q_ee, q_sp, q_am, q_rs
        else:
            qva, qcc, qee = h_s, dBx_s, dA_s
            qsp = z_s
            qam, qrs = bc_f[:, 0:64], bc_f[:, 64:128]
        kb.red(qam[:, :nb].rearrange("p (b o) -> p b o", o=1),
               src.rearrange("p (b k) -> p b k", k=16), Alu.max, kin, [kout + "_am"], absval=True)
        kb.act(qsp[:, :nb * 16], src, Act.Sign, kin, [kout + "_sg"])
        kb.ts(qam[:, :nb], qam[:, :nb], 1.0 / 6.0, 1e-12, Alu.mult, Alu.max,
              [kout + "_am"], [kout + "_am2"])
        kb.recip(qrs[:, :nb], qam[:, :nb], [kout + "_am2"], [kout + "_rs"])
        kb.stt(qva[:, :nb * 16], src, -1.0, src, Alu.mult, Alu.max, kin, [kout + "_abs"])
        kb.ts(qam[:, :nb], qam[:, :nb], 0.5, None, Alu.mult, None,
              [kout + "_am2", kout + "_rs"], [kout + "_hs"])
        rsb = qrs[:, :nb].rearrange("p (b o) -> p b o", o=1).broadcast_to([128, nb, 16])
        kb.tt(qva[:, :nb * 16].rearrange("p (b k) -> p b k", k=16),
              qva[:, :nb * 16].rearrange("p (b k) -> p b k", k=16), rsb, Alu.mult,
              [kout + "_abs", kout + "_rs"], [kout + "_va"])
        scb = qam[:, :nb].rearrange("p (b o) -> p b o", o=1).broadcast_to([128, nb, 16])
        va = qva[:, :nb * 16]
        kb.ts(qcc[:, :nb * 16], va, 0.25, None, Alu.is_gt, None, [kout + "_va"], [kout + "_c"])
        kb.ts(qee[:, :nb * 16], va, 5.0, None, Alu.is_gt, None, [kout + "_va"], [kout + "_e"])
        kb.tt(qsp[:, :nb * 16].rearrange("p (b k) -> p b k", k=16),
              qsp[:, :nb * 16].rearrange("p (b k) -> p b k", k=16), scb, Alu.mult,
              [kout + "_sg", kout + "_hs"], [kout + "_sp"], eng="gpsimd")
        for mc, me in ((0.75, 5.0), (1.25, 3.5), (1.75, 2.5)):
            kb.stt(qcc[:, :nb * 16], va, mc, qcc[:, :nb * 16], Alu.is_gt, Alu.add,
                   [kout + "_va", kout + "_c"], [kout + "_c"])
            kb.stt(qee[:, :nb * 16], va, me, qee[:, :nb * 16], Alu.is_gt, Alu.add,
                   [kout + "_va", kout + "_e"], [kout + "_e"])
        kb.stt(qcc[:, :nb * 16], qee[:, :nb * 16], 2.0, qcc[:, :nb * 16], Alu.mult, Alu.add,
               [kout + "_e", kout + "_c"], [kout + "_c2"])
        kb.tt(dst, qcc[:, :nb * 16], qsp[:, :nb * 16], Alu.mult,
              [kout + "_c2", kout + "_sp"], [kout])

    QSET0 = dict(va=W_VA, cc=W_CC, ee=W_EE, am=q_am, rs=q_rs, sps=None)
    def quant_multi(jobs):
        js = []
        for (src, kin, kout, nb, dsts, qs) in jobs:
            if qs is None:
                qs = QSET0
            sps = qs["sps"] or [q_sp, z_s]
            w = nb * 16
            nd = len(dsts)
            js.append((src, kin, kout, nb, dsts, qs, sps, w, nd, w // nd, nb // nd))
        for src, kin, kout, nb, dsts, qs, sps, w, nd, hw, nbh in js:
            kb.red(qs["am"][:, :nb].rearrange("p (b o) -> p b o", o=1),
                   src.rearrange("p (b k) -> p b k", k=16), Alu.max, kin, [kout + "_am"],
                   absval=True)
            for i in range(nd):
                kb.act(sps[i][:, :hw], src[:, i * hw:(i + 1) * hw], Act.Sign, kin,
                       [kout + f"_sg{i}"])
        for src, kin, kout, nb, dsts, qs, sps, w, nd, hw, nbh in js:
            kb.ts(qs["am"][:, :nb], qs["am"][:, :nb], 1.0 / 6.0, 1e-12, Alu.mult, Alu.max,
                  [kout + "_am"], [kout + "_am2"])
        for src, kin, kout, nb, dsts, qs, sps, w, nd, hw, nbh in js:
            kb.recip(qs["rs"][:, :nb], qs["am"][:, :nb], [kout + "_am2"], [kout + "_rs"])
            kb.stt(qs["va"][:, :w], src, -1.0, src, Alu.mult, Alu.max, kin, [kout + "_abs"])
        for src, kin, kout, nb, dsts, qs, sps, w, nd, hw, nbh in js:
            rsb = qs["rs"][:, :nb].rearrange("p (b o) -> p b o", o=1).broadcast_to([128, nb, 16])
            kb.tt(qs["va"][:, :w].rearrange("p (b k) -> p b k", k=16),
                  qs["va"][:, :w].rearrange("p (b k) -> p b k", k=16), rsb, Alu.mult,
                  [kout + "_abs", kout + "_rs"], [kout + "_va"])
        for src, kin, kout, nb, dsts, qs, sps, w, nd, hw, nbh in js:
            va = qs["va"][:, :w]
            kb.ts(qs["cc"][:, :w], va, 0.25, None, Alu.is_gt, None, [kout + "_va"], [kout + "_c"])
            kb.ts(qs["ee"][:, :w], va, 5.0, None, Alu.is_gt, None, [kout + "_va"], [kout + "_e"])
            for i in range(nd):
                scb = qs["am"][:, i * nbh:(i + 1) * nbh].rearrange("p (b o) -> p b o", o=1)\
                    .broadcast_to([128, nbh, 16])
                kb.tt(sps[i][:, :hw].rearrange("p (b k) -> p b k", k=16),
                      sps[i][:, :hw].rearrange("p (b k) -> p b k", k=16), scb, Alu.mult,
                      [kout + f"_sg{i}", kout + "_am2"], [kout + f"_sp{i}"], eng="gpsimd")
        for mc, me in ((0.75, 5.0), (1.25, 3.5), (1.75, 2.5)):
            for src, kin, kout, nb, dsts, qs, sps, w, nd, hw, nbh in js:
                va = qs["va"][:, :w]
                kb.stt(qs["cc"][:, :w], va, mc, qs["cc"][:, :w], Alu.is_gt, Alu.add,
                       [kout + "_va", kout + "_c"], [kout + "_c"])
                kb.stt(qs["ee"][:, :w], va, me, qs["ee"][:, :w], Alu.is_gt, Alu.add,
                       [kout + "_va", kout + "_e"], [kout + "_e"])
        for src, kin, kout, nb, dsts, qs, sps, w, nd, hw, nbh in js:
            kb.stt(qs["cc"][:, :w], qs["cc"][:, :w], 0.5, qs["ee"][:, :w], Alu.mult, Alu.add,
                   [kout + "_e", kout + "_c"], [kout + "_c2"])
        for src, kin, kout, nb, dsts, qs, sps, w, nd, hw, nbh in js:
            for i, (dst, dkey) in enumerate(dsts):
                kb.tt(dst, qs["cc"][:, i * hw:(i + 1) * hw], sps[i][:, :hw], Alu.mult,
                      [kout + "_c2", kout + f"_sp{i}"], [dkey])

    def quant_wide(src, kin, kout, nb, dsts):
        quant_multi([(src, kin, kout, nb, dsts, None)])

    S["quant_slice"] = quant_slice
    S["quant_wide"] = quant_wide
    S["quant_multi"] = quant_multi
    xnf32 = xn_fb.bitcast(F32)
    pt32 = pt_s.bitcast(F32)
    QSET1 = dict(va=xnf32[:, 0:1024], cc=rq_s[:, 0:1024], ee=rq_s[:, 1024:2048],
                 am=pt32[:, 0:64], rs=pt32[:, 64:128],
                 sps=[xnf32[:, 1024:1536], xnf32[:, 1536:2048]])
    psT16e = ps[6][:, 0:64].bitcast(BF16)
    psT16f = ps[7][:, 0:64].bitcast(BF16)
    W2Q = []
    W2T = []
    if stage in ("full", "ffnw", "l1"):
        for wname, scr, NT, K in [("w_gate", "wg_scr", 22, 1024), ("w_up", "wu_scr", 22, 1024),
                                  ("w_down", "wd_scr", 8, IF)]:
            for nt in range(NT):
                W2T.append((wname, scr, nt, K))

    stgB = rq_s.bitcast(F32)[:, 1024:2048]
    def w2trans(wname, scr, nt, K, woff, pvs, pks, wslots, wslotks):
        nkc = K // 128
        for kc in range(nkc):
            pcq = (kc * 128) // 1024
            kwq = min(1024, K - pcq * 1024)
            qk = f"wq_{wname}_{nt}_{pcq}_{0 if (kc * 128) % 1024 < kwq // 2 else 1}"
            pv, pk = pvs[kc % 2], pks[kc % 2]
            wslot, wsk = wslots[kc % 4], wslotks[kc % 4]
            kb.transp(pv[:, :], wnk[:, woff + kc * 128:woff + (kc + 1) * 128], c_identb[:],
                      [qk, "ident_bf16"], [pk + "T", pk])
            kb.act(wslot, pv[:, :], Act.Copy, [pk + "T", pk], [wsk])
            kb.dma(dram[scr][kc][:, nt * 128:(nt + 1) * 128], wslot,
                   [wsk], [f"wqs_{wname}_{kc}_{nt}"])
    def w2piece(wname, nt, K, pc, stg, stgk, woff, qs):
        kw = min(1024, K - pc * 1024)
        kb.dma(stg[:, :kw], dram[wname][nt * 128:(nt + 1) * 128, pc * 1024:pc * 1024 + kw],
               [stgk], [stgk, f"wq_ld_{wname}_{nt}_{pc}"], q="gpsimd")
        hw = kw // 2
        return (stg[:, :kw], [f"wq_ld_{wname}_{nt}_{pc}"],
                f"wq_{wname}_{nt}_{pc}", kw // 16,
                [(wnk[:, woff + pc * 1024: woff + pc * 1024 + hw], f"wq_{wname}_{nt}_{pc}_0"),
                 (wnk[:, woff + pc * 1024 + hw: woff + pc * 1024 + kw], f"wq_{wname}_{nt}_{pc}_1")],
                qs)
    wttks = [f"wtt{i}" for i in range(4)]
    xqks = [f"xqw{i}" for i in range(4)]
    xqsl = [xq_t[:, i * 128:(i + 1) * 128] for i in range(4)]
    wtsl = [wtt[:, i * 128:(i + 1) * 128] for i in range(4)]
    NEARLY = min(40, len(W2T) - len(W2T) % 2)
    for p in range(0, NEARLY, 2):
        def w2pair(p=p):
            a = W2T[p]; b = W2T[p + 1]
            ja = w2piece(a[0], a[2], a[3], 0, tbl32, "tbl32", 0, QSET0)
            jb = w2piece(b[0], b[2], b[3], 0, stgB, "stgB", 1024, QSET1)
            quant_multi([ja, jb])
            w2trans(a[0], a[1], a[2], a[3], 0, [psT16, psT16b], ["@ps0", "@ps3"], wtsl, wttks)
            w2trans(b[0], b[1], b[2], b[3], 1024, [psT16e, psT16f], ["@ps6", "@ps7"], xqsl, xqks)
        W2Q.append(w2pair)
    for i in range(NEARLY, len(W2T)):
        def w2one(i=i):
            wname, scr, nt, K = W2T[i]
            for pc in range((K + 1023) // 1024):
                quant_multi([w2piece(wname, nt, K, pc, w_stage, "w_stage", 0, QSET0)])
            w2trans(wname, scr, nt, K, 0, [psT16, psT16b], ["@ps0", "@ps3"], wtsl, wttks)
        W2Q.append(w2one)
    def w2pump(n):
        while n > 0 and W2Q:
            W2Q.pop(0)()
            n -= 1
    S["w2pump"] = w2pump

    for ch in range(8):
        kb.dma(w_stage[:, :96], dram["w_xproj"][ch * 128:(ch + 1) * 128, :], ["w_stage"], ["w_stage"])
        kb.cp_s(w_xpb[:, ch * 96:(ch + 1) * 96], w_stage[:, :96], ["w_stage"], [f"w_xpb_{ch}"])
    kb.dma(w_stage[:1, :1024], dram["bias_ri"], ["w_stage"], ["w_stage"])
    kb.cp_s(bias_b[:], w_stage[:1, :1024], ["w_stage"], ["bias_b"])
    for half in range(2):
        kb.mm(ps[half][:, :512], ones_r[:], bias_b[:, half * 512:(half + 1) * 512],
              ["bias_b", "ones_r"], [f"ps_bias{half}", f"@ps{half}"])
        kb.cp_v(w_bias0[:, half * 512:(half + 1) * 512], ps[half][:, :512],
                [f"ps_bias{half}", f"@ps{half}"], ["w_bias0"])
    for ch in range(8):
        cstg, cstgk = (w_stage, "w_stage") if ch % 2 == 0 else (bufD, "bufDc")
        kb.dma(cstg[:, :1024], dram["w_o"][ch * 128:(ch + 1) * 128, :], [cstgk, "w_kv"], [cstgk])
        kb.cp_s(w_kv[:, :1024], cstg[:, :1024], [cstgk], ["w_kv"])
        kb.dma(dram["wo_scr"][2 * ch], w_kv[:64, :1024], ["w_kv"], [f"wo_scr_{2 * ch}"])
        kb.dma(dram["wo_scr"][2 * ch + 1], w_kv[64:128, :1024], ["w_kv"], [f"wo_scr_{2 * ch + 1}"])
        S["w2pump"](1)
    for ch in range(8):
        for piece in range(3):
            cstg, cstgk = (w_stage, "w_stage") if (ch * 3 + piece) % 2 == 0 else (bufD, "bufDc")
            kb.dma(cstg[:, :1024],
                   dram["w_qkv"][ch * 128:(ch + 1) * 128, piece * 1024:(piece + 1) * 1024],
                   [cstgk, "w_kv"], [cstgk])
            kb.cp_s(w_kv[:, piece * 1024:(piece + 1) * 1024], cstg[:, :1024],
                    [cstgk], ["w_kv"])
        kb.dma(dram["wqkv_scr"][ch], w_kv[:], ["w_kv"], [f"wqkv_scr_{ch}"])
        S["w2pump"](1)

    for i, (cn, gn) in enumerate([("coefT_R", "gmat_R"), ("coefT_I", "gmat_I")]):
        kb.dma(w_stage[:4, :1024], dram[cn], ["w_stage"], ["w_stage"])
        kb.cp_s(coefT[:, i * 1024:(i + 1) * 1024], w_stage[:4, :1024], ["w_stage"], [f"coefT_{i}", "w_str"])
        kb.dma(w_stage[:4, :256], dram[gn], ["w_stage"], ["w_stage"])
        kb.cp_s(gmat[:, i * 256:(i + 1) * 256], w_stage[:4, :256], ["w_stage"], [f"gmat_{i}", "w_kv"])
    for i in range(2):
        for ch in range(8):
            kb.mm(ps[2][:, :256], coefT[:, i * 1024 + ch * 128:i * 1024 + (ch + 1) * 128],
                  gmat[:, i * 256:(i + 1) * 256], [f"coefT_{i}", f"gmat_{i}", "wbt"], ["ps_wb", "@ps2"])
            kb.cp_v(wbt[:, :256], ps[2][:, :256], ["ps_wb", "@ps2"], ["wbt"])
            kb.dma(dram["wb_scr"][i, ch * 128:(ch + 1) * 128, :], wbt[:, :256], ["wbt"], [f"wbscr_{i}_{ch}"])
            S["w2pump"](1)
    wb_deps = [f"wbscr_{i}_{c}" for i in range(2) for c in range(8)]
    for ic in range(8):
        ihalf, icl = ic // 4, ic % 4
        for ohalf in range(2):
            src = 0 if (ihalf + ohalf) % 2 == 0 else 1
            neg = (ihalf == 1 and ohalf == 0)
            for dl in range(8):
                kb.dma(gat[dl * 16:(dl + 1) * 16, :].rearrange("p (b e) -> p b e", b=32),
                       dram["wb_scr"][src].rearrange("(b d) (g e) -> d g b e", b=32, g=16)[8 * icl + dl],
                       wb_deps + ["gat", "wbt"], [f"gat_{dl}"])
            kb.act(w_kv[:, ohalf * 512:(ohalf + 1) * 512], gat[:],
                   Act.Copy, [f"gat_{dl}" for dl in range(8)], ["gat", "wbt", f"mtst_{ic}_{ohalf}"],
                   scale=(-1.0 if neg else 1.0))
        kb.dma(dram["mt_scr"][ic], w_kv[:, 0:1024], [f"mtst_{ic}_0", f"mtst_{ic}_1"], [f"mt_{ic}"])

    kb.dma(w_stage[:16, :1], dram["log_scale"], ["w_stage"], ["w_stage"])
    kb.dma(dram["ls_scr"], w_stage[:16, :1], ["w_stage"], ["ls_scr"])
    for hh in range(2):
        src = dram["ls_scr"].rearrange("(dc two) one -> two (dc one)", two=2)[hh]
        src = src.rearrange("(o dc) -> o dc", o=1).broadcast_to([64, 8])
        kb.dma(rp_sc[64 * hh:64 * (hh + 1), 32:40], src, ["ls_scr"], [f"ls_rep_{hh}"],
               allow_slow_non_contiguous=True)
    kb.act(rp_sc[:, 0:8], rp_sc[:, 32:40], Act.Exp, ["ls_rep_0", "ls_rep_1"], ["s_rep"])
    kb.ts(rp_sc[:, 8:16], rp_sc[:, 0:8], c_wf[:, 0:1], None, Alu.mult, None,
          ["s_rep", "wfac"], ["omega"])
    for lev in range(9):
        m = TWO_PI * (2 ** (8 - lev))
        kb.ts(rp_sc[:, 16:24], rp_sc[:, 8:16], m, None, Alu.is_ge, None, ["omega"], ["rr_t"])
        kb.stt(rp_sc[:, 8:16], rp_sc[:, 16:24], -m, rp_sc[:, 8:16], Alu.mult, Alu.add,
               ["rr_t", "omega"], ["omega"])
    kb.ts(rp_sc[:, 16:24], rp_sc[:, 8:16], math.pi, None, Alu.is_ge, None, ["omega"], ["rr_t"])
    kb.stt(rp_sc[:, 8:16], rp_sc[:, 16:24], -TWO_PI, rp_sc[:, 8:16], Alu.mult, Alu.add,
           ["rr_t", "omega"], ["omega"])
    kb.act(rp_sc[:, 24:32], rp_sc[:, 8:16], Act.Sin, ["omega"], ["seed_sh"], scale=0.5)
    kb.act(rp_powS[:, 0:8], rp_sc[:, 8:16], Act.Sin, ["omega"], ["powS_0"])
    kb.tt(rp_sc[:, 16:24], rp_sc[:, 24:32], rp_sc[:, 24:32], Alu.mult, ["seed_sh"], ["sh2"])
    kb.ts(rp_powC[:, 0:8], rp_sc[:, 16:24], -2.0, 1.0, Alu.mult, Alu.add, ["sh2"], ["powC_0"])
    for k in range(11):
        kb.tt(rp_sc[:, 16:24], rp_powC[:, k * 8:(k + 1) * 8], rp_powC[:, k * 8:(k + 1) * 8],
              Alu.mult, [f"powC_{k}"], ["pk_c2"])
        kb.ts(rp_powC[:, (k + 1) * 8:(k + 2) * 8], rp_sc[:, 16:24], 2.0, -1.0, Alu.mult, Alu.add,
              ["pk_c2"], [f"powC_{k + 1}"])
        kb.tt(rp_sc[:, 24:32], rp_powS[:, k * 8:(k + 1) * 8], rp_powC[:, k * 8:(k + 1) * 8],
              Alu.mult, [f"powS_{k}", f"powC_{k}"], ["pk_sc"])
        kb.ts(rp_powS[:, (k + 1) * 8:(k + 2) * 8], rp_sc[:, 24:32], 2.0, None, Alu.mult, None,
              ["pk_sc"], [f"powS_{k + 1}"])

    def bc8(t, width):
        return t.rearrange("p (o dc) -> p o dc", o=1).broadcast_to([128, width, 8])

    def r3(t, width):
        return t.rearrange("p (w dc) -> p w dc", dc=8)

    def rot_block(dstC, dstS, srcC, srcS, pk, rdeps, wtagC, wtagS, width):
        pc = bc8(rp_powC[:, pk * 8:(pk + 1) * 8], width)
        pss = bc8(rp_powS[:, pk * 8:(pk + 1) * 8], width)
        n = width * 8
        kb.tt(r3(rp_t1[:, 0:n], width), r3(srcC, width), pc, Alu.mult, rdeps, ["rp_t1"])
        kb.tt(r3(rp_t2[:, 0:n], width), r3(srcS, width), pss, Alu.mult, rdeps, ["rp_t2"])
        kb.tt(dstC, rp_t1[:, 0:n], rp_t2[:, 0:n], Alu.subtract, ["rp_t1", "rp_t2"], [wtagC])
        kb.tt(r3(rp_t1[:, 0:n], width), r3(srcS, width), pc, Alu.mult, rdeps + [wtagC], ["rp_t1"])
        kb.tt(r3(rp_t2[:, 0:n], width), r3(srcC, width), pss, Alu.mult, rdeps + [wtagC], ["rp_t2"])
        kb.tt(dstS, rp_t1[:, 0:n], rp_t2[:, 0:n], Alu.add, ["rp_t1", "rp_t2"], [wtagS])

    kb.memset(finC[:, 0:8], 1.0, ["finC_b0"])
    kb.memset(finS[:, 0:8], 0.0, ["finS_b0"])
    for k in range(5):
        w = 2 ** k
        rot_block(finC[:, w * 8:2 * w * 8], finS[:, w * 8:2 * w * 8],
                  finC[:, 0:w * 8], finS[:, 0:w * 8], k,
                  [f"finC_b{j}" for j in range(k + 1)] + [f"finS_b{j}" for j in range(k + 1)]
                  + [f"powC_{k}", f"powS_{k}"], f"finC_b{k + 1}", f"finS_b{k + 1}", w)
    kb.memset(coaC[:, 0:8], 1.0, ["coaC_b0"])
    kb.memset(coaS[:, 0:8], 0.0, ["coaS_b0"])
    for k in range(6):
        w = 2 ** k
        rot_block(coaC[:, w * 8:2 * w * 8], coaS[:, w * 8:2 * w * 8],
                  coaC[:, 0:w * 8], coaS[:, 0:w * 8], 5 + k,
                  [f"coaC_b{j}" for j in range(k + 1)] + [f"coaS_b{j}" for j in range(k + 1)]
                  + [f"powC_{5 + k}", f"powS_{5 + k}"], f"coaC_b{k + 1}", f"coaS_b{k + 1}", w)
    fin_done = [f"finC_b{j}" for j in range(6)] + [f"finS_b{j}" for j in range(6)]
    coa_done = [f"coaC_b{j}" for j in range(7)] + [f"coaS_b{j}" for j in range(7)]
    S["fin_done"] = fin_done
    S["coa_done"] = coa_done
    S["quant_slice"] = None

    if stage == "weights":
        for ic in range(4):
            kb.cp_s(sq_s[:], w_ffn0[:, ic * 1024:(ic + 1) * 1024], [f"w_ffn0_{ic}", "sq_s"], ["sq_s"])
            kb.dma(dram["out"][ic * 128:(ic + 1) * 128, :], sq_s[:], ["sq_s"], ["outd", "sq_s2"])
        kb.dma(dram["out"][512:640, 0:256], finC[:], fin_done, ["outd2"])
        kb.dma(dram["out"][512:640, 256:512], finS[:], fin_done, ["outd3"])
        kb.dma(dram["out"][640:768, 0:512], coaC[:], coa_done, ["outd4"])
        kb.dma(dram["out"][640:768, 512:1024], coaS[:], coa_done, ["outd5"])
        kb.dma(dram["out"][768:896, 0:128], w_acol[:], ["a_col"], ["outd6"])
        kb.dma(dram["out"][768:896, 128:129], rp_sc[:, 8:9], ["omega"], ["outd7"],
               allow_slow_non_contiguous=True)
        block = es.enter_context(nc.Block())
        es.enter_context(kb.lay(block))
        return nc, dram, kb, es


    def xpre(c2):
        cc2 = f"c{c2}"
        for j in range(4):
            kb.dma(bufB[:, j * DM:(j + 1) * DM],
                   dram["x"][c2 * TC + j * 128: c2 * TC + (j + 1) * 128, :],
                   [], [f"xB_{cc2}_{j}"])
        for j in range(4):
            xa = bufB[:, j * DM:(j + 1) * DM]
            kb.act(sq_s[:], xa, Act.Square, [f"xB_{cc2}_{j}", "sq_s"], ["sq_s"])
            kb.red(stat[:, j:j + 1], xa, Alu.add, [f"xB_{cc2}_{j}"], [f"st_sum_{j}"])
            kb.red(stat[:, 4 + j:5 + j], sq_s[:], Alu.add, ["sq_s"], [f"st_sq_{j}"])
            kb.ts(stat[:, 8 + j:9 + j], stat[:, j:j + 1], 1.0 / DM, None, Alu.mult, None,
                  [f"st_sum_{j}"], [f"st_mean_{j}"])
            kb.ts(stat[:, 12 + j:13 + j], stat[:, 8 + j:9 + j], -1.0, None, Alu.mult, None,
                  [f"st_mean_{j}"], [f"st_nm_{j}"])
            kb.ts(stat[:, 16 + j:17 + j], stat[:, 4 + j:5 + j], 1.0 / DM, None, Alu.mult, None,
                  [f"st_sq_{j}"], [f"st_msq_{j}"])
            kb.stt(stat[:, 20 + j:21 + j], stat[:, 12 + j:13 + j], stat[:, 8 + j:9 + j],
                   stat[:, 16 + j:17 + j], Alu.mult, Alu.add,
                   [f"st_nm_{j}", f"st_mean_{j}", f"st_msq_{j}"], [f"st_var_{j}"])
            kb.act(stat[:, 24 + j:25 + j], stat[:, 20 + j:21 + j], Act.Ln,
                   [f"st_var_{j}"], [f"st_ln_{j}"], bias=1e-5)
            kb.act(stat[:, 28 + j:29 + j], stat[:, 24 + j:25 + j], Act.Exp,
                   [f"st_ln_{j}"], [f"st_rstd_{j}"], scale=-0.5)
    S["xpre"] = xpre

    for c in range(NCH):
        cc = f"c{c}"
        if c == 0:
            xpre(0)
        for j in range(4):
            kb.cp_s(bufA[:, j * DM:(j + 1) * DM], bufB[:, j * DM:(j + 1) * DM],
                    [f"xB_{cc}_{j}", f"bufA_{cc}"], [f"xA_{cc}_{j}", f"bufA_{cc}"])
        for j in range(4):
            kb.ts(bufB[:, j * DM:(j + 1) * DM], bufB[:, j * DM:(j + 1) * DM],
                  stat[:, 8 + j:9 + j], stat[:, 28 + j:29 + j], Alu.subtract, Alu.mult,
                  [f"xB_{cc}_{j}", f"xA_{cc}_{j}", f"st_mean_{j}", f"st_rstd_{j}", f"bufB_{cc}"],
                  [f"xn_{cc}_{j}", f"bufB_{cc}"])
        for dc in range(8):
            for j in range(4):
                pr = (dc * 4 + j) % 2
                bk = ps[0] if pr == 0 else ps[3]
                pkx = "@ps0" if pr == 0 else "@ps3"
                kb.transp(bk[:, :128], bufB[:, j * DM + dc * 128: j * DM + (dc + 1) * 128],
                          c_identf[:], [f"xn_{cc}_{j}", "ident_f32"], [f"psT{pr}x", pkx])
                kb.act(xn_fb[:, dc * TC + j * 128: dc * TC + (j + 1) * 128], bk[:, :128],
                       Act.Copy, [f"psT{pr}x", pkx], [f"xnf_{cc}_{dc}"])
        xnf_all = [f"xnf_{cc}_{dc}" for dc in range(8)]
        kb.mmg([(ps[6][:64, :512], w_xpb[:, ch * 96: ch * 96 + 64],
                 xn_fb[:, ch * TC:(ch + 1) * TC], ch == 0, ch == 7) for ch in range(8)],
               xnf_all + [f"w_xpb_{ch}" for ch in range(8)], [f"ps_dt_{cc}", "@ps6"])
        kb.mmg([(ps[7][:32, :512], w_xpb[:, ch * 96 + 64: ch * 96 + 96],
                 xn_fb[:, ch * TC:(ch + 1) * TC], ch == 0, ch == 7) for ch in range(8)],
               xnf_all + [f"w_xpb_{ch}" for ch in range(8)], [f"ps_bc_{cc}", "@ps7"])
        kb.act(xp_b[:, :], ps[6][:64, :512], Act.Copy, [f"ps_dt_{cc}", "@ps6"], [f"xpb_{cc}"])
        kb.act(xp_bc[:, :], ps[7][:32, :512], Act.Copy, [f"ps_bc_{cc}", "@ps7"], [f"xpbc_{cc}"])
        kb.mm(ps[6][:, :512], c_R[:, 0:128], xp_bc[:], [f"xpbc_{cc}", "Rsel"], [f"ps_Bb_{cc}", "@ps6"])
        kb.cp_v(bc_f[:, 0:512], ps[6][:, :512], [f"ps_Bb_{cc}", "@ps6"], [f"bcB_{cc}"])
        kb.mm(ps[6][:, :512], c_R[:, 128:256], xp_bc[:], [f"xpbc_{cc}", "Rsel"], [f"ps_Cb_{cc}", "@ps6"])
        kb.cp_v(bc_f[:, 512:1024], ps[6][:, :512], [f"ps_Cb_{cc}", "@ps6"], [f"bcC_{cc}"])
        dAb = [dA_s[:], tbl[:, 0:512]]
        dBxb = [dBx_s[:], tbl[:, 512:1024]]
        hb = [h_s[:], av_s[:]]
        hcb = [hc_s[:], qs_c[:]]
        deltab = [delta_b[:], k_c[:]]
        ub = [u_b[:], ks_c[:]]
        drb = [3, 1]
        urb = [4, 2]
        dpb = [7, 6]
        for dc in range(8):
            p2 = dc % 2
            dlt, uu = deltab[p2], ub[p2]
            dpk = dpb[p2]
            kb.mmg([(ps[dpk][:, :512], w_dtt[:, dc * 128:(dc + 1) * 128], xp_b[:], True, True)],
                   [f"xpb_{cc}", "w_dtt"], [f"ps_dp_{cc}_{dc}", f"@ps{dpk}"])
            kb.ts(sq_s[:, p2 * 512:(p2 + 1) * 512], ps[dpk][:, :512], w_bdt[:, dc:dc + 1], 88.0,
                  Alu.add, Alu.min,
                  [f"ps_dp_{cc}_{dc}", f"@ps{dpk}", "b_dt_col"], [f"dpre_{cc}_{dc}"])
            kb.act(sq_s[:, p2 * 512:(p2 + 1) * 512], sq_s[:, p2 * 512:(p2 + 1) * 512], Act.Exp,
                   [f"dpre_{cc}_{dc}"], [f"dexp_{cc}_{dc}"])
            kb.act(dlt, sq_s[:, p2 * 512:(p2 + 1) * 512], Act.Ln, [f"dexp_{cc}_{dc}"],
                   [f"delta_{cc}_{dc}"], bias=1.0)
            kb.tt(uu, dlt, xn_fb[:, dc * TC:(dc + 1) * TC], Alu.mult,
                  [f"delta_{cc}_{dc}", f"xnf_{cc}_{dc}"], [f"u_{cc}_{dc}"])
            pend_sel = []
            for g0 in range(0, 16, 2):
                pair = [(g0 + i, (g0 + i) % 2, f"{cc}_{dc}_{g0 + i}") for i in range(2)]
                for g, gp, gk in pair:
                    kb.mm(ps[drb[gp]][:, :512], c_E[:, g * 128:(g + 1) * 128], dlt,
                          [f"delta_{cc}_{dc}", "Esel"], [f"ps_dr_{gk}", f"@ps{drb[gp]}"])
                    kb.mm(ps[urb[gp]][:, :512], c_E[:, g * 128:(g + 1) * 128], uu,
                          [f"u_{cc}_{dc}", "Esel"], [f"ps_ur_{gk}", f"@ps{urb[gp]}"])
                for fn_ in pend_sel:
                    fn_()
                pend_sel = []
                for g, gp, gk in pair:
                    kb.act(dAb[gp], ps[drb[gp]][:, :512], Act.Exp, [f"ps_dr_{gk}", f"@ps{drb[gp]}", "a_col"],
                           [f"dA_{gk}"], scale=w_acol[:, dc * 16 + g: dc * 16 + g + 1])
                for g, gp, gk in pair:
                    kb.tt(dBxb[gp], ps[urb[gp]][:, :512], bc_f[:, 0:512], Alu.mult,
                          [f"ps_ur_{gk}", f"@ps{urb[gp]}", f"bcB_{cc}"], [f"dBx_{gk}"])
                for g, gp, gk in pair:
                    init = 0.0 if c == 0 else carry[:, dc * 16 + g: dc * 16 + g + 1]
                    rd = [f"dA_{gk}", f"dBx_{gk}"] + ([] if c == 0 else [f"carry_{dc}_{g}"])
                    kb.scan(hb[gp], dAb[gp], dBxb[gp], init, rd, [f"h_{gk}"])
                for g, gp, gk in pair:
                    if c < NCH - 1:
                        kb.cp_v(carry[:, dc * 16 + g: dc * 16 + g + 1], hb[gp][:, 511:512],
                                [f"h_{gk}"], [f"carry_{dc}_{g}"])
                    kb.tt(hcb[gp], hb[gp], bc_f[:, 512:1024], Alu.mult,
                          [f"h_{gk}", f"bcC_{cc}"], [f"hc_{gk}"], eng="gpsimd")
                for g, gp, gk in pair:
                    def selfn(g=g, gp=gp, gk=gk):
                        kb.mm(ps[5][:, :512], c_S[:, g * 128:(g + 1) * 128], hcb[gp],
                              [f"hc_{gk}", "Ssel"], [f"ps_y_{cc}_{dc}", "@ps5"],
                              start=(g == 0), stop=(g == 15))
                    pend_sel.append(selfn)
            for fn_ in pend_sel:
                fn_()
            kb.stt(z_s[:], xn_fb[:, dc * TC:(dc + 1) * TC], w_dcol[:, dc:dc + 1],
                   ps[5][:, :512], Alu.mult, Alu.add,
                   [f"ps_y_{cc}_{dc}", "@ps5", f"xnf_{cc}_{dc}", "d_col"], [f"z_{cc}_{dc}"])
            for j in range(4):
                bank, pk = (ps[0], "@ps0") if j % 2 == 0 else (ps[6], "@ps6")
                kb.transp(bank[:, :128], z_s[:, j * 128:(j + 1) * 128], c_identf[:],
                          [f"z_{cc}_{dc}", "ident_f32"], [f"psT{j % 2}", pk])
                kb.tt(bufA[:, j * DM + dc * 128: j * DM + (dc + 1) * 128],
                      bufA[:, j * DM + dc * 128: j * DM + (dc + 1) * 128], bank[:, :128],
                      Alu.add, [f"psT{j % 2}", pk, f"xA_{cc}_{j}", f"bufA_{cc}"],
                      [f"x2p_{cc}_{dc}_{j}", f"bufA_{cc}"])
        f0banks = [ps[1], ps[2], ps[4], ps[5]]
        for half in range(2):
            for ic in range(8):
                par = ic % 2
                kb.dma(w_f0[:, par * 512:(par + 1) * 512],
                       dram["mt_scr"][ic][:, half * 512:(half + 1) * 512],
                       [f"mt_{ic}", f"f0w_{par}"], [f"f0w_{par}"])
                for j in range(4):
                    kb.mm(f0banks[j][:, :512],
                          xn_fb[:, ic * TC + j * 128: ic * TC + (j + 1) * 128],
                          w_f0[:, par * 512:(par + 1) * 512],
                          xnf_all + [f"f0w_{par}"],
                          [f"ps_f0_{cc}_{j}_{half}", f"@psf0{j}", f"f0w_{par}"],
                          start=(ic == 0), stop=(ic == 7))
                if not (half == 1 and ic >= 4):
                    S["w2pump"](1)
            for j in range(4):
                kb.tt(bufD[:, half * 512:(half + 1) * 512], f0banks[j][:, :512],
                      w_bias0[:, half * 512:(half + 1) * 512], Alu.add,
                      [f"ps_f0_{cc}_{j}_{half}", f"@psf0{j}", "w_bias0", f"bufDh_{j % 2}"],
                      [f"f0_{cc}_{j}_{half}", f"bufDh_{j % 2}"])
                kb.tt(bufA[:, j * DM + half * 512:j * DM + (half + 1) * 512],
                      bufA[:, j * DM + half * 512:j * DM + (half + 1) * 512],
                      bufD[:, half * 512:(half + 1) * 512],
                      Alu.add,
                      [f"f0_{cc}_{j}_{half}", f"bufDh_{j % 2}", f"bufA_{cc}"]
                      + [f"x2p_{cc}_{dc}_{j}" for dc in range(8)],
                      [f"x2_{cc}_{j}_{half}", f"bufA_{cc}"])
        if stage == "l0":
            for j in range(4):
                kb.dma(dram["out"][c * TC + j * 128: c * TC + (j + 1) * 128, :],
                       bufA[:, j * DM:(j + 1) * DM], [f"x2_{cc}_{j}"], [f"outd_{cc}_{j}"])
        elif stage in ("full", "l1"):
            build_l1(kb, S, c, NCH, stage)

    block = es.enter_context(nc.Block())
    es.enter_context(kb.lay(block))
    return nc, dram, kb, es


def build_l1(kb, S, c, NCH, stage):
    ps = S["ps"]; psT16 = S["psT16"]; psT16b = S["psT16b"]; dram = S["dram"]
    cc = f"c{c}"
    g = S
    bufA, bufB, bufD = g["bufA"], g["bufB"], g["bufD"]
    sq_s, stat, xn_fb = g["sq_s"], g["stat"], g["xn_fb"]
    c_identf, c_identb = g["c_identf"], g["c_identb"]
    w_kv, w_oh, w_f0, wtt = g["w_kv"], g["w_oh"], g["w_f0"], g["wtt"]
    tbl, q_c, k_c, qs_c, ks_c = g["tbl"], g["q_c"], g["k_c"], g["qs_c"], g["ks_c"]
    rq_s, rkh, vh, rqh, avb = g["rq_s"], g["rkh"], g["vh"], g["rqh"], g["avb"]
    nrm_k, nrm_q, mh_s, shf_s = g["nrm_k"], g["nrm_q"], g["mh_s"], g["shf_s"]
    pt_s, av_s, rcp_s = g["pt_s"], g["av_s"], g["rcp_s"]
    g_s, hh_s, hq_t = g["g_s"], g["hh_s"], g["hq_t"]
    xq_t, xq_f = g["xq_t"], g["xq_f"]
    wg_b, wu_b = g["wg_b"], g["wu_b"]
    coaC, coaS, finC, finS = g["coaC"], g["coaS"], g["finC"], g["finS"]
    z_s, hc_s = g["z_s"], g["hc_s"]
    c_o16, c_eh, c_mask, c_pm, c_osq = g["c_o16"], g["c_eh"], g["c_mask"], g["c_pm"], g["c_osq"]
    c_psw = g["c_psw"]
    ls_b = g["ls_b"]
    rp1, rp2 = g["rp1"], g["rp2"]
    bc_f, dA_s, dBx_s, h_s, z_s = g["bc_f"], g["dA_s"], g["dBx_s"], g["h_s"], g["z_s"]
    q_va, q_cc, q_ee, q_sp, q_am, q_rs = g["q_va"], g["q_cc"], g["q_ee"], g["q_sp"], g["q_am"], g["q_rs"]
    wnk = g["wnk"]
    quant_slice = S["quant_slice"]
    quant_wide = S["quant_wide"]
    fin_done, coa_done = S["fin_done"], S["coa_done"]

    for j in range(4):
        xa = bufA[:, j * DM:(j + 1) * DM]
        kb.act(sq_s[:], xa, Act.Square, [f"x2_{cc}_{j}_0", f"x2_{cc}_{j}_1"], ["sq_s"])
        kb.red(stat[:, j:j + 1], xa, Alu.add, [f"x2_{cc}_{j}_0", f"x2_{cc}_{j}_1"], [f"l2_sum_{j}"])
        kb.red(stat[:, 4 + j:5 + j], sq_s[:], Alu.add, ["sq_s"], [f"l2_sq_{j}"])
        kb.ts(stat[:, 8 + j:9 + j], stat[:, j:j + 1], 1.0 / DM, None, Alu.mult, None,
              [f"l2_sum_{j}"], [f"l2_mean_{j}"])
        kb.ts(stat[:, 12 + j:13 + j], stat[:, 8 + j:9 + j], -1.0, None, Alu.mult, None,
              [f"l2_mean_{j}"], [f"l2_nm_{j}"])
        kb.ts(stat[:, 16 + j:17 + j], stat[:, 4 + j:5 + j], 1.0 / DM, None, Alu.mult, None,
              [f"l2_sq_{j}"], [f"l2_msq_{j}"])
        kb.stt(stat[:, 20 + j:21 + j], stat[:, 12 + j:13 + j], stat[:, 8 + j:9 + j],
               stat[:, 16 + j:17 + j], Alu.mult, Alu.add,
               [f"l2_nm_{j}", f"l2_mean_{j}", f"l2_msq_{j}"], [f"l2_var_{j}"])
        kb.act(stat[:, 24 + j:25 + j], stat[:, 20 + j:21 + j], Act.Ln,
               [f"l2_var_{j}"], [f"l2_ln_{j}"], bias=1e-5)
        kb.act(stat[:, 28 + j:29 + j], stat[:, 24 + j:25 + j], Act.Exp,
               [f"l2_ln_{j}"], [f"l2_rstd_{j}"], scale=-0.5)
        kb.ts(bufB[:, j * DM:(j + 1) * DM], xa, stat[:, 8 + j:9 + j],
              stat[:, 28 + j:29 + j], Alu.subtract, Alu.mult,
              [f"x2_{cc}_{j}_0", f"x2_{cc}_{j}_1", f"l2_mean_{j}", f"l2_rstd_{j}"],
              [f"xn2_{cc}_{j}"])
    for dc in range(8):
        for j in range(4):
            bank, pk = (ps[0], "@ps0") if j % 2 == 0 else (ps[3], "@ps3")
            kb.transp(bank[:, :128], bufB[:, j * DM + dc * 128: j * DM + (dc + 1) * 128],
                      c_identf[:], [f"xn2_{cc}_{j}"], [f"psT{j % 2}", pk])
            kb.act(xn_fb[:, dc * TC + j * 128: dc * TC + (j + 1) * 128], bank[:, :128],
                   Act.Copy, [f"psT{j % 2}", pk], [f"x2f_{cc}_{dc}"])
    x2f_all = [f"x2f_{cc}_{dc}" for dc in range(8)]

    for dc in range(8):
        for wsec in range(3):
            kb.dma(w_kv[:, wsec * 1024:(wsec + 1) * 1024].rearrange("p (ch m) -> p ch m", ch=8),
                   dram["wqkv_scr"].rearrange("ch p m -> p ch m")[:, :, (wsec * 8 + dc) * 128:(wsec * 8 + dc + 1) * 128],
                   [f"wqkv_scr_{ch2}" for ch2 in range(8)] + [f"kvb{wsec}"], [f"kvb{wsec}"],
                   q="gpsimd")
        for wsec, bank in [(0, 1), (1, 2), (2, 3)]:
            for ch in range(8):
                kb.mm(ps[bank][:, :512], w_kv[:, wsec * 1024 + ch * 128: wsec * 1024 + (ch + 1) * 128],
                      xn_fb[:, ch * TC:(ch + 1) * TC],
                      [f"x2f_{cc}_{ch}", f"kvb{wsec}"],
                      [f"ps_qkv_{cc}_{dc}_{wsec}", f"@ps{bank}", f"kvb{wsec}"],
                      start=(ch == 0), stop=(ch == 7))
        t1o = 16 * c
        cslice = lambda tb: tb[:, (t1o * 8 + dc):(t1o * 8 + dc + 121):8]
        fslice = lambda tb: tb[:, dc::8]
        def bc_c(ap):
            return ap.rearrange("p (t o) -> p t o", o=1).broadcast_to([128, 16, 32])
        def bfin(ap):
            return ap.rearrange("p (o t) -> p o t", o=1).broadcast_to([128, 16, 32])
        r3t = lambda t: t.rearrange("p (a b) -> p a b", a=16)
        kb.tt(r3t(rp1[:]), bc_c(cslice(coaC)), bfin(fslice(finC)), Alu.mult,
              coa_done + fin_done, ["rp1"])
        kb.tt(r3t(rp2[:]), bc_c(cslice(coaS)), bfin(fslice(finS)), Alu.mult,
              coa_done + fin_done, ["rp2"])
        kb.tt(tbl[:, 0:512], rp1[:], rp2[:], Alu.subtract, ["rp1", "rp2"], [f"tbc_{cc}_{dc}"])
        kb.tt(r3t(rp1[:]), bc_c(cslice(coaS)), bfin(fslice(finC)), Alu.mult,
              coa_done + fin_done, ["rp1"])
        kb.tt(r3t(rp2[:]), bc_c(cslice(coaC)), bfin(fslice(finS)), Alu.mult,
              coa_done + fin_done, ["rp2"])
        kb.tt(tbl[:, 512:1024], rp1[:], rp2[:], Alu.add, ["rp1", "rp2"], [f"tbs0_{cc}_{dc}"])
        kb.ts(tbl[:, 512:1024], tbl[:, 512:1024], c_pm[:, 0:1], None, Alu.mult, None,
              [f"tbs0_{cc}_{dc}"], [f"tbs_{cc}_{dc}"])
        kb.ts(tbl[:, 1024:1536], tbl[:, 0:512], 0.125, None, Alu.mult, None,
              [f"tbc_{cc}_{dc}"], [f"tbc8_{cc}_{dc}"])
        kb.ts(tbl[:, 1536:2048], tbl[:, 512:1024], 0.125, None, Alu.mult, None,
              [f"tbs_{cc}_{dc}"], [f"tbs8_{cc}_{dc}"])
        kb.act(q_c[:], ps[1][:, :512], Act.Copy, [f"ps_qkv_{cc}_{dc}_0", "@ps1"], [f"qc_{cc}_{dc}"])
        kb.act(k_c[:], ps[2][:, :512], Act.Copy, [f"ps_qkv_{cc}_{dc}_1", "@ps2"], [f"kc_{cc}_{dc}"])
        kb.mm(ps[0][:, :512], c_psw[:], q_c[:], [f"qc_{cc}_{dc}", "pswap"],
              [f"qsp_{cc}_{dc}", "@ps0"])
        kb.tt(pt_s[:, 0:512], q_c[:], tbl[:, 1024:1536], Alu.mult,
              [f"qc_{cc}_{dc}", f"tbc8_{cc}_{dc}"], ["pt1"])
        kb.tt(pt_s[:, 512:1024], ps[0][:, :512], tbl[:, 1536:2048], Alu.mult,
              [f"qsp_{cc}_{dc}", "@ps0", f"tbs8_{cc}_{dc}"], ["pt2"])
        kb.tt(rq_s[:, dc * 512:(dc + 1) * 512], pt_s[:, 0:512], pt_s[:, 512:1024], Alu.add,
              ["pt1", "pt2"], [f"rq_{cc}_{dc}"])
        kb.mm(ps[0][:, :512], c_psw[:], k_c[:], [f"kc_{cc}_{dc}", "pswap"],
              [f"ksp_{cc}_{dc}", "@ps0"])
        kb.tt(pt_s[:, 0:512], k_c[:], tbl[:, 0:512], Alu.mult,
              [f"kc_{cc}_{dc}", f"tbc_{cc}_{dc}"], ["pt1"], eng="gpsimd")
        kb.tt(pt_s[:, 512:1024], ps[0][:, :512], tbl[:, 512:1024], Alu.mult,
              [f"ksp_{cc}_{dc}", "@ps0", f"tbs_{cc}_{dc}"], ["pt2"])
        kb.tt(ks_c[:], pt_s[:, 0:512], pt_s[:, 512:1024], Alu.add,
              ["pt1", "pt2"], [f"rk_{cc}_{dc}"])
        kb.dma(dram["rk_scr"][dc][:, c * 512:(c + 1) * 512], ks_c[:], [f"rk_{cc}_{dc}"], [f"rks_{cc}_{dc}"])
        kb.act(q_c[:], rq_s[:, dc * 512:(dc + 1) * 512], Act.Square, [f"rq_{cc}_{dc}"], [f"q2_{cc}_{dc}"])
        kb.mm(ps[6][:16, :512], c_o16[:, dc * 16:(dc + 1) * 16], q_c[:],
              [f"q2_{cc}_{dc}", "ones16"], [f"ps_qn_{cc}", "@ps6"], start=(dc == 0), stop=(dc == 7))
        kb.act(qs_c[:], ks_c[:], Act.Square, [f"rk_{cc}_{dc}"], [f"k2_{cc}_{dc}"])
        kb.mm(ps[7][:16, :512], c_o16[:, dc * 16:(dc + 1) * 16], qs_c[:],
              [f"k2_{cc}_{dc}", "ones16"], [f"ps_kn_{cc}", "@ps7"], start=(dc == 0), stop=(dc == 7))
        kb.act(qs_c[:], ps[3][:, :512], Act.Copy, [f"ps_qkv_{cc}_{dc}_2", "@ps3"], [f"vb_{cc}_{dc}"])
        for j in range(4):
            pv = psT16 if j % 2 == 0 else psT16b
            pk = "@ps0" if j % 2 == 0 else "@ps3"
            wslot = wtt[:, j * 128:(j + 1) * 128]
            kb.transp(pv[:, :], qs_c[:, j * 128:(j + 1) * 128], c_identb[:],
                      [f"vb_{cc}_{dc}", "ident_bf16"], [f"psT{j % 2}", pk])
            kb.act(wslot, pv[:, :], Act.Copy, [f"psT{j % 2}", pk], [f"wtt{j}"])
            kb.dma(dram["v_scr"][c * 4 + j][:, dc * 128:(dc + 1) * 128], wslot,
                   [f"wtt{j}"], [f"vs_{cc}_{dc}_{j}"])
    kb.cp_v(nrm_q[:, :], ps[6][:16, :512], [f"ps_qn_{cc}", "@ps6"], [f"nq_{cc}"])
    kb.cp_v(nrm_k[:, :], ps[7][:16, :512], [f"ps_kn_{cc}", "@ps7"], [f"nk_{cc}"])
    kb.red(mh_s[:16, 0:1], nrm_k[:, :], Alu.max, [f"nk_{cc}"], [f"mh0_{cc}"])
    if c == 0:
        kb.cp_v(mh_s[:16, 1:2], mh_s[:16, 0:1], [f"mh0_{cc}"], [f"mh1_{cc}"])
    else:
        kb.tt(mh_s[:16, 1:2], mh_s[:16, 1:2], mh_s[:16, 0:1], Alu.max,
              [f"mh0_{cc}", f"mh1_c{c - 1}"], [f"mh1_{cc}"])
    kb.act(nrm_q[:, :], nrm_q[:, :], Act.Ln, [f"nq_{cc}"], [f"nq2_{cc}"], bias=1e-30)
    kb.act(nrm_q[:, :], nrm_q[:, :], Act.Exp, [f"nq2_{cc}"], [f"nq3_{cc}"], scale=0.5)
    kb.red(mh_s[:16, 2:3], nrm_q[:, :], Alu.max, [f"nq3_{cc}"], [f"mh2_{cc}"])
    kb.act(mh_s[:16, 3:4], mh_s[:16, 1:2], Act.Ln, [f"mh1_{cc}"], [f"mh3_{cc}"], bias=1e-30)
    kb.act(mh_s[:16, 3:4], mh_s[:16, 3:4], Act.Exp, [f"mh3_{cc}"], [f"mh4_{cc}"], scale=0.5)
    kb.tt(mh_s[:16, 5:6], mh_s[:16, 2:3], mh_s[:16, 3:4], Alu.mult,
          [f"mh2_{cc}", f"mh4_{cc}"], [f"mh5_{cc}"])
    kb.ts(mh_s[:16, 5:6], mh_s[:16, 5:6], -0.125, None, Alu.mult, None, [f"mh5_{cc}"], [f"mh6_{cc}"])
    kb.cp_s(ls_b[:], mh_s[:16, 5:6], [f"mh6_{cc}"], [f"negB_{cc}"])
    for h in range(NH):
        shb, shk = (ps[0], "@ps0") if h % 2 == 0 else (ps[3], "@ps3")
        kb.mm(shb[:, 0:1], c_eh[:, h * 128:(h + 1) * 128], ls_b[:],
              [f"negB_{cc}", "eh"], [f"ps_sh_{cc}_{h}", shk])
        kb.cp_v(shf_s[:, h:h + 1], shb[:, 0:1], [f"ps_sh_{cc}_{h}", shk], [f"shf_{cc}_{h}"])

    nkt = 4 * (c + 1)
    for h in range(NH):
        dc, rows = h // 2, (h % 2) * 64
        kb.dma(rqh[:, :], rq_s[rows:rows + 64, dc * 512:(dc + 1) * 512],
               [f"rq_{cc}_{dc}"], [f"rqh_{cc}_{h}"], q="gpsimd")
        kb.dma(rkh[:, 0:nkt * 128], dram["rk_scr"][dc][rows:rows + 64, 0:nkt * 128],
               [f"rks_c{ci}_{dc}" for ci in range(c + 1)], [f"rkh_{cc}_{h}"], q="gpsimd")
        kb.dma(vh[:, 0:nkt * 65].rearrange("p (kt e) -> p kt e", e=65)[:, :, 0:64],
               dram["v_scr"].rearrange("kt p e -> p kt e")[:, 0:nkt, h * 64:(h + 1) * 64],
               [f"vs_c{ci}_{h // 2}_{jj}" for ci in range(c + 1) for jj in range(4)],
               [f"vh_{cc}_{h}"], q="gpsimd")
        sbanks = [4, 6, 1, 2]
        ptslots = [pt_s[:, 0:512], pt_s[:, 512:1024], tbl[:, 0:512], tbl[:, 512:1024]]
        pend_av = []
        for kt in range(nkt):
            sbk = sbanks[kt % 4]
            pts = ptslots[kt % 4]
            kb.mm(ps[sbk][:, :512], rkh[:, kt * 128:(kt + 1) * 128], rqh[:, :],
                  [f"rkh_{cc}_{h}", f"rqh_{cc}_{h}"], [f"ps_s_{cc}_{h}_{kt}", f"@ps{sbk}"])
            for fn_ in pend_av:
                fn_()
            pend_av = []
            kb.act(pts, ps[sbk][:, :512], Act.Exp,
                   [f"ps_s_{cc}_{h}_{kt}", f"@ps{sbk}", f"shf_{cc}_{h}"],
                   [f"pt_{cc}_{h}_{kt}"], bias=shf_s[:, h:h + 1])
            if kt >= 4 * c:
                kb.tt(pts, pts, c_mask[:, (kt - 4 * c) * 512:(kt - 4 * c + 1) * 512],
                      Alu.mult, [f"pt_{cc}_{h}_{kt}", "cmask"], [f"pt_{cc}_{h}_{kt}"])
            avb_bk = 5 if h % 2 == 0 else 7
            rr_bk = 0 if h % 2 == 0 else 3
            def avfn(kt=kt, pts=pts, avb_bk=avb_bk):
                kb.mm(ps[avb_bk][0:65, :512], vh[:, kt * 65:(kt + 1) * 65], pts,
                      [f"pt_{cc}_{h}_{kt}", f"vh_{cc}_{h}", "vh_ones"],
                      [f"ps_av_{cc}_{h}", f"@ps{avb_bk}"], start=(kt == 0), stop=(kt == nkt - 1))
            pend_av.append(avfn)
        for fn_ in pend_av:
            fn_()
        kb.recip(rcp_s[64:65, :512], ps[avb_bk][64:65, :512],
                 [f"ps_av_{cc}_{h}", f"@ps{avb_bk}"], [f"rcpf_{cc}_{h}"])
        kb.cp_s(hq_t[64:65, 0:512], rcp_s[64:65, :512], [f"rcpf_{cc}_{h}"], [f"rcpb_{cc}_{h}"])
        kb.mm(ps[rr_bk][0:64, :512], c_osq[64:65, 0:64], hq_t[64:65, 0:512],
              [f"rcpb_{cc}_{h}", "onesq"], [f"ps_rr_{cc}_{h}", f"@ps{rr_bk}"])
        kb.cp_s(av_s[0:64, :512], ps[avb_bk][0:64, :512],
                [f"ps_av_{cc}_{h}", f"@ps{avb_bk}"], [f"avf_{cc}_{h}"])
        kb.tt(avb[:, :], av_s[0:64, :512], ps[rr_bk][0:64, :512], Alu.mult,
              [f"avf_{cc}_{h}", f"ps_rr_{cc}_{h}", f"@ps{rr_bk}"], [f"avn_{cc}_{h}"])
        kb.dma(dram["avn_scr"][h], avb[:, :], [f"avn_{cc}_{h}"], [f"avns_{cc}_{h}"])
        S["w2pump"](1)
    S["w2pump"](60)
    for j in range(4):
        quant_wide(bufB[:, j * DM:(j + 1) * DM], [f"xn2_{cc}_{j}"], f"xq_{cc}_{j}", 64,
                   [(xq_t[:, 0:512], f"xq_{cc}_{j}_0"), (xq_t[:, 512:1024], f"xq_{cc}_{j}_1")])
        for dc in range(8):
            pv = psT16 if dc % 2 == 0 else psT16b
            pk = "@ps0" if dc % 2 == 0 else "@ps3"
            kb.transp(pv[:, :], xq_t[:, dc * 128:(dc + 1) * 128], c_identb[:],
                      [f"xq_{cc}_{j}_{dc // 4}", "ident_bf16"], [f"psT{dc % 2}", pk])
            kb.act(rq_s[:, dc * 512 + j * 128: dc * 512 + (j + 1) * 128], pv[:, :],
                   Act.Copy, [f"psT{dc % 2}", pk], [f"xqf_{cc}_{dc}"])
    xqf_all = [f"xqf_{cc}_{dc}" for dc in range(8)]
    psT16c = ps[1][:, 0:64].bitcast(BF16)
    avsl = [rkh[:, 0:512], rkh[:, 512:1024]]
    wosl = [w_oh[:, 0:1024], rkh[:, 1024:2048]]
    for h in range(NH):
        sl = h % 2
        kb.dma(avsl[sl], dram["avn_scr"][h], [f"avns_{cc}_{h}", f"avh{sl}"], [f"avh{sl}"],
               q="gpsimd")
        kb.dma(wosl[sl], dram["wo_scr"][h],
               [f"wo_scr_{h}", f"woh{sl}"], [f"woh{sl}"], q="gpsimd")
        for do in range(8):
            kb.mm(ps[do][:, :512], wosl[sl][:, do * 128:(do + 1) * 128], avsl[sl],
                  [f"avh{sl}", f"woh{sl}"],
                  [f"ps_wo_{cc}_{do}", f"@ps{do}", f"avh{sl}", f"woh{sl}"],
                  start=(h == 0), stop=(h == NH - 1))
    for do in range(8):
        kb.act(hc_s[:], ps[do][:, :512], Act.Copy, [f"ps_wo_{cc}_{do}", f"@ps{do}"],
               [f"attnf_{cc}_{do}"])
        for j in range(4):
            use1 = (do >= 2 and j % 2 == 1)
            pv = psT16c if use1 else psT16
            pk = "@ps1" if use1 else "@ps0"
            kb.transp(pv[:, :], hc_s[:, j * 128:(j + 1) * 128], c_identb[:],
                      [f"attnf_{cc}_{do}", "ident_bf16"], ["psT0" if not use1 else "psT1c", pk])
            kb.tt(bufA[:, j * DM + do * 128: j * DM + (do + 1) * 128],
                  bufA[:, j * DM + do * 128: j * DM + (do + 1) * 128], pv[:, :],
                  Alu.add, ["psT0" if not use1 else "psT1c", pk, f"x2_{cc}_{j}_0", f"x2_{cc}_{j}_1"],
                  [f"xa_{cc}_{do}_{j}"])

    NS = [(i * 512, 512) for i in range(5)] + [(2560, 256)]
    for nsi, (nso, nsw) in enumerate(NS):
        kb.dma(wg_b[:].rearrange("p (kc m) -> p kc m", kc=8)[:, :, 0:nsw],
               dram["wg_scr"].rearrange("kc p m -> p kc m")[:, :, nso:nso + nsw],
               [f"wqs_w_gate_{kc}_{nt}" for nt in range(22) for kc in range(8)] + ["wg_b"],
               ["wg_b"])
        kb.dma(wu_b[:].rearrange("p (kc m) -> p kc m", kc=8)[:, :, 0:nsw],
               dram["wu_scr"].rearrange("kc p m -> p kc m")[:, :, nso:nso + nsw],
               [f"wqs_w_up_{kc}_{nt}" for nt in range(22) for kc in range(8)] + ["wu_b"],
               ["wu_b"], q="gpsimd")
        W_H = g["W_H"]
        hqb = [hq_t, bc_f[:, 128:384].bitcast(BF16)]
        hbufs = [W_H, bufD, g["tbl"].bitcast(F32)[:, 0:1024]]
        for jp in range(2):
            hb = hbufs[(2 * nsi + jp) % 3]
            for j in (2 * jp, 2 * jp + 1):
                gb, ub = (1, 2) if j % 2 == 0 else (6, 7)
                for kc in range(8):
                    kb.mm(ps[gb][:, :nsw], rq_s[:, kc * 512 + j * 128: kc * 512 + (j + 1) * 128],
                          wg_b[:, kc * 512: kc * 512 + nsw],
                          xqf_all + ["wg_b"], [f"ps_g_{cc}_{nsi}_{j}", f"@ps{gb}"],
                          start=(kc == 0), stop=(kc == 7))
                for kc in range(8):
                    kb.mm(ps[ub][:, :nsw], rq_s[:, kc * 512 + j * 128: kc * 512 + (j + 1) * 128],
                          wu_b[:, kc * 512: kc * 512 + nsw],
                          xqf_all + ["wu_b"], [f"ps_u_{cc}_{nsi}_{j}", f"@ps{ub}"],
                          start=(kc == 0), stop=(kc == 7))
                gsb = [g_s, bc_f[:, 384:896]][j % 2]
                kb.act(gsb[:, :nsw], ps[gb][:, :nsw], Act.Silu, [f"ps_g_{cc}_{nsi}_{j}", f"@ps{gb}"],
                       [f"sig_{cc}_{nsi}_{j}"])
                kb.tt(hb[:, (j % 2) * nsw:(j % 2) * nsw + nsw], gsb[:, :nsw], ps[ub][:, :nsw],
                      Alu.mult,
                      [f"sig_{cc}_{nsi}_{j}", f"ps_u_{cc}_{nsi}_{j}", f"@ps{ub}"], [f"hh_{cc}_{nsi}_{j}"])
            def qfn(nsi=nsi, nso=nso, nsw=nsw, jp=jp, hb=hb):
                quant_wide(hb[:, 0:2 * nsw],
                           [f"hh_{cc}_{nsi}_{2 * jp}", f"hh_{cc}_{nsi}_{2 * jp + 1}"],
                           f"hq_{cc}_{nsi}_{jp}", 2 * nsw // 16,
                           [(hq_t[:, :nsw], f"hq_{cc}_{nsi}_{2 * jp}"),
                            (bc_f[:, 128:384].bitcast(BF16)[:, :nsw], f"hq_{cc}_{nsi}_{2 * jp + 1}")])
                for j in (2 * jp, 2 * jp + 1):
                    for i in range(nsw // 128):
                        fc = (nso // 128) + i
                        pv = psT16 if i % 2 == 0 else psT16b
                        pk = "@ps0" if i % 2 == 0 else "@ps3"
                        wslot = wtt[:, (i % 4) * 128:(i % 4 + 1) * 128]
                        kb.transp(pv[:, :], hqb[j % 2][:, i * 128:(i + 1) * 128], c_identb[:],
                                  [f"hq_{cc}_{nsi}_{j}", "ident_bf16"], [f"psT{i % 2}", pk])
                        kb.act(wslot, pv[:, :], Act.Copy, [f"psT{i % 2}", pk], [f"wtt{i % 4}"])
                        kb.dma(dram["hqT_scr"][j][fc], wslot, [f"wtt{i % 4}"], [f"hqT_{cc}_{j}_{fc}"])
            pend = S.get("pend_q", [])
            pend.append(qfn)
            while len(pend) > 2:
                pend.pop(0)()
            S["pend_q"] = pend
    for fn_ in S.get("pend_q", []):
        fn_()
    S["pend_q"] = []
    if c + 1 < NCH:
        S["xpre"](c + 1)
    for fc in range(22):
        par = fc % 2
        kb.dma(wnk[:, par * 1024:(par + 1) * 1024], dram["wd_scr"][fc],
               [f"wqs_w_down_{fc}_{nt}" for nt in range(8)] + [f"wnk{par}"], [f"wnk{par}"])
        kb.dma(w_f0[:, par * 512:(par + 1) * 512].rearrange("p (j m) -> p j m", j=4),
               dram["hqT_scr"].rearrange("j fc p m -> p j fc m")[:, :, fc, :],
               [f"hqT_{cc}_{j2}_{fc}" for j2 in range(4)] + [f"dwl_{par}"], [f"dwl_{par}"], q="gpsimd")
        for j in range(4):
            for half in range(2):
                bk = 2 * j + half
                kb.mm(ps[bk][:, :512], w_f0[:, par * 512 + j * 128: par * 512 + (j + 1) * 128],
                      wnk[:, par * 1024 + half * 512: par * 1024 + (half + 1) * 512],
                      [f"dwl_{par}", f"wnk{par}"],
                      [f"ps_d_{cc}_{j}_{half}", f"@ps{bk}", f"dwl_{par}", f"wnk{par}"],
                      start=(fc == 0), stop=(fc == 21))
    for j in range(4):
        for half in range(2):
            bk = 2 * j + half
            kb.tt(bufD[:, half * 512:(half + 1) * 512],
                  bufA[:, j * DM + half * 512: j * DM + (half + 1) * 512], ps[bk][:, :512],
                  Alu.add,
                  [f"ps_d_{cc}_{j}_{half}", f"@ps{bk}", f"bufDh_{half}"]
                  + [f"xa_{cc}_{do}_{j}" for do in range(8)],
                  [f"x3_{cc}_{j}_{half}", f"bufDh_{half}"])
        kb.dma(dram["out"][c * TC + j * 128: c * TC + (j + 1) * 128, :],
               bufD[:, :], [f"x3_{cc}_{j}_0", f"x3_{cc}_{j}_1", "bufDh_0", "bufDh_1"],
               [f"outd_{cc}_{j}", "bufDh_0", "bufDh_1"])


def kernel(**inputs):
    consts = host_constants()
    w = prep_weights(inputs)
    x = np.asarray(inputs["L0_attn_x"], np.float32)
    nc, dram, kb, es = build_program(T=2048, stage="full")
    feed = {}
    feed.update(consts)
    feed.update(w)
    in_maps = []
    for ci in range(8):
        m = dict(feed)
        m["x"] = np.ascontiguousarray(x[ci])
        in_maps.append(m)
    res = run_bass_kernel_spmd(nc, in_maps, core_ids=list(range(8)))
    es.close()
    out = np.stack([res.results[ci]["out"] for ci in range(8)], axis=0)
    return out


# version 48 (final)
# speedup vs baseline: 1.0112x; 1.0070x over previous
import math
import contextlib
import numpy as np
import ml_dtypes
import concourse.bass as bass
import concourse.mybir as mybir
from concourse.bass_utils import run_bass_kernel_spmd

F32 = mybir.dt.float32
BF16 = mybir.dt.bfloat16
Alu = mybir.AluOpType
Act = mybir.ActivationFunctionType
AX = mybir.AxisListType.X
BF = ml_dtypes.bfloat16

DM, NST, DTR, NH, DH, IF, TC = 1024, 16, 64, 16, 64, 2816, 512

RIEMANN = (14.134725142, 21.022039639, 25.01085758, 30.424876126, 32.935061588,
           37.586178159, 40.918719012, 43.327073281, 48.005150881, 49.773832478,
           52.970321478, 56.446247697, 59.347044003, 60.831778525, 65.112544048,
           67.079810529, 69.546401711, 72.067157674, 75.704690699, 77.144840069,
           79.33737502, 82.910380854, 84.735492981, 87.425274613, 88.809111208,
           92.491899271, 94.651344041, 95.870634228, 98.831194218, 101.317851006,
           103.72553804, 105.446623052)
TWO_PI = 2.0 * math.pi
CONST_VALS = [1e-5, -0.5, 0.5, -1.0, 1.0, 2.0, 0.0, 1e-30]


class KB:
    ENGINES = ("tensor", "vector", "scalar", "gpsimd", "sync")
    QUEUES = ("sync", "gpsimd")
    NDMA = 40

    def __init__(self, nc):
        self.nc = nc
        self.ops = {e: [] for e in self.ENGINES}
        self.kinds = {e: [] for e in self.ENGINES}
        self.ndma_ops = {e: 0 for e in self.ENGINES}
        self.ncomp_ops = {e: 0 for e in self.ENGINES}
        self.last_writer = {}
        self.readers = {}

    def _emit(self, eng, fn, reads, writes, isdma=False):
        seq = len(self.ops[eng])
        deps = {}
        def add(e, s2):
            if not (e == eng and s2 == seq):
                kind, idx = self.kinds[e][s2]
                key = (e, "q", idx % self.NDMA) if kind else (e, "c")
                if deps.get(key, (None, -1))[1] < s2:
                    deps[key] = (e, s2)
        for k in reads:
            w = self.last_writer.get(k)
            if w:
                add(*w)
        for k in writes:
            w = self.last_writer.get(k)
            if w:
                add(*w)
            for r in self.readers.get(k, ()):
                add(*r)
        self.ops[eng].append((fn, sorted(deps.values()), isdma))
        if isdma:
            self.kinds[eng].append((True, self.ndma_ops[eng]))
            self.ndma_ops[eng] += 1
        else:
            self.kinds[eng].append((False, self.ncomp_ops[eng]))
            self.ncomp_ops[eng] += 1
        for k in reads:
            self.readers.setdefault(k, []).append((eng, seq))
        for k in writes:
            self.last_writer[k] = (eng, seq)
            self.readers[k] = []

    GRAN = 4096

    @staticmethod
    def _keys(ap):
        try:
            name = ap.tensor.name
        except Exception:
            return []
        try:
            import concourse.mybir as _mb
            esz = _mb.dt.size(ap.tensor.dtype)
            dims = list(ap.ap)
            off = ap.offset * esz
            ext = 0
            for (step, count) in dims[1:]:
                ext += abs(step) * (count - 1)
            end = off + (ext + 1) * esz
            g = KB.GRAN
            lo, hi = off // g, (end - 1) // g
            if hi - lo > 8:
                return ["#" + name]
            return [f"#{name}:{i}" for i in range(lo, hi + 1)]
        except Exception:
            return ["#" + name]

    def _auto(self, r, w, rins, wouts):
        r = list(r)
        w = list(w)
        for ap in rins:
            if ap is not None and not isinstance(ap, (int, float)):
                r.extend(self._keys(ap))
        for ap in wouts:
            if ap is not None and not isinstance(ap, (int, float)):
                w.extend(self._keys(ap))
        return r, w

    def dma(self, out, in_, r, w, q="sync", **kw):
        r, w = self._auto(r, w, [in_], [out])
        self._emit(q, lambda o: o.dma_start(out=out, in_=in_, **kw), r, w, isdma=True)

    def mm(self, psb, lhsT, rhs, r, w, start=True, stop=True):
        r, w = self._auto(r, w, [lhsT, rhs], [psb])
        self._emit("tensor", lambda o: o.matmul(psb, lhsT, rhs, start=start, stop=stop), r, w)

    def mmg(self, items, r, w):
        rins = []
        wouts = []
        for (psb, lhsT, rhs, st, sp) in items:
            rins += [lhsT, rhs]
            wouts.append(psb)
        r, w = self._auto(r, w, rins, wouts)
        def fn(o):
            last = None
            for (psb, lhsT, rhs, st, sp) in items:
                last = o.matmul(psb, lhsT, rhs, start=st, stop=sp)
            return last
        self._emit("tensor", fn, r, w)

    def transp(self, psb, in_, ident, r, w):
        r, w = self._auto(r, w, [in_, ident], [psb])
        self._emit("tensor", lambda o: o.transpose(psb, in_, ident), r, w)

    def act(self, out, in_, func, r, w, bias=0.0, scale=1.0):
        r, w = self._auto(list(r) + ["cvals"], w, [in_, bias, scale], [out])
        self._emit("scalar", lambda o: o.activation(out, in_, func, bias=bias, scale=scale), r, w)

    def cp_s(self, out, in_, r, w):
        r, w = self._auto(r, w, [in_], [out])
        self._emit("scalar", lambda o: o.copy(out, in_), r, w)

    def cp_v(self, out, in_, r, w):
        r, w = self._auto(r, w, [in_], [out])
        self._emit("vector", lambda o: o.tensor_copy(out, in_), r, w)

    def tt(self, out, a, b, op, r, w, eng="vector"):
        r, w = self._auto(r, w, [a, b], [out])
        self._emit(eng, lambda o: o.tensor_tensor(out, a, b, op), r, w)

    def ts(self, out, in_, s1, s2, op0, op1, r, w, eng="vector"):
        r, w = self._auto(r, w, [in_, s1, s2], [out])
        if s2 is None and op1 is None:
            self._emit(eng, lambda o: o.tensor_scalar(out, in_, s1, None, op0), r, w)
        else:
            self._emit(eng, lambda o: o.tensor_scalar(out, in_, s1, s2, op0, op1), r, w)

    def stt(self, out, in0, scalar, in1, op0, op1, r, w, eng="vector"):
        r, w = self._auto(r, w, [in0, scalar, in1], [out])
        self._emit(eng, lambda o: o.scalar_tensor_tensor(out, in0, scalar, in1, op0, op1), r, w)

    def red(self, out, in_, op, r, w, absval=False):
        r, w = self._auto(r, w, [in_], [out])
        self._emit("vector", lambda o: o.tensor_reduce(out, in_, AX, op, apply_absolute_value=absval), r, w)

    def scan(self, out, d0, d1, init, r, w):
        r, w = self._auto(r, w, [d0, d1, init], [out])
        self._emit("vector", lambda o: o.tensor_tensor_scan(out, d0, d1, init, Alu.mult, Alu.add), r, w)

    def memset(self, t, v, w, eng="vector"):
        _, w = self._auto([], w, [], [t])
        self._emit(eng, lambda o: o.memset(t, v), [], w)

    def recip(self, out, in_, r, w):
        r, w = self._auto(r, w, [in_], [out])
        self._emit("vector", lambda o: o.reciprocal(out, in_), r, w)

    def lay(self, block):
        stack = contextlib.ExitStack()
        sems = {e: stack.enter_context(self.nc.semaphore(f"sem_{e}"))
                for e in self.ENGINES if any(not k for k, _ in self.kinds[e])}
        qsems = {}
        for e in self.ENGINES:
            nd = self.ndma_ops[e]
            if nd:
                qsems[e] = [stack.enter_context(self.nc.semaphore(f"sem_{e}_{i}"))
                            for i in range(min(self.NDMA, nd))]

        def make(eng):
            ops = self.ops[eng]
            kinds = self.kinds[eng]
            def body(o):
                hw = {}
                floor = 0
                for seq, (fn, deps, isdma) in enumerate(ops):
                    for pe, pseq in deps:
                        pk, pidx = self.kinds[pe][pseq]
                        if pe == eng and not pk and not isdma:
                            if pseq >= floor and eng != "tensor":
                                o.drain()
                                floor = seq
                            continue
                        if pk:
                            slot = pidx % self.NDMA
                            need = 16 * (pidx // self.NDMA + 1)
                            key = (pe, "q", slot)
                            if hw.get(key, -1) < need:
                                o.wait_ge(qsems[pe][slot], need)
                                hw[key] = need
                        else:
                            need = pidx + 1
                            key = (pe, "c")
                            if hw.get(key, -1) < need:
                                o.wait_ge(sems[pe], need)
                                hw[key] = need
                    if isdma:
                        _, didx = kinds[seq]
                        slot = didx % self.NDMA
                        rnd = didx // self.NDMA
                        if rnd > 0:
                            key = (eng, "q", slot)
                            need = 16 * rnd
                            if hw.get(key, -1) < need:
                                o.wait_ge(qsems[eng][slot], need)
                                hw[key] = need
                        fn(o).then_inc(qsems[eng][slot], 16)
                    else:
                        fn(o).then_inc(sems[eng], 1)
                nd = self.ndma_ops[eng]
                if nd:
                    for slot in range(min(self.NDMA, nd)):
                        cnt = (nd - slot + self.NDMA - 1) // self.NDMA
                        o.wait_ge(qsems[eng][slot], 16 * cnt)
            return body
        for eng in ("sync", "tensor", "vector", "scalar", "gpsimd"):
            if self.ops[eng]:
                getattr(block, eng)(make(eng))
        return stack


def host_constants():
    c = {}
    c["ident_f32"] = np.eye(128, dtype=np.float32)
    c["ident_bf16"] = np.eye(128, dtype=BF)
    E = np.zeros((16, 128, 128), dtype=BF)
    for g in range(16):
        for p in range(128):
            E[g, 8 * g + p // 16, p] = 1
    c["Esel"] = np.ascontiguousarray(E.transpose(1, 0, 2).reshape(128, 16 * 128))
    c["Ssel"] = np.ascontiguousarray(E.transpose(2, 0, 1).reshape(128, 16 * 128))
    R = np.zeros((32, 2, 128), dtype=BF)
    for p in range(128):
        R[p % 16, 0, p] = 1
        R[16 + p % 16, 1, p] = 1
    c["Rsel"] = np.ascontiguousarray(R.reshape(32, 256))
    o16 = np.zeros((128, 8, 16), dtype=BF)
    for cc in range(8):
        for j in range(2):
            o16[64 * j:64 * j + 64, cc, 2 * cc + j] = 1
    c["ones16"] = np.ascontiguousarray(o16.reshape(128, 128))
    eh = np.zeros((16, 16, 128), dtype=BF)
    for h in range(16):
        eh[h, h, :] = 1
    c["eh"] = np.ascontiguousarray(eh.transpose(1, 0, 2).reshape(16, 16 * 128))
    masks = np.zeros((128, 4, 512), dtype=BF)
    for m in range(4):
        for k in range(128):
            qs = k + m * 128
            if qs < 512:
                masks[k, m, qs:] = 1
    c["cmask"] = np.ascontiguousarray(masks.reshape(128, 4 * 512))
    gamma = np.asarray(RIEMANN, dtype=np.float32)
    inv_freq = (1.0 / (gamma / gamma[0])).astype(np.float32)
    c["wfac"] = np.ascontiguousarray(np.tile(inv_freq, 4)[:, None])
    pm = np.zeros((128, 1), np.float32)
    for p in range(128):
        pm[p, 0] = -1.0 if (p % 64) < 32 else 1.0
    c["pm_col"] = pm
    c["onesq"] = np.ones((128, 128), dtype=BF)
    psw = np.zeros((128, 128), dtype=BF)
    for p in range(128):
        psw[(p ^ 32) if (p % 64) < 64 else p, p] = 0
    psw[:] = 0
    for p in range(128):
        src = (p + 32) if (p % 64) < 32 else (p - 32)
        psw[src, p] = 1
    c["pswap"] = psw
    cv = np.zeros((128, 8), dtype=np.float32)
    for i, v in enumerate(CONST_VALS):
        cv[:, i] = v
    c["cvals"] = cv
    hsel = np.zeros((16, 8, 128), dtype=BF)
    for dc in range(8):
        for p in range(128):
            hsel[2 * dc + p // 64, dc, p] = 1
    c["hsel"] = np.ascontiguousarray(hsel.reshape(16, 8 * 128))
    return c


def qk_perm():
    perm = np.arange(3072)
    for sec in range(2):
        for h in range(NH):
            base = sec * 1024 + h * 64
            perm[base:base + 64] = base + np.concatenate([np.arange(0, 64, 2), np.arange(1, 64, 2)])
    return perm


def prep_weights(ws):
    out = {}
    out["w_xproj"] = np.ascontiguousarray(np.asarray(ws["L0_attn_W_xproj"], np.float32).T)
    out["w_dt_t"] = np.ascontiguousarray(np.asarray(ws["L0_attn_W_dt"], np.float32).T)
    out["b_dt_col"] = np.ascontiguousarray(np.asarray(ws["L0_attn_b_dt"], np.float32).reshape(8, 128).T)
    al = np.asarray(ws["L0_attn_A_log"], np.float32)
    acol = np.zeros((128, 128), dtype=np.float32)
    for dc in range(8):
        for g in range(16):
            for dl in range(8):
                acol[dl * 16:dl * 16 + 16, dc * 16 + g] = al[128 * dc + 8 * g + dl, :]
    out["a_log_col"] = acol
    out["d_col"] = np.ascontiguousarray(np.asarray(ws["L0_attn_D"], np.float32).reshape(8, 128).T)

    def coefgmat(G1, G2):
        ct = np.zeros((4, 1024), np.float32)
        for b in range(32):
            for cc in range(4):
                ct[cc, b * 32:b * 32 + 32] = G1[0, b, cc, :]
        gm = np.zeros((4, 256), np.float32)
        for cc in range(4):
            gm[cc] = np.ascontiguousarray(G2[cc, :, 0, :].T).reshape(256)
        return ct, gm
    out["coefT_R"], out["gmat_R"] = coefgmat(np.asarray(ws["L0_ffn_G1_R"], np.float32),
                                             np.asarray(ws["L0_ffn_G2_R"], np.float32))
    out["coefT_I"], out["gmat_I"] = coefgmat(np.asarray(ws["L0_ffn_G1_I"], np.float32),
                                             np.asarray(ws["L0_ffn_G2_I"], np.float32))
    out["bias_ri"] = np.concatenate([np.asarray(ws["L0_ffn_bias_real"], np.float32),
                                     np.asarray(ws["L0_ffn_bias_imag"], np.float32)])[None, :]
    out["w_qkv"] = np.ascontiguousarray(np.asarray(ws["L1_attn_w_qkv"], np.float32)[:, qk_perm()])
    out["w_o"] = np.ascontiguousarray(np.asarray(ws["L1_attn_w_o"], np.float32))
    out["log_scale"] = np.asarray(ws["L1_attn_log_scale"], np.float32).reshape(16, 1)
    out["w_gate"] = np.ascontiguousarray(np.asarray(ws["L1_ffn_w_gate"], np.float32))
    out["w_up"] = np.ascontiguousarray(np.asarray(ws["L1_ffn_w_up"], np.float32))
    out["w_down"] = np.ascontiguousarray(np.asarray(ws["L1_ffn_w_down"], np.float32))
    return out


INPUT_SPECS = [
    ("x", None, F32), ("ident_f32", (128, 128), F32), ("ident_bf16", (128, 128), BF16),
    ("Esel", (128, 2048), BF16), ("Ssel", (128, 2048), BF16), ("Rsel", (32, 256), BF16),
    ("ones16", (128, 128), BF16), ("eh", (16, 2048), BF16), ("cmask", (128, 2048), BF16),
    ("wfac", (128, 1), F32), ("hsel", (16, 1024), BF16), ("cvals", (128, 8), F32),
    ("pm_col", (128, 1), F32), ("onesq", (128, 128), BF16), ("pswap", (128, 128), BF16),
    ("w_xproj", (1024, 96), F32), ("w_dt_t", (64, 1024), F32), ("b_dt_col", (128, 8), F32),
    ("a_log_col", (128, 128), F32), ("d_col", (128, 8), F32),
    ("coefT_R", (4, 1024), F32), ("gmat_R", (4, 256), F32),
    ("coefT_I", (4, 1024), F32), ("gmat_I", (4, 256), F32), ("bias_ri", (1, 1024), F32),
    ("w_qkv", (1024, 3072), F32), ("w_o", (1024, 1024), F32), ("log_scale", (16, 1), F32),
    ("w_gate", (IF, 1024), F32), ("w_up", (IF, 1024), F32), ("w_down", (1024, IF), F32),
]


def build_program(T=2048, stage="full"):
    nc = bass.Bass("TRN2", target_bir_lowering=False)
    NCH = T // TC
    dram = {}
    for nm, sh, dt in INPUT_SPECS:
        shape = [T, DM] if nm == "x" else list(sh)
        dram[nm] = nc.dram_tensor(nm, shape, dt, kind="ExternalInput").ap()
    dram["out"] = nc.dram_tensor("out", [T, DM], F32, kind="ExternalOutput").ap()
    dram["wb_scr"] = nc.dram_tensor("wb_scr", [2, 1024, 256], F32, kind="Internal").ap()
    dram["wqkv_scr"] = nc.dram_tensor("wqkv_scr", [8, 128, 3072], BF16, kind="Internal").ap()
    dram["wg_scr"] = nc.dram_tensor("wg_scr", [8, 128, IF], BF16, kind="Internal").ap()
    dram["wu_scr"] = nc.dram_tensor("wu_scr", [8, 128, IF], BF16, kind="Internal").ap()
    dram["wd_scr"] = nc.dram_tensor("wd_scr", [22, 128, DM], BF16, kind="Internal").ap()
    dram["wo_scr"] = nc.dram_tensor("wo_scr", [16, 64, 1024], BF16, kind="Internal").ap()
    dram["ls_scr"] = nc.dram_tensor("ls_scr", [16, 1], F32, kind="Internal").ap()
    dram["mt_scr"] = nc.dram_tensor("mt_scr", [8, 128, 1024], BF16, kind="Internal").ap()
    dram["rk_scr"] = nc.dram_tensor("rk_scr", [8, 128, 2048], BF16, kind="Internal").ap()
    dram["v_scr"] = nc.dram_tensor("v_scr", [16, 128, 1024], BF16, kind="Internal").ap()
    dram["hqT_scr"] = nc.dram_tensor("hqT_scr", [4, 22, 128, 128], BF16, kind="Internal").ap()
    dram["avn_scr"] = nc.dram_tensor("avn_scr", [16, 64, 512], BF16, kind="Internal").ap()

    kb = KB(nc)
    es = contextlib.ExitStack()
    def sb(name, shape, dtype=F32):
        return es.enter_context(nc.sbuf_tensor(name, list(shape), dtype))

    S = {}
    def sbd(name, shape, dtype=F32):
        S[name] = sb(name, shape, dtype)
        return S[name]

    c_identf = sbd("c_identf", (128, 128))
    c_identb = sbd("c_identb", (128, 128), BF16)
    c_E = sbd("c_E", (128, 2048), BF16)
    c_S = sbd("c_S", (128, 2048), BF16)
    c_R = sbd("c_R", (32, 256), BF16)
    c_o16 = sbd("c_o16", (128, 128), BF16)
    c_eh = sbd("c_eh", (16, 2048), BF16)
    c_mask = sbd("c_mask", (128, 2048), BF16)
    c_wf = sbd("c_wf", (128, 1))
    c_hsel = sbd("c_hsel", (16, 1024), BF16)
    c_cv = sbd("c_cv", (128, 8))
    c_pm = sbd("c_pm", (128, 1))
    c_osq = sbd("c_osq", (128, 128), BF16)
    c_psw = sbd("c_psw", (128, 128), BF16)
    w_acol = sbd("w_acol", (128, 128))
    w_dcol = sbd("w_dcol", (128, 8))
    w_bdt = sbd("w_bdt", (128, 8))
    w_xpb = sbd("w_xpb", (128, 8 * 96), BF16)
    w_dtt = sbd("w_dtt", (64, 1024), BF16)
    w_bias0 = sbd("w_bias0", (128, 1024))
    w_stage = sbd("w_stage", (128, 1024))
    w_kv = sbd("w_kv", (128, 3072), BF16)
    w_str = sbd("w_str", (128, IF), BF16)
    w_oh = sbd("w_oh", (64, 1024), BF16)
    wg_b = sbd("wg_b", (128, 8 * 512), BF16)
    wu_b = sbd("wu_b", (128, 8 * 512), BF16)
    w_f0 = sbd("w_f0", (128, 2 * 512), BF16)
    wtt = sbd("wtt", (128, 512), BF16)
    avb = sbd("avb", (64, TC), BF16)
    rqh = sbd("rqh", (64, TC), BF16)
    bias_b = sbd("bias_b", (1, 1024), BF16)
    ls_b = sbd("ls_b", (16, 1), BF16)
    bufA = sbd("bufA", (128, 4 * DM))
    bufB = sbd("bufB", (128, 4 * DM))
    bufD = sbd("bufD", (128, DM))
    sq_s = sbd("sq_s", (128, DM))
    stat = sbd("stat", (128, 32))
    xn_fb = sbd("xn_fb", (128, 8 * TC), BF16)
    delta_b = sbd("delta_b", (128, TC), BF16)
    u_b = sbd("u_b", (128, TC), BF16)
    xp_b = sbd("xp_b", (64, TC), BF16)
    xp_bc = sbd("xp_bc", (32, TC), BF16)
    z_s = sbd("z_s", (128, TC))
    bc_f = sbd("bc_f", (128, 2 * TC))
    W_EE = sbd("W_EE", (128, 2 * TC), BF16)
    dA_s = W_EE[:, TC:2 * TC]; q_ee = W_EE[:, 0:TC]
    S["dA_s"] = dA_s; S["q_ee"] = q_ee
    W_CC = sbd("W_CC", (128, 2 * TC), BF16)
    dBx_s = W_CC[:, TC:2 * TC]; q_cc = W_CC[:, 0:TC]
    S["dBx_s"] = dBx_s; S["q_cc"] = q_cc
    W_VA = sbd("W_VA", (128, 2 * TC))
    h_s = W_VA[:, TC:2 * TC]; q_va = W_VA[:, 0:TC]
    S["h_s"] = h_s; S["q_va"] = q_va
    hc_s = sbd("hc_s", (128, TC), BF16)
    carry = sbd("carry", (128, 128))
    rp_powC = sbd("rp_powC", (128, 12 * 8))
    rp_powS = sbd("rp_powS", (128, 12 * 8))
    rp_sc = sbd("rp_sc", (128, 40))
    rp1 = sbd("rp1", (128, 512), BF16)
    rp2 = sbd("rp2", (128, 512), BF16)
    rp_t1 = sbd("rp_t1", (128, 256))
    rp_t2 = sbd("rp_t2", (128, 256))
    finC = sbd("finC", (128, 32 * 8)); finS = sbd("finS", (128, 32 * 8))
    coaC = sbd("coaC", (128, 64 * 8)); coaS = sbd("coaS", (128, 64 * 8))
    tbl = sbd("tbl", (128, 4 * TC), BF16)
    q_c = sbd("q_c", (128, TC), BF16)
    k_c = sbd("k_c", (128, TC), BF16)
    qs_c = sbd("qs_c", (128, TC), BF16)
    ks_c = sbd("ks_c", (128, TC), BF16)
    rq_s = sbd("rq_s", (128, 8 * TC), BF16)
    rkh = sbd("rkh", (64, 2048), BF16)
    vh = sbd("vh", (128, 16 * 65), BF16)
    nrm_k = sbd("nrm_k", (16, 512))
    nrm_q = sbd("nrm_q", (16, 512))
    mh_s = sbd("mh_s", (16, 8))
    shf_s = sbd("shf_s", (128, 16))
    pt_s = sbd("pt_s", (128, 2 * TC), BF16)
    av_s = sbd("av_s", (128, TC))
    W_H = sbd("W_H", (128, 2 * TC))
    hh_s = W_H[:, 0:TC]; rcp_s = W_H[:, TC:2 * TC]
    S["hh_s"] = hh_s; S["rcp_s"] = rcp_s
    g_s = sbd("g_s", (128, TC))
    q_sp = sbd("q_sp", (128, 512))
    q_am = sbd("q_am", (128, 64))
    q_rs = sbd("q_rs", (128, 64))
    hq_t = sbd("hq_t", (128, TC), BF16)
    xq_t = sbd("xq_t", (128, DM), BF16)
    xq_f = sbd("xq_f", (128, 8 * 128), BF16)
    wnk = sbd("wnk", (128, IF), BF16)
    ones_b = sbd("ones_b", (128, 1), BF16)
    ones_r = sbd("ones_r", (1, 128), BF16)
    gat = sbd("gat", (128, 512))
    wbt = gat
    coefT = w_str[0:4, 0:2048]
    gmat = w_kv[0:4, 2048:2560]

    ps = [es.enter_context(nc.psum_tensor(f"ps{i}", [128, 512], F32)) for i in range(8)]
    psT16 = ps[0][:, 0:64].bitcast(BF16)
    psT16b = ps[3][:, 0:64].bitcast(BF16)
    S["ps"] = ps
    S["psT16"] = psT16
    S["psT16b"] = psT16b
    S["dram"] = dram

    for nm, t in [("ident_f32", c_identf), ("ident_bf16", c_identb), ("Esel", c_E),
                  ("Ssel", c_S), ("Rsel", c_R), ("ones16", c_o16), ("eh", c_eh),
                  ("cmask", c_mask), ("wfac", c_wf), ("hsel", c_hsel),
                  ("a_log_col", w_acol), ("d_col", w_dcol), ("b_dt_col", w_bdt),
                  ("cvals", c_cv), ("pm_col", c_pm), ("onesq", c_osq), ("pswap", c_psw)]:
        kb.dma(t[:], dram[nm], [], [nm])
    for i, v in enumerate(CONST_VALS):
        nc.const_aps.register(F32, v, c_cv[:, i:i + 1])
    kb.memset(ones_b[:], 1.0, ["ones_b"])
    kb.memset(ones_r[:], 1.0, ["ones_r"])
    kb.memset(gat[:], 0.0, ["gat"])
    kb.memset(vh[:, 64::65], 1.0, ["vh_ones"])
    kb.act(w_acol[:], w_acol[:], Act.Exp, ["a_log_col"], ["a_exp"])
    kb.ts(w_acol[:], w_acol[:], -1.0, None, Alu.mult, None, ["a_exp"], ["a_col"])
    kb.dma(w_stage[:64, :1024], dram["w_dt_t"], ["w_stage"], ["w_stage"])
    kb.cp_s(w_dtt[:], w_stage[:64, :1024], ["w_stage"], ["w_dtt"])
    tbl32 = tbl.bitcast(F32)
    def quant_slice(src, dst, kin, kout, nb, par=0):
        if par == 0:
            qva, qcc, qee, qsp, qam, qrs = q_va, q_cc, q_ee, q_sp, q_am, q_rs
        else:
            qva, qcc, qee = h_s, dBx_s, dA_s
            qsp = z_s
            qam, qrs = bc_f[:, 0:64], bc_f[:, 64:128]
        kb.red(qam[:, :nb].rearrange("p (b o) -> p b o", o=1),
               src.rearrange("p (b k) -> p b k", k=16), Alu.max, kin, [kout + "_am"], absval=True)
        kb.act(qsp[:, :nb * 16], src, Act.Sign, kin, [kout + "_sg"])
        kb.ts(qam[:, :nb], qam[:, :nb], 1.0 / 6.0, 1e-12, Alu.mult, Alu.max,
              [kout + "_am"], [kout + "_am2"])
        kb.recip(qrs[:, :nb], qam[:, :nb], [kout + "_am2"], [kout + "_rs"])
        kb.stt(qva[:, :nb * 16], src, -1.0, src, Alu.mult, Alu.max, kin, [kout + "_abs"])
        kb.ts(qam[:, :nb], qam[:, :nb], 0.5, None, Alu.mult, None,
              [kout + "_am2", kout + "_rs"], [kout + "_hs"])
        rsb = qrs[:, :nb].rearrange("p (b o) -> p b o", o=1).broadcast_to([128, nb, 16])
        kb.tt(qva[:, :nb * 16].rearrange("p (b k) -> p b k", k=16),
              qva[:, :nb * 16].rearrange("p (b k) -> p b k", k=16), rsb, Alu.mult,
              [kout + "_abs", kout + "_rs"], [kout + "_va"])
        scb = qam[:, :nb].rearrange("p (b o) -> p b o", o=1).broadcast_to([128, nb, 16])
        va = qva[:, :nb * 16]
        kb.ts(qcc[:, :nb * 16], va, 0.25, None, Alu.is_gt, None, [kout + "_va"], [kout + "_c"])
        kb.ts(qee[:, :nb * 16], va, 5.0, None, Alu.is_gt, None, [kout + "_va"], [kout + "_e"])
        kb.tt(qsp[:, :nb * 16].rearrange("p (b k) -> p b k", k=16),
              qsp[:, :nb * 16].rearrange("p (b k) -> p b k", k=16), scb, Alu.mult,
              [kout + "_sg", kout + "_hs"], [kout + "_sp"], eng="gpsimd")
        for mc, me in ((0.75, 5.0), (1.25, 3.5), (1.75, 2.5)):
            kb.stt(qcc[:, :nb * 16], va, mc, qcc[:, :nb * 16], Alu.is_gt, Alu.add,
                   [kout + "_va", kout + "_c"], [kout + "_c"])
            kb.stt(qee[:, :nb * 16], va, me, qee[:, :nb * 16], Alu.is_gt, Alu.add,
                   [kout + "_va", kout + "_e"], [kout + "_e"])
        kb.stt(qcc[:, :nb * 16], qee[:, :nb * 16], 2.0, qcc[:, :nb * 16], Alu.mult, Alu.add,
               [kout + "_e", kout + "_c"], [kout + "_c2"])
        kb.tt(dst, qcc[:, :nb * 16], qsp[:, :nb * 16], Alu.mult,
              [kout + "_c2", kout + "_sp"], [kout])

    QSET0 = dict(va=W_VA, cc=W_CC, ee=W_EE, am=q_am, rs=q_rs, sps=None)
    def quant_multi(jobs):
        js = []
        for (src, kin, kout, nb, dsts, qs) in jobs:
            if qs is None:
                qs = QSET0
            sps = qs["sps"] or [q_sp, z_s]
            w = nb * 16
            nd = len(dsts)
            js.append((src, kin, kout, nb, dsts, qs, sps, w, nd, w // nd, nb // nd))
        for src, kin, kout, nb, dsts, qs, sps, w, nd, hw, nbh in js:
            kb.red(qs["am"][:, :nb].rearrange("p (b o) -> p b o", o=1),
                   src.rearrange("p (b k) -> p b k", k=16), Alu.max, kin, [kout + "_am"],
                   absval=True)
            for i in range(nd):
                kb.act(sps[i][:, :hw], src[:, i * hw:(i + 1) * hw], Act.Sign, kin,
                       [kout + f"_sg{i}"])
        for src, kin, kout, nb, dsts, qs, sps, w, nd, hw, nbh in js:
            kb.ts(qs["am"][:, :nb], qs["am"][:, :nb], 1.0 / 6.0, 1e-12, Alu.mult, Alu.max,
                  [kout + "_am"], [kout + "_am2"])
        for src, kin, kout, nb, dsts, qs, sps, w, nd, hw, nbh in js:
            kb.recip(qs["rs"][:, :nb], qs["am"][:, :nb], [kout + "_am2"], [kout + "_rs"])
            kb.stt(qs["va"][:, :w], src, -1.0, src, Alu.mult, Alu.max, kin, [kout + "_abs"])
        for src, kin, kout, nb, dsts, qs, sps, w, nd, hw, nbh in js:
            rsb = qs["rs"][:, :nb].rearrange("p (b o) -> p b o", o=1).broadcast_to([128, nb, 16])
            kb.tt(qs["va"][:, :w].rearrange("p (b k) -> p b k", k=16),
                  qs["va"][:, :w].rearrange("p (b k) -> p b k", k=16), rsb, Alu.mult,
                  [kout + "_abs", kout + "_rs"], [kout + "_va"])
        for src, kin, kout, nb, dsts, qs, sps, w, nd, hw, nbh in js:
            va = qs["va"][:, :w]
            kb.ts(qs["cc"][:, :w], va, 0.25, None, Alu.is_gt, None, [kout + "_va"], [kout + "_c"])
            kb.ts(qs["ee"][:, :w], va, 5.0, None, Alu.is_gt, None, [kout + "_va"], [kout + "_e"])
            for i in range(nd):
                scb = qs["am"][:, i * nbh:(i + 1) * nbh].rearrange("p (b o) -> p b o", o=1)\
                    .broadcast_to([128, nbh, 16])
                kb.tt(sps[i][:, :hw].rearrange("p (b k) -> p b k", k=16),
                      sps[i][:, :hw].rearrange("p (b k) -> p b k", k=16), scb, Alu.mult,
                      [kout + f"_sg{i}", kout + "_am2"], [kout + f"_sp{i}"], eng="gpsimd")
        for mc, me in ((0.75, 5.0), (1.25, 3.5), (1.75, 2.5)):
            for src, kin, kout, nb, dsts, qs, sps, w, nd, hw, nbh in js:
                va = qs["va"][:, :w]
                kb.stt(qs["cc"][:, :w], va, mc, qs["cc"][:, :w], Alu.is_gt, Alu.add,
                       [kout + "_va", kout + "_c"], [kout + "_c"])
                kb.stt(qs["ee"][:, :w], va, me, qs["ee"][:, :w], Alu.is_gt, Alu.add,
                       [kout + "_va", kout + "_e"], [kout + "_e"])
        for src, kin, kout, nb, dsts, qs, sps, w, nd, hw, nbh in js:
            kb.stt(qs["cc"][:, :w], qs["cc"][:, :w], 0.5, qs["ee"][:, :w], Alu.mult, Alu.add,
                   [kout + "_e", kout + "_c"], [kout + "_c2"])
        for src, kin, kout, nb, dsts, qs, sps, w, nd, hw, nbh in js:
            for i, (dst, dkey) in enumerate(dsts):
                kb.tt(dst, qs["cc"][:, i * hw:(i + 1) * hw], sps[i][:, :hw], Alu.mult,
                      [kout + "_c2", kout + f"_sp{i}"], [dkey])

    def quant_wide(src, kin, kout, nb, dsts):
        quant_multi([(src, kin, kout, nb, dsts, None)])

    S["quant_slice"] = quant_slice
    S["quant_wide"] = quant_wide
    S["quant_multi"] = quant_multi
    xnf32 = xn_fb.bitcast(F32)
    pt32 = pt_s.bitcast(F32)
    QSET1 = dict(va=xnf32[:, 0:1024], cc=rq_s[:, 0:1024], ee=rq_s[:, 1024:2048],
                 am=pt32[:, 0:64], rs=pt32[:, 64:128],
                 sps=[xnf32[:, 1024:1536], xnf32[:, 1536:2048]])
    psT16e = ps[6][:, 0:64].bitcast(BF16)
    psT16f = ps[7][:, 0:64].bitcast(BF16)
    W2Q = []
    W2T = []
    if stage in ("full", "ffnw", "l1"):
        for wname, scr, NT, K in [("w_gate", "wg_scr", 22, 1024), ("w_up", "wu_scr", 22, 1024),
                                  ("w_down", "wd_scr", 8, IF)]:
            for nt in range(NT):
                W2T.append((wname, scr, nt, K))

    stgB = rq_s.bitcast(F32)[:, 1024:2048]
    def w2trans(wname, scr, nt, K, woff, pvs, pks, wslots, wslotks):
        nkc = K // 128
        for kc in range(nkc):
            pcq = (kc * 128) // 1024
            kwq = min(1024, K - pcq * 1024)
            qk = f"wq_{wname}_{nt}_{pcq}_{0 if (kc * 128) % 1024 < kwq // 2 else 1}"
            pv, pk = pvs[kc % 2], pks[kc % 2]
            wslot, wsk = wslots[kc % 4], wslotks[kc % 4]
            kb.transp(pv[:, :], wnk[:, woff + kc * 128:woff + (kc + 1) * 128], c_identb[:],
                      [qk, "ident_bf16"], [pk + "T", pk])
            kb.act(wslot, pv[:, :], Act.Copy, [pk + "T", pk], [wsk])
            kb.dma(dram[scr][kc][:, nt * 128:(nt + 1) * 128], wslot,
                   [wsk], [f"wqs_{wname}_{kc}_{nt}"])
    def w2piece(wname, nt, K, pc, stg, stgk, woff, qs):
        kw = min(1024, K - pc * 1024)
        kb.dma(stg[:, :kw], dram[wname][nt * 128:(nt + 1) * 128, pc * 1024:pc * 1024 + kw],
               [stgk], [stgk, f"wq_ld_{wname}_{nt}_{pc}"], q="gpsimd")
        hw = kw // 2
        return (stg[:, :kw], [f"wq_ld_{wname}_{nt}_{pc}"],
                f"wq_{wname}_{nt}_{pc}", kw // 16,
                [(wnk[:, woff + pc * 1024: woff + pc * 1024 + hw], f"wq_{wname}_{nt}_{pc}_0"),
                 (wnk[:, woff + pc * 1024 + hw: woff + pc * 1024 + kw], f"wq_{wname}_{nt}_{pc}_1")],
                qs)
    wttks = [f"wtt{i}" for i in range(4)]
    xqks = [f"xqw{i}" for i in range(4)]
    xqsl = [xq_t[:, i * 128:(i + 1) * 128] for i in range(4)]
    wtsl = [wtt[:, i * 128:(i + 1) * 128] for i in range(4)]
    NEARLY = min(40, len(W2T) - len(W2T) % 2)
    for p in range(0, NEARLY, 2):
        def w2pair(p=p):
            a = W2T[p]; b = W2T[p + 1]
            ja = w2piece(a[0], a[2], a[3], 0, tbl32, "tbl32", 0, QSET0)
            jb = w2piece(b[0], b[2], b[3], 0, stgB, "stgB", 1024, QSET1)
            quant_multi([ja, jb])
            w2trans(a[0], a[1], a[2], a[3], 0, [psT16, psT16b], ["@ps0", "@ps3"], wtsl, wttks)
            w2trans(b[0], b[1], b[2], b[3], 1024, [psT16e, psT16f], ["@ps6", "@ps7"], xqsl, xqks)
        W2Q.append(w2pair)
    for i in range(NEARLY, len(W2T)):
        def w2one(i=i):
            wname, scr, nt, K = W2T[i]
            for pc in range((K + 1023) // 1024):
                quant_multi([w2piece(wname, nt, K, pc, w_stage, "w_stage", 0, QSET0)])
            w2trans(wname, scr, nt, K, 0, [psT16, psT16b], ["@ps0", "@ps3"], wtsl, wttks)
        W2Q.append(w2one)
    def w2pump(n):
        while n > 0 and W2Q:
            W2Q.pop(0)()
            n -= 1
    S["w2pump"] = w2pump

    for ch in range(8):
        kb.dma(w_stage[:, :96], dram["w_xproj"][ch * 128:(ch + 1) * 128, :], ["w_stage"], ["w_stage"])
        kb.cp_s(w_xpb[:, ch * 96:(ch + 1) * 96], w_stage[:, :96], ["w_stage"], [f"w_xpb_{ch}"])
    kb.dma(w_stage[:1, :1024], dram["bias_ri"], ["w_stage"], ["w_stage"])
    kb.cp_s(bias_b[:], w_stage[:1, :1024], ["w_stage"], ["bias_b"])
    for half in range(2):
        kb.mm(ps[half][:, :512], ones_r[:], bias_b[:, half * 512:(half + 1) * 512],
              ["bias_b", "ones_r"], [f"ps_bias{half}", f"@ps{half}"])
        kb.cp_v(w_bias0[:, half * 512:(half + 1) * 512], ps[half][:, :512],
                [f"ps_bias{half}", f"@ps{half}"], ["w_bias0"])
    for ch in range(8):
        cstg, cstgk = (w_stage, "w_stage") if ch % 2 == 0 else (bufD, "bufDc")
        kb.dma(cstg[:, :1024], dram["w_o"][ch * 128:(ch + 1) * 128, :], [cstgk, "w_kv"], [cstgk])
        kb.cp_s(w_kv[:, :1024], cstg[:, :1024], [cstgk], ["w_kv"])
        kb.dma(dram["wo_scr"][2 * ch], w_kv[:64, :1024], ["w_kv"], [f"wo_scr_{2 * ch}"])
        kb.dma(dram["wo_scr"][2 * ch + 1], w_kv[64:128, :1024], ["w_kv"], [f"wo_scr_{2 * ch + 1}"])
        S["w2pump"](1)
    for ch in range(8):
        for piece in range(3):
            cstg, cstgk = (w_stage, "w_stage") if (ch * 3 + piece) % 2 == 0 else (bufD, "bufDc")
            kb.dma(cstg[:, :1024],
                   dram["w_qkv"][ch * 128:(ch + 1) * 128, piece * 1024:(piece + 1) * 1024],
                   [cstgk, "w_kv"], [cstgk])
            kb.cp_s(w_kv[:, piece * 1024:(piece + 1) * 1024], cstg[:, :1024],
                    [cstgk], ["w_kv"])
        kb.dma(dram["wqkv_scr"][ch], w_kv[:], ["w_kv"], [f"wqkv_scr_{ch}"])
        S["w2pump"](1)

    for i, (cn, gn) in enumerate([("coefT_R", "gmat_R"), ("coefT_I", "gmat_I")]):
        kb.dma(w_stage[:4, :1024], dram[cn], ["w_stage"], ["w_stage"])
        kb.cp_s(coefT[:, i * 1024:(i + 1) * 1024], w_stage[:4, :1024], ["w_stage"], [f"coefT_{i}", "w_str"])
        kb.dma(w_stage[:4, :256], dram[gn], ["w_stage"], ["w_stage"])
        kb.cp_s(gmat[:, i * 256:(i + 1) * 256], w_stage[:4, :256], ["w_stage"], [f"gmat_{i}", "w_kv"])
    for i in range(2):
        for ch in range(8):
            kb.mm(ps[2][:, :256], coefT[:, i * 1024 + ch * 128:i * 1024 + (ch + 1) * 128],
                  gmat[:, i * 256:(i + 1) * 256], [f"coefT_{i}", f"gmat_{i}", "wbt"], ["ps_wb", "@ps2"])
            kb.cp_v(wbt[:, :256], ps[2][:, :256], ["ps_wb", "@ps2"], ["wbt"])
            kb.dma(dram["wb_scr"][i, ch * 128:(ch + 1) * 128, :], wbt[:, :256], ["wbt"], [f"wbscr_{i}_{ch}"])
            S["w2pump"](1)
    wb_deps = [f"wbscr_{i}_{c}" for i in range(2) for c in range(8)]
    for ic in range(8):
        ihalf, icl = ic // 4, ic % 4
        for ohalf in range(2):
            src = 0 if (ihalf + ohalf) % 2 == 0 else 1
            neg = (ihalf == 1 and ohalf == 0)
            for dl in range(8):
                kb.dma(gat[dl * 16:(dl + 1) * 16, :].rearrange("p (b e) -> p b e", b=32),
                       dram["wb_scr"][src].rearrange("(b d) (g e) -> d g b e", b=32, g=16)[8 * icl + dl],
                       wb_deps + ["gat", "wbt"], [f"gat_{dl}"])
            kb.act(w_kv[:, ohalf * 512:(ohalf + 1) * 512], gat[:],
                   Act.Copy, [f"gat_{dl}" for dl in range(8)], ["gat", "wbt", f"mtst_{ic}_{ohalf}"],
                   scale=(-1.0 if neg else 1.0))
        kb.dma(dram["mt_scr"][ic], w_kv[:, 0:1024], [f"mtst_{ic}_0", f"mtst_{ic}_1"], [f"mt_{ic}"])

    kb.dma(w_stage[:16, :1], dram["log_scale"], ["w_stage"], ["w_stage"])
    kb.dma(dram["ls_scr"], w_stage[:16, :1], ["w_stage"], ["ls_scr"])
    for hh in range(2):
        src = dram["ls_scr"].rearrange("(dc two) one -> two (dc one)", two=2)[hh]
        src = src.rearrange("(o dc) -> o dc", o=1).broadcast_to([64, 8])
        kb.dma(rp_sc[64 * hh:64 * (hh + 1), 32:40], src, ["ls_scr"], [f"ls_rep_{hh}"],
               allow_slow_non_contiguous=True)
    kb.act(rp_sc[:, 0:8], rp_sc[:, 32:40], Act.Exp, ["ls_rep_0", "ls_rep_1"], ["s_rep"])
    kb.ts(rp_sc[:, 8:16], rp_sc[:, 0:8], c_wf[:, 0:1], None, Alu.mult, None,
          ["s_rep", "wfac"], ["omega"])
    for lev in range(9):
        m = TWO_PI * (2 ** (8 - lev))
        kb.ts(rp_sc[:, 16:24], rp_sc[:, 8:16], m, None, Alu.is_ge, None, ["omega"], ["rr_t"])
        kb.stt(rp_sc[:, 8:16], rp_sc[:, 16:24], -m, rp_sc[:, 8:16], Alu.mult, Alu.add,
               ["rr_t", "omega"], ["omega"])
    kb.ts(rp_sc[:, 16:24], rp_sc[:, 8:16], math.pi, None, Alu.is_ge, None, ["omega"], ["rr_t"])
    kb.stt(rp_sc[:, 8:16], rp_sc[:, 16:24], -TWO_PI, rp_sc[:, 8:16], Alu.mult, Alu.add,
           ["rr_t", "omega"], ["omega"])
    kb.act(rp_sc[:, 24:32], rp_sc[:, 8:16], Act.Sin, ["omega"], ["seed_sh"], scale=0.5)
    kb.act(rp_powS[:, 0:8], rp_sc[:, 8:16], Act.Sin, ["omega"], ["powS_0"])
    kb.tt(rp_sc[:, 16:24], rp_sc[:, 24:32], rp_sc[:, 24:32], Alu.mult, ["seed_sh"], ["sh2"])
    kb.ts(rp_powC[:, 0:8], rp_sc[:, 16:24], -2.0, 1.0, Alu.mult, Alu.add, ["sh2"], ["powC_0"])
    for k in range(11):
        kb.tt(rp_sc[:, 16:24], rp_powC[:, k * 8:(k + 1) * 8], rp_powC[:, k * 8:(k + 1) * 8],
              Alu.mult, [f"powC_{k}"], ["pk_c2"])
        kb.ts(rp_powC[:, (k + 1) * 8:(k + 2) * 8], rp_sc[:, 16:24], 2.0, -1.0, Alu.mult, Alu.add,
              ["pk_c2"], [f"powC_{k + 1}"])
        kb.tt(rp_sc[:, 24:32], rp_powS[:, k * 8:(k + 1) * 8], rp_powC[:, k * 8:(k + 1) * 8],
              Alu.mult, [f"powS_{k}", f"powC_{k}"], ["pk_sc"])
        kb.ts(rp_powS[:, (k + 1) * 8:(k + 2) * 8], rp_sc[:, 24:32], 2.0, None, Alu.mult, None,
              ["pk_sc"], [f"powS_{k + 1}"])

    def bc8(t, width):
        return t.rearrange("p (o dc) -> p o dc", o=1).broadcast_to([128, width, 8])

    def r3(t, width):
        return t.rearrange("p (w dc) -> p w dc", dc=8)

    def rot_block(dstC, dstS, srcC, srcS, pk, rdeps, wtagC, wtagS, width):
        pc = bc8(rp_powC[:, pk * 8:(pk + 1) * 8], width)
        pss = bc8(rp_powS[:, pk * 8:(pk + 1) * 8], width)
        n = width * 8
        kb.tt(r3(rp_t1[:, 0:n], width), r3(srcC, width), pc, Alu.mult, rdeps, ["rp_t1"])
        kb.tt(r3(rp_t2[:, 0:n], width), r3(srcS, width), pss, Alu.mult, rdeps, ["rp_t2"])
        kb.tt(dstC, rp_t1[:, 0:n], rp_t2[:, 0:n], Alu.subtract, ["rp_t1", "rp_t2"], [wtagC])
        kb.tt(r3(rp_t1[:, 0:n], width), r3(srcS, width), pc, Alu.mult, rdeps + [wtagC], ["rp_t1"])
        kb.tt(r3(rp_t2[:, 0:n], width), r3(srcC, width), pss, Alu.mult, rdeps + [wtagC], ["rp_t2"])
        kb.tt(dstS, rp_t1[:, 0:n], rp_t2[:, 0:n], Alu.add, ["rp_t1", "rp_t2"], [wtagS])

    kb.memset(finC[:, 0:8], 1.0, ["finC_b0"])
    kb.memset(finS[:, 0:8], 0.0, ["finS_b0"])
    for k in range(5):
        w = 2 ** k
        rot_block(finC[:, w * 8:2 * w * 8], finS[:, w * 8:2 * w * 8],
                  finC[:, 0:w * 8], finS[:, 0:w * 8], k,
                  [f"finC_b{j}" for j in range(k + 1)] + [f"finS_b{j}" for j in range(k + 1)]
                  + [f"powC_{k}", f"powS_{k}"], f"finC_b{k + 1}", f"finS_b{k + 1}", w)
    kb.memset(coaC[:, 0:8], 1.0, ["coaC_b0"])
    kb.memset(coaS[:, 0:8], 0.0, ["coaS_b0"])
    for k in range(6):
        w = 2 ** k
        rot_block(coaC[:, w * 8:2 * w * 8], coaS[:, w * 8:2 * w * 8],
                  coaC[:, 0:w * 8], coaS[:, 0:w * 8], 5 + k,
                  [f"coaC_b{j}" for j in range(k + 1)] + [f"coaS_b{j}" for j in range(k + 1)]
                  + [f"powC_{5 + k}", f"powS_{5 + k}"], f"coaC_b{k + 1}", f"coaS_b{k + 1}", w)
    fin_done = [f"finC_b{j}" for j in range(6)] + [f"finS_b{j}" for j in range(6)]
    coa_done = [f"coaC_b{j}" for j in range(7)] + [f"coaS_b{j}" for j in range(7)]
    S["fin_done"] = fin_done
    S["coa_done"] = coa_done
    S["quant_slice"] = None

    if stage == "weights":
        for ic in range(4):
            kb.cp_s(sq_s[:], w_ffn0[:, ic * 1024:(ic + 1) * 1024], [f"w_ffn0_{ic}", "sq_s"], ["sq_s"])
            kb.dma(dram["out"][ic * 128:(ic + 1) * 128, :], sq_s[:], ["sq_s"], ["outd", "sq_s2"])
        kb.dma(dram["out"][512:640, 0:256], finC[:], fin_done, ["outd2"])
        kb.dma(dram["out"][512:640, 256:512], finS[:], fin_done, ["outd3"])
        kb.dma(dram["out"][640:768, 0:512], coaC[:], coa_done, ["outd4"])
        kb.dma(dram["out"][640:768, 512:1024], coaS[:], coa_done, ["outd5"])
        kb.dma(dram["out"][768:896, 0:128], w_acol[:], ["a_col"], ["outd6"])
        kb.dma(dram["out"][768:896, 128:129], rp_sc[:, 8:9], ["omega"], ["outd7"],
               allow_slow_non_contiguous=True)
        block = es.enter_context(nc.Block())
        es.enter_context(kb.lay(block))
        return nc, dram, kb, es


    def xpre(c2):
        cc2 = f"c{c2}"
        for j in range(4):
            kb.dma(bufB[:, j * DM:(j + 1) * DM],
                   dram["x"][c2 * TC + j * 128: c2 * TC + (j + 1) * 128, :],
                   [], [f"xB_{cc2}_{j}"])
        for j in range(4):
            xa = bufB[:, j * DM:(j + 1) * DM]
            kb.act(sq_s[:], xa, Act.Square, [f"xB_{cc2}_{j}", "sq_s"], ["sq_s"])
            kb.red(stat[:, j:j + 1], xa, Alu.add, [f"xB_{cc2}_{j}"], [f"st_sum_{j}"])
            kb.red(stat[:, 4 + j:5 + j], sq_s[:], Alu.add, ["sq_s"], [f"st_sq_{j}"])
            kb.ts(stat[:, 8 + j:9 + j], stat[:, j:j + 1], 1.0 / DM, None, Alu.mult, None,
                  [f"st_sum_{j}"], [f"st_mean_{j}"])
            kb.ts(stat[:, 12 + j:13 + j], stat[:, 8 + j:9 + j], -1.0, None, Alu.mult, None,
                  [f"st_mean_{j}"], [f"st_nm_{j}"])
            kb.ts(stat[:, 16 + j:17 + j], stat[:, 4 + j:5 + j], 1.0 / DM, None, Alu.mult, None,
                  [f"st_sq_{j}"], [f"st_msq_{j}"])
            kb.stt(stat[:, 20 + j:21 + j], stat[:, 12 + j:13 + j], stat[:, 8 + j:9 + j],
                   stat[:, 16 + j:17 + j], Alu.mult, Alu.add,
                   [f"st_nm_{j}", f"st_mean_{j}", f"st_msq_{j}"], [f"st_var_{j}"])
            kb.act(stat[:, 24 + j:25 + j], stat[:, 20 + j:21 + j], Act.Ln,
                   [f"st_var_{j}"], [f"st_ln_{j}"], bias=1e-5)
            kb.act(stat[:, 28 + j:29 + j], stat[:, 24 + j:25 + j], Act.Exp,
                   [f"st_ln_{j}"], [f"st_rstd_{j}"], scale=-0.5)
    S["xpre"] = xpre

    for c in range(NCH):
        cc = f"c{c}"
        if c == 0:
            xpre(0)
        for j in range(4):
            kb.cp_s(bufA[:, j * DM:(j + 1) * DM], bufB[:, j * DM:(j + 1) * DM],
                    [f"xB_{cc}_{j}", f"bufA_{cc}"], [f"xA_{cc}_{j}", f"bufA_{cc}"])
        for j in range(4):
            kb.ts(bufB[:, j * DM:(j + 1) * DM], bufB[:, j * DM:(j + 1) * DM],
                  stat[:, 8 + j:9 + j], stat[:, 28 + j:29 + j], Alu.subtract, Alu.mult,
                  [f"xB_{cc}_{j}", f"xA_{cc}_{j}", f"st_mean_{j}", f"st_rstd_{j}", f"bufB_{cc}"],
                  [f"xn_{cc}_{j}", f"bufB_{cc}"])
        for dc in range(8):
            for j in range(4):
                pr = (dc * 4 + j) % 2
                bk = ps[0] if pr == 0 else ps[3]
                pkx = "@ps0" if pr == 0 else "@ps3"
                kb.transp(bk[:, :128], bufB[:, j * DM + dc * 128: j * DM + (dc + 1) * 128],
                          c_identf[:], [f"xn_{cc}_{j}", "ident_f32"], [f"psT{pr}x", pkx])
                kb.act(xn_fb[:, dc * TC + j * 128: dc * TC + (j + 1) * 128], bk[:, :128],
                       Act.Copy, [f"psT{pr}x", pkx], [f"xnf_{cc}_{dc}"])
        xnf_all = [f"xnf_{cc}_{dc}" for dc in range(8)]
        kb.mmg([(ps[6][:64, :512], w_xpb[:, ch * 96: ch * 96 + 64],
                 xn_fb[:, ch * TC:(ch + 1) * TC], ch == 0, ch == 7) for ch in range(8)],
               xnf_all + [f"w_xpb_{ch}" for ch in range(8)], [f"ps_dt_{cc}", "@ps6"])
        kb.mmg([(ps[7][:32, :512], w_xpb[:, ch * 96 + 64: ch * 96 + 96],
                 xn_fb[:, ch * TC:(ch + 1) * TC], ch == 0, ch == 7) for ch in range(8)],
               xnf_all + [f"w_xpb_{ch}" for ch in range(8)], [f"ps_bc_{cc}", "@ps7"])
        kb.act(xp_b[:, :], ps[6][:64, :512], Act.Copy, [f"ps_dt_{cc}", "@ps6"], [f"xpb_{cc}"])
        kb.act(xp_bc[:, :], ps[7][:32, :512], Act.Copy, [f"ps_bc_{cc}", "@ps7"], [f"xpbc_{cc}"])
        kb.mm(ps[6][:, :512], c_R[:, 0:128], xp_bc[:], [f"xpbc_{cc}", "Rsel"], [f"ps_Bb_{cc}", "@ps6"])
        kb.cp_v(bc_f[:, 0:512], ps[6][:, :512], [f"ps_Bb_{cc}", "@ps6"], [f"bcB_{cc}"])
        kb.mm(ps[6][:, :512], c_R[:, 128:256], xp_bc[:], [f"xpbc_{cc}", "Rsel"], [f"ps_Cb_{cc}", "@ps6"])
        kb.cp_v(bc_f[:, 512:1024], ps[6][:, :512], [f"ps_Cb_{cc}", "@ps6"], [f"bcC_{cc}"])
        dAb = [dA_s[:], tbl[:, 0:512]]
        dBxb = [dBx_s[:], tbl[:, 512:1024]]
        hb = [h_s[:], av_s[:]]
        hcb = [hc_s[:], qs_c[:]]
        deltab = [delta_b[:], k_c[:]]
        ub = [u_b[:], ks_c[:]]
        drb = [3, 1]
        urb = [4, 2]
        dpb = [7, 6]
        for dc in range(8):
            p2 = dc % 2
            dlt, uu = deltab[p2], ub[p2]
            dpk = dpb[p2]
            kb.mmg([(ps[dpk][:, :512], w_dtt[:, dc * 128:(dc + 1) * 128], xp_b[:], True, True)],
                   [f"xpb_{cc}", "w_dtt"], [f"ps_dp_{cc}_{dc}", f"@ps{dpk}"])
            kb.ts(sq_s[:, p2 * 512:(p2 + 1) * 512], ps[dpk][:, :512], w_bdt[:, dc:dc + 1], 88.0,
                  Alu.add, Alu.min,
                  [f"ps_dp_{cc}_{dc}", f"@ps{dpk}", "b_dt_col"], [f"dpre_{cc}_{dc}"])
            kb.act(sq_s[:, p2 * 512:(p2 + 1) * 512], sq_s[:, p2 * 512:(p2 + 1) * 512], Act.Exp,
                   [f"dpre_{cc}_{dc}"], [f"dexp_{cc}_{dc}"])
            kb.act(dlt, sq_s[:, p2 * 512:(p2 + 1) * 512], Act.Ln, [f"dexp_{cc}_{dc}"],
                   [f"delta_{cc}_{dc}"], bias=1.0)
            kb.tt(uu, dlt, xn_fb[:, dc * TC:(dc + 1) * TC], Alu.mult,
                  [f"delta_{cc}_{dc}", f"xnf_{cc}_{dc}"], [f"u_{cc}_{dc}"])
            pend_sel = []
            for g0 in range(0, 16, 2):
                pair = [(g0 + i, (g0 + i) % 2, f"{cc}_{dc}_{g0 + i}") for i in range(2)]
                for g, gp, gk in pair:
                    kb.mm(ps[drb[gp]][:, :512], c_E[:, g * 128:(g + 1) * 128], dlt,
                          [f"delta_{cc}_{dc}", "Esel"], [f"ps_dr_{gk}", f"@ps{drb[gp]}"])
                    kb.mm(ps[urb[gp]][:, :512], c_E[:, g * 128:(g + 1) * 128], uu,
                          [f"u_{cc}_{dc}", "Esel"], [f"ps_ur_{gk}", f"@ps{urb[gp]}"])
                for fn_ in pend_sel:
                    fn_()
                pend_sel = []
                for g, gp, gk in pair:
                    kb.act(dAb[gp], ps[drb[gp]][:, :512], Act.Exp, [f"ps_dr_{gk}", f"@ps{drb[gp]}", "a_col"],
                           [f"dA_{gk}"], scale=w_acol[:, dc * 16 + g: dc * 16 + g + 1])
                for g, gp, gk in pair:
                    kb.tt(dBxb[gp], ps[urb[gp]][:, :512], bc_f[:, 0:512], Alu.mult,
                          [f"ps_ur_{gk}", f"@ps{urb[gp]}", f"bcB_{cc}"], [f"dBx_{gk}"])
                for g, gp, gk in pair:
                    init = 0.0 if c == 0 else carry[:, dc * 16 + g: dc * 16 + g + 1]
                    rd = [f"dA_{gk}", f"dBx_{gk}"] + ([] if c == 0 else [f"carry_{dc}_{g}"])
                    kb.scan(hb[gp], dAb[gp], dBxb[gp], init, rd, [f"h_{gk}"])
                for g, gp, gk in pair:
                    if c < NCH - 1:
                        kb.cp_v(carry[:, dc * 16 + g: dc * 16 + g + 1], hb[gp][:, 511:512],
                                [f"h_{gk}"], [f"carry_{dc}_{g}"])
                    kb.tt(hcb[gp], hb[gp], bc_f[:, 512:1024], Alu.mult,
                          [f"h_{gk}", f"bcC_{cc}"], [f"hc_{gk}"], eng="gpsimd")
                for g, gp, gk in pair:
                    def selfn(g=g, gp=gp, gk=gk):
                        kb.mm(ps[5][:, :512], c_S[:, g * 128:(g + 1) * 128], hcb[gp],
                              [f"hc_{gk}", "Ssel"], [f"ps_y_{cc}_{dc}", "@ps5"],
                              start=(g == 0), stop=(g == 15))
                    pend_sel.append(selfn)
            for fn_ in pend_sel:
                fn_()
            kb.stt(z_s[:], xn_fb[:, dc * TC:(dc + 1) * TC], w_dcol[:, dc:dc + 1],
                   ps[5][:, :512], Alu.mult, Alu.add,
                   [f"ps_y_{cc}_{dc}", "@ps5", f"xnf_{cc}_{dc}", "d_col"], [f"z_{cc}_{dc}"])
            for j in range(4):
                bank, pk = (ps[0], "@ps0") if j % 2 == 0 else (ps[6], "@ps6")
                kb.transp(bank[:, :128], z_s[:, j * 128:(j + 1) * 128], c_identf[:],
                          [f"z_{cc}_{dc}", "ident_f32"], [f"psT{j % 2}", pk])
                kb.tt(bufA[:, j * DM + dc * 128: j * DM + (dc + 1) * 128],
                      bufA[:, j * DM + dc * 128: j * DM + (dc + 1) * 128], bank[:, :128],
                      Alu.add, [f"psT{j % 2}", pk, f"xA_{cc}_{j}", f"bufA_{cc}"],
                      [f"x2p_{cc}_{dc}_{j}", f"bufA_{cc}"])
        f0banks = [ps[1], ps[2], ps[4], ps[5]]
        for half in range(2):
            for ic in range(8):
                par = ic % 2
                kb.dma(w_f0[:, par * 512:(par + 1) * 512],
                       dram["mt_scr"][ic][:, half * 512:(half + 1) * 512],
                       [f"mt_{ic}", f"f0w_{par}"], [f"f0w_{par}"])
                for j in range(4):
                    kb.mm(f0banks[j][:, :512],
                          xn_fb[:, ic * TC + j * 128: ic * TC + (j + 1) * 128],
                          w_f0[:, par * 512:(par + 1) * 512],
                          xnf_all + [f"f0w_{par}"],
                          [f"ps_f0_{cc}_{j}_{half}", f"@psf0{j}", f"f0w_{par}"],
                          start=(ic == 0), stop=(ic == 7))
                if not (half == 1 and ic >= 4):
                    S["w2pump"](1)
            for j in range(4):
                kb.tt(bufD[:, half * 512:(half + 1) * 512], f0banks[j][:, :512],
                      w_bias0[:, half * 512:(half + 1) * 512], Alu.add,
                      [f"ps_f0_{cc}_{j}_{half}", f"@psf0{j}", "w_bias0", f"bufDh_{j % 2}"],
                      [f"f0_{cc}_{j}_{half}", f"bufDh_{j % 2}"])
                kb.tt(bufA[:, j * DM + half * 512:j * DM + (half + 1) * 512],
                      bufA[:, j * DM + half * 512:j * DM + (half + 1) * 512],
                      bufD[:, half * 512:(half + 1) * 512],
                      Alu.add,
                      [f"f0_{cc}_{j}_{half}", f"bufDh_{j % 2}", f"bufA_{cc}"]
                      + [f"x2p_{cc}_{dc}_{j}" for dc in range(8)],
                      [f"x2_{cc}_{j}_{half}", f"bufA_{cc}"])
        if stage == "l0":
            for j in range(4):
                kb.dma(dram["out"][c * TC + j * 128: c * TC + (j + 1) * 128, :],
                       bufA[:, j * DM:(j + 1) * DM], [f"x2_{cc}_{j}"], [f"outd_{cc}_{j}"])
        elif stage in ("full", "l1"):
            build_l1(kb, S, c, NCH, stage)

    block = es.enter_context(nc.Block())
    es.enter_context(kb.lay(block))
    return nc, dram, kb, es


def build_l1(kb, S, c, NCH, stage):
    ps = S["ps"]; psT16 = S["psT16"]; psT16b = S["psT16b"]; dram = S["dram"]
    cc = f"c{c}"
    g = S
    bufA, bufB, bufD = g["bufA"], g["bufB"], g["bufD"]
    sq_s, stat, xn_fb = g["sq_s"], g["stat"], g["xn_fb"]
    c_identf, c_identb = g["c_identf"], g["c_identb"]
    w_kv, w_oh, w_f0, wtt = g["w_kv"], g["w_oh"], g["w_f0"], g["wtt"]
    tbl, q_c, k_c, qs_c, ks_c = g["tbl"], g["q_c"], g["k_c"], g["qs_c"], g["ks_c"]
    rq_s, rkh, vh, rqh, avb = g["rq_s"], g["rkh"], g["vh"], g["rqh"], g["avb"]
    nrm_k, nrm_q, mh_s, shf_s = g["nrm_k"], g["nrm_q"], g["mh_s"], g["shf_s"]
    pt_s, av_s, rcp_s = g["pt_s"], g["av_s"], g["rcp_s"]
    g_s, hh_s, hq_t = g["g_s"], g["hh_s"], g["hq_t"]
    xq_t, xq_f = g["xq_t"], g["xq_f"]
    wg_b, wu_b = g["wg_b"], g["wu_b"]
    coaC, coaS, finC, finS = g["coaC"], g["coaS"], g["finC"], g["finS"]
    z_s, hc_s = g["z_s"], g["hc_s"]
    c_o16, c_eh, c_mask, c_pm, c_osq = g["c_o16"], g["c_eh"], g["c_mask"], g["c_pm"], g["c_osq"]
    c_psw = g["c_psw"]
    ls_b = g["ls_b"]
    rp1, rp2 = g["rp1"], g["rp2"]
    bc_f, dA_s, dBx_s, h_s, z_s = g["bc_f"], g["dA_s"], g["dBx_s"], g["h_s"], g["z_s"]
    q_va, q_cc, q_ee, q_sp, q_am, q_rs = g["q_va"], g["q_cc"], g["q_ee"], g["q_sp"], g["q_am"], g["q_rs"]
    wnk = g["wnk"]
    quant_slice = S["quant_slice"]
    quant_wide = S["quant_wide"]
    fin_done, coa_done = S["fin_done"], S["coa_done"]

    for j in range(4):
        xa = bufA[:, j * DM:(j + 1) * DM]
        kb.act(sq_s[:], xa, Act.Square, [f"x2_{cc}_{j}_0", f"x2_{cc}_{j}_1"], ["sq_s"])
        kb.red(stat[:, j:j + 1], xa, Alu.add, [f"x2_{cc}_{j}_0", f"x2_{cc}_{j}_1"], [f"l2_sum_{j}"])
        kb.red(stat[:, 4 + j:5 + j], sq_s[:], Alu.add, ["sq_s"], [f"l2_sq_{j}"])
        kb.ts(stat[:, 8 + j:9 + j], stat[:, j:j + 1], 1.0 / DM, None, Alu.mult, None,
              [f"l2_sum_{j}"], [f"l2_mean_{j}"])
        kb.ts(stat[:, 12 + j:13 + j], stat[:, 8 + j:9 + j], -1.0, None, Alu.mult, None,
              [f"l2_mean_{j}"], [f"l2_nm_{j}"])
        kb.ts(stat[:, 16 + j:17 + j], stat[:, 4 + j:5 + j], 1.0 / DM, None, Alu.mult, None,
              [f"l2_sq_{j}"], [f"l2_msq_{j}"])
        kb.stt(stat[:, 20 + j:21 + j], stat[:, 12 + j:13 + j], stat[:, 8 + j:9 + j],
               stat[:, 16 + j:17 + j], Alu.mult, Alu.add,
               [f"l2_nm_{j}", f"l2_mean_{j}", f"l2_msq_{j}"], [f"l2_var_{j}"])
        kb.act(stat[:, 24 + j:25 + j], stat[:, 20 + j:21 + j], Act.Ln,
               [f"l2_var_{j}"], [f"l2_ln_{j}"], bias=1e-5)
        kb.act(stat[:, 28 + j:29 + j], stat[:, 24 + j:25 + j], Act.Exp,
               [f"l2_ln_{j}"], [f"l2_rstd_{j}"], scale=-0.5)
        kb.ts(bufB[:, j * DM:(j + 1) * DM], xa, stat[:, 8 + j:9 + j],
              stat[:, 28 + j:29 + j], Alu.subtract, Alu.mult,
              [f"x2_{cc}_{j}_0", f"x2_{cc}_{j}_1", f"l2_mean_{j}", f"l2_rstd_{j}"],
              [f"xn2_{cc}_{j}"])
    for dc in range(8):
        for j in range(4):
            bank, pk = (ps[0], "@ps0") if j % 2 == 0 else (ps[3], "@ps3")
            kb.transp(bank[:, :128], bufB[:, j * DM + dc * 128: j * DM + (dc + 1) * 128],
                      c_identf[:], [f"xn2_{cc}_{j}"], [f"psT{j % 2}", pk])
            kb.act(xn_fb[:, dc * TC + j * 128: dc * TC + (j + 1) * 128], bank[:, :128],
                   Act.Copy, [f"psT{j % 2}", pk], [f"x2f_{cc}_{dc}"])
    x2f_all = [f"x2f_{cc}_{dc}" for dc in range(8)]

    for dc in range(8):
        for wsec in range(3):
            kb.dma(w_kv[:, wsec * 1024:(wsec + 1) * 1024].rearrange("p (ch m) -> p ch m", ch=8),
                   dram["wqkv_scr"].rearrange("ch p m -> p ch m")[:, :, (wsec * 8 + dc) * 128:(wsec * 8 + dc + 1) * 128],
                   [f"wqkv_scr_{ch2}" for ch2 in range(8)] + [f"kvb{wsec}"], [f"kvb{wsec}"],
                   q="gpsimd")
        for wsec, bank in [(0, 1), (1, 2), (2, 3)]:
            for ch in range(8):
                kb.mm(ps[bank][:, :512], w_kv[:, wsec * 1024 + ch * 128: wsec * 1024 + (ch + 1) * 128],
                      xn_fb[:, ch * TC:(ch + 1) * TC],
                      [f"x2f_{cc}_{ch}", f"kvb{wsec}"],
                      [f"ps_qkv_{cc}_{dc}_{wsec}", f"@ps{bank}", f"kvb{wsec}"],
                      start=(ch == 0), stop=(ch == 7))
        t1o = 16 * c
        cslice = lambda tb: tb[:, (t1o * 8 + dc):(t1o * 8 + dc + 121):8]
        fslice = lambda tb: tb[:, dc::8]
        def bc_c(ap):
            return ap.rearrange("p (t o) -> p t o", o=1).broadcast_to([128, 16, 32])
        def bfin(ap):
            return ap.rearrange("p (o t) -> p o t", o=1).broadcast_to([128, 16, 32])
        r3t = lambda t: t.rearrange("p (a b) -> p a b", a=16)
        kb.tt(r3t(rp1[:]), bc_c(cslice(coaC)), bfin(fslice(finC)), Alu.mult,
              coa_done + fin_done, ["rp1"])
        kb.tt(r3t(rp2[:]), bc_c(cslice(coaS)), bfin(fslice(finS)), Alu.mult,
              coa_done + fin_done, ["rp2"])
        kb.tt(tbl[:, 0:512], rp1[:], rp2[:], Alu.subtract, ["rp1", "rp2"], [f"tbc_{cc}_{dc}"])
        kb.tt(r3t(rp1[:]), bc_c(cslice(coaS)), bfin(fslice(finC)), Alu.mult,
              coa_done + fin_done, ["rp1"])
        kb.tt(r3t(rp2[:]), bc_c(cslice(coaC)), bfin(fslice(finS)), Alu.mult,
              coa_done + fin_done, ["rp2"])
        kb.tt(tbl[:, 512:1024], rp1[:], rp2[:], Alu.add, ["rp1", "rp2"], [f"tbs0_{cc}_{dc}"])
        kb.ts(tbl[:, 512:1024], tbl[:, 512:1024], c_pm[:, 0:1], None, Alu.mult, None,
              [f"tbs0_{cc}_{dc}"], [f"tbs_{cc}_{dc}"])
        kb.ts(tbl[:, 1024:1536], tbl[:, 0:512], 0.125, None, Alu.mult, None,
              [f"tbc_{cc}_{dc}"], [f"tbc8_{cc}_{dc}"])
        kb.ts(tbl[:, 1536:2048], tbl[:, 512:1024], 0.125, None, Alu.mult, None,
              [f"tbs_{cc}_{dc}"], [f"tbs8_{cc}_{dc}"])
        kb.act(q_c[:], ps[1][:, :512], Act.Copy, [f"ps_qkv_{cc}_{dc}_0", "@ps1"], [f"qc_{cc}_{dc}"])
        kb.act(k_c[:], ps[2][:, :512], Act.Copy, [f"ps_qkv_{cc}_{dc}_1", "@ps2"], [f"kc_{cc}_{dc}"])
        kb.mm(ps[0][:, :512], c_psw[:], q_c[:], [f"qc_{cc}_{dc}", "pswap"],
              [f"qsp_{cc}_{dc}", "@ps0"])
        kb.tt(pt_s[:, 0:512], q_c[:], tbl[:, 1024:1536], Alu.mult,
              [f"qc_{cc}_{dc}", f"tbc8_{cc}_{dc}"], ["pt1"])
        kb.tt(pt_s[:, 512:1024], ps[0][:, :512], tbl[:, 1536:2048], Alu.mult,
              [f"qsp_{cc}_{dc}", "@ps0", f"tbs8_{cc}_{dc}"], ["pt2"])
        kb.tt(rq_s[:, dc * 512:(dc + 1) * 512], pt_s[:, 0:512], pt_s[:, 512:1024], Alu.add,
              ["pt1", "pt2"], [f"rq_{cc}_{dc}"])
        kb.mm(ps[0][:, :512], c_psw[:], k_c[:], [f"kc_{cc}_{dc}", "pswap"],
              [f"ksp_{cc}_{dc}", "@ps0"])
        kb.tt(pt_s[:, 0:512], k_c[:], tbl[:, 0:512], Alu.mult,
              [f"kc_{cc}_{dc}", f"tbc_{cc}_{dc}"], ["pt1"], eng="gpsimd")
        kb.tt(pt_s[:, 512:1024], ps[0][:, :512], tbl[:, 512:1024], Alu.mult,
              [f"ksp_{cc}_{dc}", "@ps0", f"tbs_{cc}_{dc}"], ["pt2"])
        kb.tt(ks_c[:], pt_s[:, 0:512], pt_s[:, 512:1024], Alu.add,
              ["pt1", "pt2"], [f"rk_{cc}_{dc}"])
        kb.dma(dram["rk_scr"][dc][:, c * 512:(c + 1) * 512], ks_c[:], [f"rk_{cc}_{dc}"], [f"rks_{cc}_{dc}"])
        kb.act(q_c[:], rq_s[:, dc * 512:(dc + 1) * 512], Act.Square, [f"rq_{cc}_{dc}"], [f"q2_{cc}_{dc}"])
        kb.mm(ps[6][:16, :512], c_o16[:, dc * 16:(dc + 1) * 16], q_c[:],
              [f"q2_{cc}_{dc}", "ones16"], [f"ps_qn_{cc}", "@ps6"], start=(dc == 0), stop=(dc == 7))
        kb.act(qs_c[:], ks_c[:], Act.Square, [f"rk_{cc}_{dc}"], [f"k2_{cc}_{dc}"])
        kb.mm(ps[7][:16, :512], c_o16[:, dc * 16:(dc + 1) * 16], qs_c[:],
              [f"k2_{cc}_{dc}", "ones16"], [f"ps_kn_{cc}", "@ps7"], start=(dc == 0), stop=(dc == 7))
        kb.act(qs_c[:], ps[3][:, :512], Act.Copy, [f"ps_qkv_{cc}_{dc}_2", "@ps3"], [f"vb_{cc}_{dc}"])
        for j in range(4):
            pv = psT16 if j % 2 == 0 else psT16b
            pk = "@ps0" if j % 2 == 0 else "@ps3"
            wslot = wtt[:, j * 128:(j + 1) * 128]
            kb.transp(pv[:, :], qs_c[:, j * 128:(j + 1) * 128], c_identb[:],
                      [f"vb_{cc}_{dc}", "ident_bf16"], [f"psT{j % 2}", pk])
            kb.act(wslot, pv[:, :], Act.Copy, [f"psT{j % 2}", pk], [f"wtt{j}"])
            kb.dma(dram["v_scr"][c * 4 + j][:, dc * 128:(dc + 1) * 128], wslot,
                   [f"wtt{j}"], [f"vs_{cc}_{dc}_{j}"])
    kb.cp_v(nrm_q[:, :], ps[6][:16, :512], [f"ps_qn_{cc}", "@ps6"], [f"nq_{cc}"])
    kb.cp_v(nrm_k[:, :], ps[7][:16, :512], [f"ps_kn_{cc}", "@ps7"], [f"nk_{cc}"])
    kb.red(mh_s[:16, 0:1], nrm_k[:, :], Alu.max, [f"nk_{cc}"], [f"mh0_{cc}"])
    if c == 0:
        kb.cp_v(mh_s[:16, 1:2], mh_s[:16, 0:1], [f"mh0_{cc}"], [f"mh1_{cc}"])
    else:
        kb.tt(mh_s[:16, 1:2], mh_s[:16, 1:2], mh_s[:16, 0:1], Alu.max,
              [f"mh0_{cc}", f"mh1_c{c - 1}"], [f"mh1_{cc}"])
    kb.act(nrm_q[:, :], nrm_q[:, :], Act.Ln, [f"nq_{cc}"], [f"nq2_{cc}"], bias=1e-30)
    kb.act(nrm_q[:, :], nrm_q[:, :], Act.Exp, [f"nq2_{cc}"], [f"nq3_{cc}"], scale=0.5)
    kb.red(mh_s[:16, 2:3], nrm_q[:, :], Alu.max, [f"nq3_{cc}"], [f"mh2_{cc}"])
    kb.act(mh_s[:16, 3:4], mh_s[:16, 1:2], Act.Ln, [f"mh1_{cc}"], [f"mh3_{cc}"], bias=1e-30)
    kb.act(mh_s[:16, 3:4], mh_s[:16, 3:4], Act.Exp, [f"mh3_{cc}"], [f"mh4_{cc}"], scale=0.5)
    kb.tt(mh_s[:16, 5:6], mh_s[:16, 2:3], mh_s[:16, 3:4], Alu.mult,
          [f"mh2_{cc}", f"mh4_{cc}"], [f"mh5_{cc}"])
    kb.ts(mh_s[:16, 5:6], mh_s[:16, 5:6], -0.125, None, Alu.mult, None, [f"mh5_{cc}"], [f"mh6_{cc}"])
    kb.cp_s(ls_b[:], mh_s[:16, 5:6], [f"mh6_{cc}"], [f"negB_{cc}"])
    for h in range(NH):
        shb, shk = (ps[0], "@ps0") if h % 2 == 0 else (ps[3], "@ps3")
        kb.mm(shb[:, 0:1], c_eh[:, h * 128:(h + 1) * 128], ls_b[:],
              [f"negB_{cc}", "eh"], [f"ps_sh_{cc}_{h}", shk])
        kb.cp_v(shf_s[:, h:h + 1], shb[:, 0:1], [f"ps_sh_{cc}_{h}", shk], [f"shf_{cc}_{h}"])

    nkt = 4 * (c + 1)
    for h in range(NH):
        dc, rows = h // 2, (h % 2) * 64
        kb.dma(rqh[:, :], rq_s[rows:rows + 64, dc * 512:(dc + 1) * 512],
               [f"rq_{cc}_{dc}"], [f"rqh_{cc}_{h}"], q="gpsimd")
        kb.dma(rkh[:, 0:nkt * 128], dram["rk_scr"][dc][rows:rows + 64, 0:nkt * 128],
               [f"rks_c{ci}_{dc}" for ci in range(c + 1)], [f"rkh_{cc}_{h}"], q="gpsimd")
        kb.dma(vh[:, 0:nkt * 65].rearrange("p (kt e) -> p kt e", e=65)[:, :, 0:64],
               dram["v_scr"].rearrange("kt p e -> p kt e")[:, 0:nkt, h * 64:(h + 1) * 64],
               [f"vs_c{ci}_{h // 2}_{jj}" for ci in range(c + 1) for jj in range(4)],
               [f"vh_{cc}_{h}"], q="gpsimd")
        sbanks = [4, 6, 1, 2]
        ptslots = [pt_s[:, 0:512], pt_s[:, 512:1024], tbl[:, 0:512], tbl[:, 512:1024]]
        pend_av = []
        for kt in range(nkt):
            sbk = sbanks[kt % 4]
            pts = ptslots[kt % 4]
            kb.mm(ps[sbk][:, :512], rkh[:, kt * 128:(kt + 1) * 128], rqh[:, :],
                  [f"rkh_{cc}_{h}", f"rqh_{cc}_{h}"], [f"ps_s_{cc}_{h}_{kt}", f"@ps{sbk}"])
            for fn_ in pend_av:
                fn_()
            pend_av = []
            kb.act(pts, ps[sbk][:, :512], Act.Exp,
                   [f"ps_s_{cc}_{h}_{kt}", f"@ps{sbk}", f"shf_{cc}_{h}"],
                   [f"pt_{cc}_{h}_{kt}"], bias=shf_s[:, h:h + 1])
            if kt >= 4 * c:
                kb.tt(pts, pts, c_mask[:, (kt - 4 * c) * 512:(kt - 4 * c + 1) * 512],
                      Alu.mult, [f"pt_{cc}_{h}_{kt}", "cmask"], [f"pt_{cc}_{h}_{kt}"])
            avb_bk = 5 if h % 2 == 0 else 7
            rr_bk = 0 if h % 2 == 0 else 3
            def avfn(kt=kt, pts=pts, avb_bk=avb_bk):
                kb.mm(ps[avb_bk][0:65, :512], vh[:, kt * 65:(kt + 1) * 65], pts,
                      [f"pt_{cc}_{h}_{kt}", f"vh_{cc}_{h}", "vh_ones"],
                      [f"ps_av_{cc}_{h}", f"@ps{avb_bk}"], start=(kt == 0), stop=(kt == nkt - 1))
            pend_av.append(avfn)
        for fn_ in pend_av:
            fn_()
        kb.recip(rcp_s[64:65, :512], ps[avb_bk][64:65, :512],
                 [f"ps_av_{cc}_{h}", f"@ps{avb_bk}"], [f"rcpf_{cc}_{h}"])
        kb.cp_s(hq_t[64:65, 0:512], rcp_s[64:65, :512], [f"rcpf_{cc}_{h}"], [f"rcpb_{cc}_{h}"])
        kb.mm(ps[rr_bk][0:64, :512], c_osq[64:65, 0:64], hq_t[64:65, 0:512],
              [f"rcpb_{cc}_{h}", "onesq"], [f"ps_rr_{cc}_{h}", f"@ps{rr_bk}"])
        kb.cp_s(av_s[0:64, :512], ps[avb_bk][0:64, :512],
                [f"ps_av_{cc}_{h}", f"@ps{avb_bk}"], [f"avf_{cc}_{h}"])
        kb.tt(avb[:, :], av_s[0:64, :512], ps[rr_bk][0:64, :512], Alu.mult,
              [f"avf_{cc}_{h}", f"ps_rr_{cc}_{h}", f"@ps{rr_bk}"], [f"avn_{cc}_{h}"])
        kb.dma(dram["avn_scr"][h], avb[:, :], [f"avn_{cc}_{h}"], [f"avns_{cc}_{h}"])
        S["w2pump"](1)
    S["w2pump"](60)
    for j in range(4):
        quant_wide(bufB[:, j * DM:(j + 1) * DM], [f"xn2_{cc}_{j}"], f"xq_{cc}_{j}", 64,
                   [(xq_t[:, 0:512], f"xq_{cc}_{j}_0"), (xq_t[:, 512:1024], f"xq_{cc}_{j}_1")])
        for dc in range(8):
            pv = psT16 if dc % 2 == 0 else psT16b
            pk = "@ps0" if dc % 2 == 0 else "@ps3"
            kb.transp(pv[:, :], xq_t[:, dc * 128:(dc + 1) * 128], c_identb[:],
                      [f"xq_{cc}_{j}_{dc // 4}", "ident_bf16"], [f"psT{dc % 2}", pk])
            kb.act(rq_s[:, dc * 512 + j * 128: dc * 512 + (j + 1) * 128], pv[:, :],
                   Act.Copy, [f"psT{dc % 2}", pk], [f"xqf_{cc}_{dc}"])
    xqf_all = [f"xqf_{cc}_{dc}" for dc in range(8)]
    psT16c = ps[1][:, 0:64].bitcast(BF16)
    avsl = [rkh[:, 0:512], rkh[:, 512:1024]]
    wosl = [w_oh[:, 0:1024], rkh[:, 1024:2048]]
    for h in range(NH):
        sl = h % 2
        kb.dma(avsl[sl], dram["avn_scr"][h], [f"avns_{cc}_{h}", f"avh{sl}"], [f"avh{sl}"])
        kb.dma(wosl[sl], dram["wo_scr"][h],
               [f"wo_scr_{h}", f"woh{sl}"], [f"woh{sl}"], q="gpsimd")
        for do in range(8):
            kb.mm(ps[do][:, :512], wosl[sl][:, do * 128:(do + 1) * 128], avsl[sl],
                  [f"avh{sl}", f"woh{sl}"],
                  [f"ps_wo_{cc}_{do}", f"@ps{do}", f"avh{sl}", f"woh{sl}"],
                  start=(h == 0), stop=(h == NH - 1))
    for do in range(8):
        kb.act(hc_s[:], ps[do][:, :512], Act.Copy, [f"ps_wo_{cc}_{do}", f"@ps{do}"],
               [f"attnf_{cc}_{do}"])
        for j in range(4):
            use1 = (do >= 2 and j % 2 == 1)
            pv = psT16c if use1 else psT16
            pk = "@ps1" if use1 else "@ps0"
            kb.transp(pv[:, :], hc_s[:, j * 128:(j + 1) * 128], c_identb[:],
                      [f"attnf_{cc}_{do}", "ident_bf16"], ["psT0" if not use1 else "psT1c", pk])
            kb.tt(bufA[:, j * DM + do * 128: j * DM + (do + 1) * 128],
                  bufA[:, j * DM + do * 128: j * DM + (do + 1) * 128], pv[:, :],
                  Alu.add, ["psT0" if not use1 else "psT1c", pk, f"x2_{cc}_{j}_0", f"x2_{cc}_{j}_1"],
                  [f"xa_{cc}_{do}_{j}"])

    NS = [(i * 512, 512) for i in range(5)] + [(2560, 256)]
    for nsi, (nso, nsw) in enumerate(NS):
        kb.dma(wg_b[:].rearrange("p (kc m) -> p kc m", kc=8)[:, :, 0:nsw],
               dram["wg_scr"].rearrange("kc p m -> p kc m")[:, :, nso:nso + nsw],
               [f"wqs_w_gate_{kc}_{nt}" for nt in range(22) for kc in range(8)] + ["wg_b"],
               ["wg_b"])
        kb.dma(wu_b[:].rearrange("p (kc m) -> p kc m", kc=8)[:, :, 0:nsw],
               dram["wu_scr"].rearrange("kc p m -> p kc m")[:, :, nso:nso + nsw],
               [f"wqs_w_up_{kc}_{nt}" for nt in range(22) for kc in range(8)] + ["wu_b"],
               ["wu_b"], q="gpsimd")
        W_H = g["W_H"]
        hqb = [hq_t, bc_f[:, 128:384].bitcast(BF16)]
        hbufs = [W_H, bufD, g["tbl"].bitcast(F32)[:, 0:1024]]
        for jp in range(2):
            hb = hbufs[(2 * nsi + jp) % 3]
            for j in (2 * jp, 2 * jp + 1):
                gb, ub = (1, 2) if j % 2 == 0 else (6, 7)
                for kc in range(8):
                    kb.mm(ps[gb][:, :nsw], rq_s[:, kc * 512 + j * 128: kc * 512 + (j + 1) * 128],
                          wg_b[:, kc * 512: kc * 512 + nsw],
                          xqf_all + ["wg_b"], [f"ps_g_{cc}_{nsi}_{j}", f"@ps{gb}"],
                          start=(kc == 0), stop=(kc == 7))
                for kc in range(8):
                    kb.mm(ps[ub][:, :nsw], rq_s[:, kc * 512 + j * 128: kc * 512 + (j + 1) * 128],
                          wu_b[:, kc * 512: kc * 512 + nsw],
                          xqf_all + ["wu_b"], [f"ps_u_{cc}_{nsi}_{j}", f"@ps{ub}"],
                          start=(kc == 0), stop=(kc == 7))
                gsb = [g_s, bc_f[:, 384:896]][j % 2]
                kb.act(gsb[:, :nsw], ps[gb][:, :nsw], Act.Silu, [f"ps_g_{cc}_{nsi}_{j}", f"@ps{gb}"],
                       [f"sig_{cc}_{nsi}_{j}"])
                kb.tt(hb[:, (j % 2) * nsw:(j % 2) * nsw + nsw], gsb[:, :nsw], ps[ub][:, :nsw],
                      Alu.mult,
                      [f"sig_{cc}_{nsi}_{j}", f"ps_u_{cc}_{nsi}_{j}", f"@ps{ub}"], [f"hh_{cc}_{nsi}_{j}"])
            def qfn(nsi=nsi, nso=nso, nsw=nsw, jp=jp, hb=hb):
                quant_wide(hb[:, 0:2 * nsw],
                           [f"hh_{cc}_{nsi}_{2 * jp}", f"hh_{cc}_{nsi}_{2 * jp + 1}"],
                           f"hq_{cc}_{nsi}_{jp}", 2 * nsw // 16,
                           [(hq_t[:, :nsw], f"hq_{cc}_{nsi}_{2 * jp}"),
                            (bc_f[:, 128:384].bitcast(BF16)[:, :nsw], f"hq_{cc}_{nsi}_{2 * jp + 1}")])
                for j in (2 * jp, 2 * jp + 1):
                    for i in range(nsw // 128):
                        fc = (nso // 128) + i
                        pv = psT16 if i % 2 == 0 else psT16b
                        pk = "@ps0" if i % 2 == 0 else "@ps3"
                        wslot = wtt[:, (i % 4) * 128:(i % 4 + 1) * 128]
                        kb.transp(pv[:, :], hqb[j % 2][:, i * 128:(i + 1) * 128], c_identb[:],
                                  [f"hq_{cc}_{nsi}_{j}", "ident_bf16"], [f"psT{i % 2}", pk])
                        kb.act(wslot, pv[:, :], Act.Copy, [f"psT{i % 2}", pk], [f"wtt{i % 4}"])
                        kb.dma(dram["hqT_scr"][j][fc], wslot, [f"wtt{i % 4}"], [f"hqT_{cc}_{j}_{fc}"])
            pend = S.get("pend_q", [])
            pend.append(qfn)
            while len(pend) > 2:
                pend.pop(0)()
            S["pend_q"] = pend
    for fn_ in S.get("pend_q", []):
        fn_()
    S["pend_q"] = []
    if c + 1 < NCH:
        S["xpre"](c + 1)
    for fc in range(22):
        par = fc % 2
        kb.dma(wnk[:, par * 1024:(par + 1) * 1024], dram["wd_scr"][fc],
               [f"wqs_w_down_{fc}_{nt}" for nt in range(8)] + [f"wnk{par}"], [f"wnk{par}"])
        kb.dma(w_f0[:, par * 512:(par + 1) * 512].rearrange("p (j m) -> p j m", j=4),
               dram["hqT_scr"].rearrange("j fc p m -> p j fc m")[:, :, fc, :],
               [f"hqT_{cc}_{j2}_{fc}" for j2 in range(4)] + [f"dwl_{par}"], [f"dwl_{par}"], q="gpsimd")
        for j in range(4):
            for half in range(2):
                bk = 2 * j + half
                kb.mm(ps[bk][:, :512], w_f0[:, par * 512 + j * 128: par * 512 + (j + 1) * 128],
                      wnk[:, par * 1024 + half * 512: par * 1024 + (half + 1) * 512],
                      [f"dwl_{par}", f"wnk{par}"],
                      [f"ps_d_{cc}_{j}_{half}", f"@ps{bk}", f"dwl_{par}", f"wnk{par}"],
                      start=(fc == 0), stop=(fc == 21))
    for j in range(4):
        for half in range(2):
            bk = 2 * j + half
            kb.tt(bufD[:, half * 512:(half + 1) * 512],
                  bufA[:, j * DM + half * 512: j * DM + (half + 1) * 512], ps[bk][:, :512],
                  Alu.add,
                  [f"ps_d_{cc}_{j}_{half}", f"@ps{bk}", f"bufDh_{half}"]
                  + [f"xa_{cc}_{do}_{j}" for do in range(8)],
                  [f"x3_{cc}_{j}_{half}", f"bufDh_{half}"])
        kb.dma(dram["out"][c * TC + j * 128: c * TC + (j + 1) * 128, :],
               bufD[:, :], [f"x3_{cc}_{j}_0", f"x3_{cc}_{j}_1", "bufDh_0", "bufDh_1"],
               [f"outd_{cc}_{j}", "bufDh_0", "bufDh_1"])


def kernel(**inputs):
    consts = host_constants()
    w = prep_weights(inputs)
    x = np.asarray(inputs["L0_attn_x"], np.float32)
    nc, dram, kb, es = build_program(T=2048, stage="full")
    feed = {}
    feed.update(consts)
    feed.update(w)
    in_maps = []
    for ci in range(8):
        m = dict(feed)
        m["x"] = np.ascontiguousarray(x[ci])
        in_maps.append(m)
    res = run_bass_kernel_spmd(nc, in_maps, core_ids=list(range(8)))
    es.close()
    out = np.stack([res.results[ci]["out"] for ci in range(8)], axis=0)
    return out
